# version 04
# baseline (speedup 1.0000x reference)
import os
import numpy as np
from contextlib import ExitStack
import concourse.bass as bass
import concourse.mybir as mybir
from concourse.bass_utils import run_bass_kernel_spmd
from concourse.alu_op_type import AluOpType as OP

F32 = mybir.dt.float32
I16 = mybir.dt.int16
I32 = mybir.dt.int32
AF = mybir.ActivationFunctionType

D = 1024
T = 2048
NCORE = 8
DCH = D // 128
H = 16
HS = 64
FUT = 4
E = 16
FF = 4096
SHD = 2048
CAP = 384
EPS = 1e-5

_EMIT_UPTO = 99
_DEBUG_DUMP = None


class Sem:
    __slots__ = ("h", "n", "name")

    def __init__(self, stack, nc, name):
        self.h = stack.enter_context(nc.semaphore(name))
        self.n = 0
        self.name = name


class Prog:
    def __init__(self):
        self.nc = bass.Bass("TRN2", target_bir_lowering=False,
                            debug=bool(os.environ.get("BASSDBG")))
        self.stack = ExitStack()
        self.sems = {}
        self.dram = {}

    def sem(self, name):
        if name not in self.sems:
            self.sems[name] = Sem(self.stack, self.nc, name)
        return self.sems[name]

    _names = None

    def sb(self, stack, name, shape, dtype=F32):
        if self._names is None:
            self._names = {}
        n = self._names.get(name, 0)
        self._names[name] = n + 1
        if n:
            name = f"{name}_{n}"
        return stack.enter_context(self.nc.sbuf_tensor(name, list(shape), dtype))

    def din(self, name, shape, dtype=F32):
        t = self.nc.dram_tensor(name, list(shape), dtype, kind="ExternalInput")
        self.dram[name] = t
        return t

    def dout(self, name, shape, dtype=F32):
        t = self.nc.dram_tensor(name, list(shape), dtype, kind="ExternalOutput")
        self.dram[name] = t
        return t

    def dtmp(self, name, shape, dtype=F32):
        t = self.nc.dram_tensor(name, list(shape), dtype)
        self.dram[name] = t
        return t

    def vmark(self, sem):
        self.nc.vector.drain().then_inc(sem.h, 1); sem.n += 1

    def smark(self, sem):
        self.nc.scalar.drain().then_inc(sem.h, 1); sem.n += 1

    def tmark(self, sem):
        self.nc.tensor.drain().then_inc(sem.h, 1); sem.n += 1


class PsumPool:

    def __init__(self, p, stack):
        self.p = p
        self.banks = [stack.enter_context(p.nc.psum_tensor(f"pb{i}", [128, 512], F32))
                      for i in range(8)]
        self.sems = [p.sem(f"pbs{i}") for i in range(8)]
        self.pending = [0] * 8
        self.rr = 0

    def __post(self):
        pass

    def acquire(self, eng="tensor"):
        for _ in range(8):
            i = self.rr
            self.rr = (self.rr + 1) % 8
            if i not in getattr(self, "reserved", ()):
                break
        if self.pending[i] > 0:
            getattr(self.p.nc, eng).wait_ge(self.sems[i].h, self.pending[i])
        return i

    def reserve(self, eng="tensor"):
        if not hasattr(self, "reserved"):
            self.reserved = set()
        i = self.acquire(eng)
        self.reserved.add(i)
        return i

    def release(self, i):
        self.reserved.discard(i)

    def bank(self, i):
        return self.banks[i]

    def consumed(self, i, instr):
        instr.then_inc(self.sems[i].h, 1)
        self.sems[i].n += 1
        self.pending[i] = self.sems[i].n


def build_program():
    p = Prog()
    nc = p.nc

    x_in = p.din("x_in", [T, D])
    out_t = p.dout("out", [T, D])
    ident = p.din("ident", [128, 128])
    ones = p.din("ones_col", [128, 1])
    onesr = p.din("ones_row", [1, 128])
    maskc = p.din("mask_c", [128, 512])
    ones2 = p.din("ones2", [128, 4])
    sel2t = p.din("sel2t", [2, 128])
    w0qkvp = p.din("w0qkvp", [8, D, 384])
    b0qkv = p.din("b0qkv", [128, 24])
    w0outp = p.din("w0outp", [8, 128, D])
    b0out = p.din("b0out", [128, 8])
    w0r = p.din("w0r", [D, D])
    w0k = p.din("w0k", [D, D])
    w0v = p.din("w0v", [D, D])
    mur = p.din("mur", [128, 8])
    muk = p.din("muk", [128, 8])
    w1qkvf = p.din("w1qkvf", [D, 4 * D])
    w1conv = p.din("w1conv", [FUT, 64, 64])
    w1projp = p.din("w1projp", [8, 128, D])
    w1ff1 = p.din("w1ff1", [D, FF])
    w1ff2 = p.din("w1ff2", [FF, D])
    ln1g = p.din("ln1g", [128, 8])
    ln1b = p.din("ln1b", [128, 8])
    ln2g = p.din("ln2g", [128, 8])
    ln2b = p.din("ln2b", [128, 8])
    selh8 = p.din("selh8", [128, 32])
    sel8d = p.din("sel8d", [8, 2])
    sel2to8 = p.din("sel2to8", [2, 8])
    self8 = p.din("self8", [8, FUT * 128])
    gwt = p.din("gwt", [D, E])
    iotap1 = p.din("iotap1", [16, 128])
    iotac = p.din("iotac", [128, 384])
    iotar = p.din("iotar", [128, 3])
    selpm = p.din("selpm", [128, 16])
    selpmt = p.din("selpmt", [16, 128])
    upwt = p.din("upwt", [E, D, FF])
    gwt2 = p.din("gwt2", [E, D, FF])
    dwnt = p.din("dwnt", [E, FF, D])
    upbf = p.din("upbf", [E, 128, 32])
    gbf = p.din("gbf", [E, 128, 32])
    dbw = p.din("dbw", [E, D])
    shgt = p.din("shgt", [D, SHD])
    shut = p.din("shut", [D, SHD])
    shdt = p.din("shdt", [SHD, D])
    scq = p.dtmp("scq", [D, T])
    sck = p.dtmp("sck", [D, T])
    scf = p.dtmp("scf", [D, T])
    scvt = p.dtmp("scvt", [T, D])
    sckf = p.dtmp("sckf", [FUT, D, T])
    scvf = p.dtmp("scvf", [FUT, D, T])
    sc_x = p.dtmp("sc_x", [128, DCH * T])
    xg_in = p.dtmp("xg_in", [D, T])
    xg_all = p.dtmp("xg_all", [NCORE * D, T])
    scxt = p.dtmp("scxt", [T, D])

    with p.stack:
        psum = PsumPool(p, p.stack)
        BUFA = p.sb(p.stack, "BUFA", [128, DCH * T])
        BUFB = p.sb(p.stack, "BUFB", [128, DCH * T])
        IDENT = p.sb(p.stack, "IDENT", [128, 128])
        ONES = p.sb(p.stack, "ONES", [128, 1])
        ONESR = p.sb(p.stack, "ONESR", [1, 128])
        MKC = p.sb(p.stack, "MKC", [128, 512])
        ONES2 = p.sb(p.stack, "ONES2", [128, 4])
        SEL2T = p.sb(p.stack, "SEL2T", [2, 128])
        BQKV = p.sb(p.stack, "BQKV", [128, 24])
        BOUT = p.sb(p.stack, "BOUT", [128, 8])
        MUR = p.sb(p.stack, "MUR", [128, 8])
        MUK = p.sb(p.stack, "MUK", [128, 8])
        OMR = p.sb(p.stack, "OMR", [128, 8])
        OMK = p.sb(p.stack, "OMK", [128, 8])
        GWT = p.sb(p.stack, "GWT", [128, DCH * E])

        sL = p.sem("ld")
        sW = p.sem("wload")
        sV = p.sem("vec")
        sS = p.sem("sca")
        sT = p.sem("ten")
        sO = p.sem("outd")

        for (dst, src) in [(IDENT, ident.ap()), (ONES, ones.ap()),
                           (ONESR, onesr.ap()), (MKC, maskc.ap()),
                           (ONES2, ones2.ap()), (SEL2T, sel2t.ap()),
                           (BQKV, b0qkv.ap()), (BOUT, b0out.ap()),
                           (MUR, mur.ap()), (MUK, muk.ap())]:
            nc.sync.dma_start(out=dst[:], in_=src).then_inc(sL.h, 16); sL.n += 16
        nc.sync.dma_start(out=GWT[:].rearrange("p (c e) -> p c e", c=DCH),
                          in_=gwt.ap().rearrange("(c p) e -> p c e", p=128)
                          ).then_inc(sL.h, 16); sL.n += 16

        def bcast_row(row_ap, dst_ap, n, nparts=128):
            for c0 in range(0, n, 512):
                w = min(512, n - c0)
                b = psum.acquire()
                nc.tensor.matmul(psum.bank(b)[0:nparts, 0:w],
                                 ONESR[:, 0:nparts], row_ap[:, c0:c0 + w],
                                 start=True, stop=True).then_inc(sT.h, 1)
                sT.n += 1
                nc.vector.wait_ge(sT.h, sT.n)
                cp = nc.vector.tensor_copy(dst_ap[:, c0:c0 + w],
                                           psum.bank(b)[0:nparts, 0:w])
                psum.consumed(b, cp)

        def transpose_fm(src_ap_fn, dst_ap_fn, ktiles, dchunks, consumer="vector"):
            for k in range(ktiles):
                for dch in range(dchunks):
                    b = psum.acquire()
                    nc.tensor.transpose(psum.bank(b)[:, 0:128],
                                        src_ap_fn(k, dch), IDENT[:]
                                        ).then_inc(sT.h, 1)
                    sT.n += 1
                    nc.vector.wait_ge(sT.h, sT.n)
                    cp = nc.vector.tensor_copy(dst_ap_fn(k, dch),
                                               psum.bank(b)[:, 0:128])
                    psum.consumed(b, cp)

        nc.sync.dma_start(
            out=BUFB[:].rearrange("p (k d) -> p k d", k=16),
            in_=x_in.ap().rearrange("(k p) d -> p k d", p=128),
        ).then_inc(sL.h, 16); sL.n += 16
        nc.tensor.wait_ge(sL.h, sL.n)
        transpose_fm(
            lambda k, dch: BUFB[:, k * D + dch * 128: k * D + dch * 128 + 128],
            lambda k, dch: BUFA[:, dch * T + k * 128: dch * T + k * 128 + 128],
            16, DCH)
        p.vmark(sV)

        def emit_layernorm(src, dst, ph, gb=None):
            SQ = p.sb(ph, "SQ", [128, DCH * 512])
            STAT = p.sb(ph, "STAT", [1, 4 * 512])
            BC0 = p.sb(ph, "BC0", [128, 512])
            BC1 = p.sb(ph, "BC1", [128, 512])
            nc.scalar.wait_ge(sV.h, sV.n)
            nc.tensor.wait_ge(sV.h, sV.n)
            for tc in range(4):
                bs = psum.acquire(); bq = psum.acquire()
                nc.scalar.wait_ge(sT.h, sT.n)
                for dch in range(DCH):
                    sl = src[:, dch * T + tc * 512: dch * T + tc * 512 + 512]
                    nc.scalar.activation(out=SQ[:, dch * 512: dch * 512 + 512],
                                         in_=sl, func=AF.Square).then_inc(sS.h, 1)
                    sS.n += 1
                    nc.tensor.matmul(psum.bank(bs)[0:1, :], ONES[:], sl,
                                     start=(dch == 0), stop=(dch == DCH - 1))
                    nc.tensor.wait_ge(sS.h, sS.n)
                    mmq = nc.tensor.matmul(psum.bank(bq)[0:1, :], ONES[:],
                                           SQ[:, dch * 512: dch * 512 + 512],
                                           start=(dch == 0), stop=(dch == DCH - 1))
                    if dch == DCH - 1:
                        mmq.then_inc(sT.h, 1); sT.n += 1
                nc.vector.wait_ge(sT.h, sT.n)
                nc.vector.wait_ge(sV.h, sV.n)
                srow = lambda i: STAT[0:1, i * 512: i * 512 + 512]
                mrow, vrow, trow = srow(2), srow(0), srow(1)
                c0 = nc.vector.tensor_scalar(out=mrow, in0=psum.bank(bs)[0:1, :],
                                             scalar1=1.0 / D, scalar2=None,
                                             op0=OP.mult)
                psum.consumed(bs, c0)
                c1 = nc.vector.tensor_scalar(out=vrow, in0=psum.bank(bq)[0:1, :],
                                             scalar1=1.0 / D, scalar2=None,
                                             op0=OP.mult)
                psum.consumed(bq, c1)
                nc.vector.drain()
                nc.vector.tensor_tensor(trow, mrow, mrow, op=OP.mult)
                nc.vector.drain()
                nc.vector.tensor_tensor(vrow, vrow, trow, op=OP.subtract)
                nc.vector.drain()
                nc.vector.tensor_scalar(out=vrow, in0=vrow, scalar1=EPS,
                                        scalar2=None, op0=OP.add)
                p.vmark(sV)
                nc.scalar.wait_ge(sV.h, sV.n)
                nc.scalar.activation(out=trow, in_=vrow, func=AF.Sqrt)
                p.smark(sS)
                nc.vector.wait_ge(sS.h, sS.n)
                nc.vector.reciprocal(srow(3), trow)
                p.vmark(sV)
                nc.tensor.wait_ge(sV.h, sV.n)
                bcast_row(srow(2), BC0[:], 512)
                bcast_row(srow(3), BC1[:], 512)
                nc.vector.drain()
                for dch in range(DCH):
                    sl = src[:, dch * T + tc * 512: dch * T + tc * 512 + 512]
                    dl = dst[:, dch * T + tc * 512: dch * T + tc * 512 + 512]
                    nc.vector.tensor_tensor(dl, sl, BC0[:], op=OP.subtract)
                    nc.vector.drain()
                    nc.vector.tensor_tensor(dl, dl, BC1[:], op=OP.mult)
                    nc.vector.drain()
                    if gb is not None:
                        g, bb = gb
                        nc.vector.tensor_scalar(out=dl, in0=dl,
                                                scalar1=g[:, dch:dch + 1],
                                                scalar2=bb[:, dch:dch + 1],
                                                op0=OP.mult, op1=OP.add)
                        nc.vector.drain()
                p.vmark(sV)

        if _EMIT_UPTO >= 1:
            with ExitStack() as ph:
                emit_layernorm(BUFA, BUFB, ph)

        if _EMIT_UPTO >= 2:
            with ExitStack() as ph:
                XR = p.sb(ph, "XR", [128, DCH * 512])
                RR = p.sb(ph, "RR", [128, DCH * 512])
                K2 = p.sb(ph, "K2", [128, DCH * 512])
                WT = [p.sb(ph, f"WTf{i}", [128, D]) for i in range(2)]
                TMPc = p.sb(ph, "TMPc", [128, 512])
                nc.vector.wait_ge(sL.h, sL.n)
                for (m, o) in ((MUR, OMR), (MUK, OMK)):
                    nc.vector.tensor_scalar(out=o[:], in0=m[:], scalar1=-1.0,
                                            scalar2=1.0, op0=OP.mult, op1=OP.add)
                p.vmark(sV)

                def mix(dst, mu, om, tc):
                    nc.vector.wait_ge(sT.h, sT.n)
                    for dch in range(DCH):
                        base = dch * T + tc * 512
                        dsl = dst[:, dch * 512: dch * 512 + 512]
                        cur = BUFB[:, base: base + 512]
                        nc.vector.drain()
                        nc.vector.tensor_scalar(out=TMPc[:], in0=cur,
                                                scalar1=mu[:, dch: dch + 1],
                                                scalar2=None, op0=OP.mult)
                        nc.vector.drain()
                        if tc == 0:
                            prev = BUFB[:, base: base + 511]
                            nc.vector.scalar_tensor_tensor(
                                out=dst[:, dch * 512 + 1: dch * 512 + 512],
                                in0=prev, scalar=om[:, dch: dch + 1],
                                in1=TMPc[:, 1:512], op0=OP.mult, op1=OP.add)
                            nc.vector.tensor_copy(dst[:, dch * 512: dch * 512 + 1],
                                                  TMPc[:, 0:1])
                        else:
                            prev = BUFB[:, base - 1: base + 511]
                            nc.vector.scalar_tensor_tensor(
                                out=dsl, in0=prev, scalar=om[:, dch: dch + 1],
                                in1=TMPc[:], op0=OP.mult, op1=OP.add)
                    p.vmark(sV)

                def stream_mm(wdram, rhs_buf, act_out):
                    banks = [psum.acquire() for _ in range(8)]
                    for dch in range(DCH):
                        wbuf = WT[dch % 2]
                        nc.sync.wait_ge(sT.h, sT.n)
                        nc.sync.dma_start(out=wbuf[:],
                                          in_=wdram.ap()[dch * 128: dch * 128 + 128, :]
                                          ).then_inc(sW.h, 16)
                        sW.n += 16
                        nc.tensor.wait_ge(sW.h, sW.n)
                        for j in range(8):
                            nc.tensor.matmul(
                                psum.bank(banks[j])[:, 0:512],
                                wbuf[:, j * 128: j * 128 + 128],
                                rhs_buf[:, dch * 512: dch * 512 + 512],
                                start=(dch == 0), stop=(dch == DCH - 1))
                        p.tmark(sT)
                    for j in range(8):
                        act_out(j, banks[j])

                for tc in range(4):
                    mix(XR, MUR, OMR, tc)
                    nc.tensor.wait_ge(sV.h, sV.n)

                    def act_r(j, b):
                        nc.scalar.wait_ge(sT.h, sT.n)
                        i = nc.scalar.activation(out=RR[:, j * 512: j * 512 + 512],
                                                 in_=psum.bank(b)[:, 0:512],
                                                 func=AF.Sigmoid)
                        psum.consumed(b, i)
                    stream_mm(w0r, XR, act_r)
                    p.smark(sS)

                    mix(XR, MUK, OMK, tc)
                    nc.tensor.wait_ge(sV.h, sV.n)

                    def act_k(j, b):
                        nc.scalar.wait_ge(sT.h, sT.n)
                        sl = K2[:, j * 512: j * 512 + 512]
                        nc.scalar.activation(out=sl, in_=psum.bank(b)[:, 0:512],
                                             func=AF.Relu)
                        nc.scalar.drain()
                        i = nc.scalar.activation(out=sl, in_=sl, func=AF.Square)
                        psum.consumed(b, i)
                    stream_mm(w0k, XR, act_k)
                    p.smark(sS)
                    nc.tensor.wait_ge(sS.h, sS.n)

                    def act_v(j, b):
                        nc.vector.wait_ge(sT.h, sT.n)
                        xsl = BUFA[:, j * T + tc * 512: j * T + tc * 512 + 512]
                        rsl = RR[:, j * 512: j * 512 + 512]
                        nc.vector.drain()
                        nc.vector.tensor_tensor(TMPc[:], psum.bank(b)[:, 0:512],
                                                rsl, op=OP.mult)
                        nc.vector.drain()
                        i = nc.vector.tensor_tensor(xsl, xsl, TMPc[:], op=OP.add)
                        psum.consumed(b, i)
                    stream_mm(w0v, K2, act_v)
                    p.vmark(sV)

        if _EMIT_UPTO >= 3:
            with ExitStack() as ph:
                WQB = p.sb(ph, "WQB", [128, D])
                WOUT = p.sb(ph, "WOUT", [128, D])
                QP = p.sb(ph, "QP", [128, T])
                KP = p.sb(ph, "KP", [128, T])
                VXP = p.sb(ph, "VXP", [128, T])
                PBUF = p.sb(ph, "PBUF", [128, 2 * 8 * 512])
                OUTP = p.sb(ph, "OUTP", [128, T])
                RBC = p.sb(ph, "RBC", [128, 256])
                DROW = p.sb(ph, "DROW", [2, 256])
                sP = p.sem("pairw")

                nc.vector.wait_ge(sL.h, sL.n)
                nc.vector.wait_ge(sV.h, sV.n)
                for dch in range(DCH):
                    nc.vector.drain()
                    xs = BUFA[:, dch * T:(dch + 1) * T]
                    nc.vector.tensor_scalar(out=xs, in0=xs,
                                            scalar1=BOUT[:, dch:dch + 1],
                                            scalar2=None, op0=OP.add)
                p.vmark(sV)

                for pr in range(8):
                    nc.sync.wait_ge(sT.h, sT.n)
                    nc.sync.dma_start(out=WOUT[:], in_=w0outp.ap()[pr]
                                      ).then_inc(sP.h, 16); sP.n += 16
                    for blk, DEST in ((0, QP), (1, KP), (2, VXP)):
                        nc.sync.wait_ge(sT.h, sT.n)
                        nc.sync.dma_start(
                            out=WQB[:].rearrange("p (c j) -> p c j", c=8),
                            in_=w0qkvp.ap()[pr, :, blk * 128:(blk + 1) * 128]
                                .rearrange("(c p) j -> p c j", p=128),
                        ).then_inc(sP.h, 16); sP.n += 16
                        nc.tensor.wait_ge(sP.h, sP.n)
                        nc.tensor.wait_ge(sV.h, sV.n)
                        for tc in range(4):
                            b = psum.acquire()
                            for dch in range(DCH):
                                mm = nc.tensor.matmul(
                                    psum.bank(b)[:, 0:512],
                                    WQB[:, dch * 128:(dch + 1) * 128],
                                    BUFB[:, dch * T + tc * 512: dch * T + tc * 512 + 512],
                                    start=(dch == 0), stop=(dch == DCH - 1))
                            mm.then_inc(sT.h, 1); sT.n += 1
                            nc.scalar.wait_ge(sT.h, sT.n)
                            i = nc.scalar.activation(
                                out=DEST[:, tc * 512: tc * 512 + 512],
                                in_=psum.bank(b)[:, 0:512], func=AF.Identity,
                                bias=BQKV[:, blk * 8 + pr: blk * 8 + pr + 1],
                                scale=1.0)
                            psum.consumed(b, i)
                    p.smark(sS)
                    nc.tensor.wait_ge(sS.h, sS.n)
                    for k in range(16):
                        b = psum.acquire()
                        nc.tensor.transpose(psum.bank(b)[:, 0:128],
                                            VXP[:, k * 128:(k + 1) * 128],
                                            IDENT[:]).then_inc(sT.h, 1)
                        sT.n += 1
                        nc.vector.wait_ge(sT.h, sT.n)
                        cp = nc.vector.tensor_copy(VXP[:, k * 128:(k + 1) * 128],
                                                   psum.bank(b)[:, 0:128])
                        psum.consumed(b, cp)
                    p.vmark(sV)
                    nc.tensor.wait_ge(sV.h, sV.n)

                    for qc in range(8):
                        nst = 2 * (qc + 1)
                        for stp in range(nst // 2):
                            bq = [psum.acquire(), psum.acquire()]
                            for half in range(2):
                                st = stp * 2 + half
                                for hh in range(2):
                                    mm = nc.tensor.matmul(
                                        psum.bank(bq[hh])[:, half * 256:(half + 1) * 256],
                                        KP[64 * hh:64 * hh + 64, st * 128:(st + 1) * 128],
                                        QP[64 * hh:64 * hh + 64, qc * 256:(qc + 1) * 256],
                                        start=True, stop=True,
                                        tile_position=(64 * hh, 0))
                            mm.then_inc(sT.h, 1); sT.n += 1
                            nc.scalar.wait_ge(sT.h, sT.n)
                            for hh in range(2):
                                slot = PBUF[:, (hh * 8 + stp) * 512:
                                            (hh * 8 + stp) * 512 + 512]
                                i = nc.scalar.activation(out=slot,
                                                         in_=psum.bank(bq[hh])[:, 0:512],
                                                         func=AF.Exp, scale=0.125)
                                psum.consumed(bq[hh], i)
                        p.smark(sS)
                        nc.vector.wait_ge(sS.h, sS.n)
                        for hh in range(2):
                            slot = PBUF[:, (hh * 8 + nst // 2 - 1) * 512:
                                        (hh * 8 + nst // 2 - 1) * 512 + 512]
                            nc.vector.drain()
                            nc.vector.tensor_tensor(slot, slot, MKC[:], op=OP.mult)
                        p.vmark(sV)
                        nc.tensor.wait_ge(sV.h, sV.n)
                        bav = [psum.reserve(), psum.reserve()]
                        bden = psum.reserve()
                        for st in range(nst):
                            stp, half = st // 2, st % 2
                            for hh in range(2):
                                ps = PBUF[:, (hh * 8 + stp) * 512 + half * 256:
                                          (hh * 8 + stp) * 512 + half * 256 + 256]
                                mmd = nc.tensor.matmul(
                                    psum.bank(bden)[0:2, 0:256],
                                    ONES2[:, hh * 2:hh * 2 + 2], ps,
                                    start=(st == 0 and hh == 0),
                                    stop=(st == nst - 1 and hh == 1))
                                nc.tensor.matmul(
                                    psum.bank(bav[hh])[64 * hh:64 * hh + 64, 0:256],
                                    VXP[:, st * 128 + 64 * hh: st * 128 + 64 * hh + 64],
                                    ps, start=(st == 0), stop=(st == nst - 1),
                                    tile_position=(0, 64 * hh))
                        p.tmark(sT)
                        nc.vector.wait_ge(sT.h, sT.n)
                        rc = nc.vector.reciprocal(DROW[:], psum.bank(bden)[0:2, 0:256])
                        psum.consumed(bden, rc)
                        p.vmark(sV)
                        nc.tensor.wait_ge(sV.h, sV.n)
                        brb = psum.acquire()
                        nc.tensor.matmul(psum.bank(brb)[:, 0:256], SEL2T[:], DROW[:],
                                         start=True, stop=True).then_inc(sT.h, 1)
                        sT.n += 1
                        nc.vector.wait_ge(sT.h, sT.n)
                        cp = nc.vector.tensor_copy(RBC[:], psum.bank(brb)[:, 0:256])
                        psum.consumed(brb, cp)
                        nc.vector.drain()
                        for hh in range(2):
                            i = nc.vector.tensor_tensor(
                                OUTP[64 * hh:64 * hh + 64, qc * 256:(qc + 1) * 256],
                                psum.bank(bav[hh])[64 * hh:64 * hh + 64, 0:256],
                                RBC[64 * hh:64 * hh + 64, :], op=OP.mult)
                            psum.consumed(bav[hh], i)
                        p.vmark(sV)
                        psum.release(bav[0]); psum.release(bav[1])
                        psum.release(bden)

                    nc.tensor.wait_ge(sV.h, sV.n)
                    for j in range(DCH):
                        for tc in range(4):
                            b = psum.acquire()
                            nc.tensor.matmul(psum.bank(b)[:, 0:512],
                                             WOUT[:, j * 128:(j + 1) * 128],
                                             OUTP[:, tc * 512: tc * 512 + 512],
                                             start=True, stop=True).then_inc(sT.h, 1)
                            sT.n += 1
                            nc.vector.wait_ge(sT.h, sT.n)
                            xsl = BUFA[:, j * T + tc * 512: j * T + tc * 512 + 512]
                            nc.vector.drain()
                            i = nc.vector.tensor_tensor(xsl, xsl,
                                                        psum.bank(b)[:, 0:512],
                                                        op=OP.add)
                            psum.consumed(b, i)
                    p.vmark(sV)

        sX = p.sem("xsp")
        sCC = p.sem("cc")
        LLOG = p.sb(p.stack, "LLOG", [16, T])
        LN1G = p.sb(p.stack, "LN1G", [128, 8])
        LN1B = p.sb(p.stack, "LN1B", [128, 8])
        LN2G = p.sb(p.stack, "LN2G", [128, 8])
        LN2B = p.sb(p.stack, "LN2B", [128, 8])
        SELH8 = p.sb(p.stack, "SELH8", [128, 32])
        SEL8D = p.sb(p.stack, "SEL8D", [8, 2])
        SEL2TO8 = p.sb(p.stack, "SEL2TO8", [2, 8])
        SELF8 = p.sb(p.stack, "SELF8", [8, FUT * 128])
        sL2 = p.sem("ld2")
        for (dst, srcap) in [(LN1G, ln1g.ap()), (LN1B, ln1b.ap()),
                             (LN2G, ln2g.ap()), (LN2B, ln2b.ap()),
                             (SELH8, selh8.ap()), (SEL8D, sel8d.ap()),
                             (SEL2TO8, sel2to8.ap()), (SELF8, self8.ap())]:
            nc.scalar.dma_start(out=dst[:], in_=srcap).then_inc(sL2.h, 16)
            sL2.n += 16
        if _EMIT_UPTO >= 4:
            with ExitStack() as ph:
                emit_layernorm(BUFA, BUFB, ph)
            with ExitStack() as ph:
                nc.tensor.wait_ge(sV.h, sV.n)
                nc.tensor.wait_ge(sL.h, sL.n)
                for tc in range(4):
                    b = psum.acquire()
                    for dch in range(DCH):
                        mm = nc.tensor.matmul(psum.bank(b)[0:16, 0:512],
                                              GWT[:, dch * 16:(dch + 1) * 16],
                                              BUFB[:, dch * T + tc * 512:
                                                   dch * T + tc * 512 + 512],
                                              start=(dch == 0), stop=(dch == DCH - 1))
                    mm.then_inc(sT.h, 1); sT.n += 1
                    nc.vector.wait_ge(sT.h, sT.n)
                    cp = nc.vector.tensor_copy(LLOG[:, tc * 512: tc * 512 + 512],
                                               psum.bank(b)[0:16, 0:512])
                    psum.consumed(b, cp)
                p.vmark(sV)
                nc.sync.wait_ge(sV.h, sV.n)
                nc.sync.dma_start(
                    out=xg_in.ap().rearrange("(c p) t -> p c t", p=128),
                    in_=BUFB[:].rearrange("p (c t) -> p c t", c=DCH),
                ).then_inc(sX.h, 16); sX.n += 16
                nc.sync.dma_start(out=sc_x.ap(), in_=BUFA[:]
                                  ).then_inc(sX.h, 16); sX.n += 16
                nc.gpsimd.wait_ge(sX.h, sX.n)
                nc.gpsimd.collective_compute(
                    "AllGather", OP.bypass,
                    replica_groups=[list(range(NCORE))],
                    ins=[xg_in.ap().opt()], outs=[xg_all.ap().opt()],
                ).then_inc(sCC.h, 1); sCC.n += 1
                XTT = p.sb(ph, "XTT", [128, D])
                sXT = p.sem("xtew")
                for k in range(16):
                    nc.vector.wait_ge(sXT.h, sXT.n)
                    for dch in range(DCH):
                        b = psum.acquire()
                        nc.tensor.transpose(psum.bank(b)[:, 0:128],
                                            BUFB[:, dch * T + k * 128:
                                                 dch * T + k * 128 + 128],
                                            IDENT[:]).then_inc(sT.h, 1)
                        sT.n += 1
                        nc.vector.wait_ge(sT.h, sT.n)
                        cp = nc.vector.tensor_copy(
                            XTT[:, dch * 128:(dch + 1) * 128],
                            psum.bank(b)[:, 0:128])
                        psum.consumed(b, cp)
                    p.vmark(sV)
                    nc.sync.wait_ge(sV.h, sV.n)
                    nc.sync.dma_start(out=scxt.ap()[k * 128:(k + 1) * 128, :],
                                      in_=XTT[:]).then_inc(sXT.h, 16)
                    sXT.n += 16

        if _EMIT_UPTO >= 5:
            with ExitStack() as ph:
                for eng in (nc.vector, nc.scalar, nc.tensor):
                    eng.wait_ge(sX.h, sX.n)
                    eng.wait_ge(sXT.h, sXT.n)
                    eng.wait_ge(sL2.h, sL2.n)
                emit_layernorm(BUFB, BUFA, ph, gb=(LN1G, LN1B))
            with ExitStack() as ph:
                nc.sync.wait_ge(sV.h, sV.n)
                nc.sync.wait_ge(sS.h, sS.n)
                nc.sync.wait_ge(sT.h, sT.n)
                WQT = p.sb(ph, "WQT", [128, D])
                QSB = p.sb(ph, "QSB", [128, 512])
                TSB = p.sb(ph, "TSB", [128, 512])
                sQW = p.sem("qkvw")
                sQD = p.sem("qkvd")
                for jj in range(32):
                    stream = jj // 8; pr = jj % 8
                    nc.sync.wait_ge(sT.h, sT.n)
                    nc.sync.dma_start(
                        out=WQT[:].rearrange("p (c j) -> p c j", c=8),
                        in_=w1qkvf.ap()[:, jj * 128:(jj + 1) * 128]
                            .rearrange("(c p) j -> p c j", p=128),
                    ).then_inc(sQW.h, 16); sQW.n += 16
                    nc.tensor.wait_ge(sQW.h, sQW.n)
                    nc.tensor.wait_ge(sV.h, sV.n)
                    for tc in range(4):
                        b = psum.acquire()
                        for dch in range(DCH):
                            mm = nc.tensor.matmul(
                                psum.bank(b)[:, 0:512], WQT[:, dch * 128:(dch + 1) * 128],
                                BUFA[:, dch * T + tc * 512: dch * T + tc * 512 + 512],
                                start=(dch == 0), stop=(dch == DCH - 1))
                        mm.then_inc(sT.h, 1); sT.n += 1
                        nc.vector.wait_ge(sT.h, sT.n)
                        nc.vector.wait_ge(sQD.h, sQD.n)
                        cp = nc.vector.tensor_copy(QSB[:], psum.bank(b)[:, 0:512])
                        psum.consumed(b, cp)
                        p.vmark(sV)
                        if stream != 2:
                            dstd = (scq, sck, None, scf)[stream]
                            nc.sync.wait_ge(sV.h, sV.n)
                            nc.sync.dma_start(
                                out=dstd.ap()[pr * 128:(pr + 1) * 128,
                                              tc * 512:(tc + 1) * 512],
                                in_=QSB[:]).then_inc(sQD.h, 16); sQD.n += 16
                        else:
                            nc.tensor.wait_ge(sV.h, sV.n)
                            for kk in range(4):
                                b2 = psum.acquire()
                                nc.tensor.transpose(psum.bank(b2)[:, 0:128],
                                                    QSB[:, kk * 128:(kk + 1) * 128],
                                                    IDENT[:]).then_inc(sT.h, 1)
                                sT.n += 1
                                nc.vector.wait_ge(sT.h, sT.n)
                                cp2 = nc.vector.tensor_copy(
                                    TSB[:, kk * 128:(kk + 1) * 128],
                                    psum.bank(b2)[:, 0:128])
                                psum.consumed(b2, cp2)
                            p.vmark(sV)
                            nc.sync.wait_ge(sV.h, sV.n)
                            nc.sync.dma_start(
                                out=scvt.ap()[tc * 512:(tc + 1) * 512,
                                              pr * 128:(pr + 1) * 128]
                                    .rearrange("(k p) d -> p k d", p=128),
                                in_=TSB[:].rearrange("p (k d) -> p k d", k=4),
                            ).then_inc(sQD.h, 16); sQD.n += 16
            with ExitStack() as ph:
                sQW = p.sem("qkvw"); sQD = p.sem("qkvd")
                for eng in (nc.sync, nc.vector, nc.scalar, nc.tensor):
                    eng.wait_ge(sQW.h, sQW.n)
                    eng.wait_ge(sQD.h, sQD.n)
                FSB = p.sb(ph, "FSB", [128, T])
                WCV2 = p.sb(ph, "WCV2", [128, 64])
                WKT = p.sb(ph, "WKT", [128, D])
                KSB = p.sb(ph, "KSB", [128, 512])
                sF = p.sem("fld")
                sKD = p.sem("kfd")
                UPF = BUFA
                for fi in range(FUT):
                    nc.sync.wait_ge(sT.h, sT.n)
                    nc.sync.dma_start(out=WCV2[0:64, :], in_=w1conv.ap()[fi]
                                      ).then_inc(sF.h, 16); sF.n += 16
                    nc.sync.dma_start(out=WCV2[64:128, :], in_=w1conv.ap()[fi]
                                      ).then_inc(sF.h, 16); sF.n += 16
                    for hp in range(8):
                        nc.sync.wait_ge(sT.h, sT.n)
                        nc.sync.dma_start(out=FSB[:],
                                          in_=scf.ap()[hp * 128:(hp + 1) * 128, :]
                                          ).then_inc(sF.h, 16); sF.n += 16
                        nc.tensor.wait_ge(sF.h, sF.n)
                        nc.tensor.wait_ge(sQD.h, sQD.n)
                        for tc in range(4):
                            b0 = psum.acquire(); b1 = psum.acquire()
                            nc.tensor.matmul(psum.bank(b0)[0:64, 0:512],
                                             WCV2[0:64, :],
                                             FSB[0:64, tc * 512: tc * 512 + 512],
                                             start=True, stop=True)
                            mm1 = nc.tensor.matmul(psum.bank(b1)[64:128, 0:512],
                                                   WCV2[64:128, :],
                                                   FSB[64:128, tc * 512: tc * 512 + 512],
                                                   start=True, stop=True,
                                                   tile_position=(64, 64))
                            mm1.then_inc(sT.h, 1); sT.n += 1
                            nc.vector.wait_ge(sT.h, sT.n)
                            dsl0 = UPF[0:64, hp * T + tc * 512: hp * T + tc * 512 + 512]
                            dsl1 = UPF[64:128, hp * T + tc * 512: hp * T + tc * 512 + 512]
                            c0 = nc.vector.tensor_copy(dsl0, psum.bank(b0)[0:64, 0:512])
                            psum.consumed(b0, c0)
                            c1 = nc.vector.tensor_copy(dsl1, psum.bank(b1)[64:128, 0:512])
                            psum.consumed(b1, c1)
                    p.vmark(sV)
                    nc.tensor.wait_ge(sV.h, sV.n)
                    for (col0, dstd) in ((D, sckf), (2 * D, scvf)):
                        for tc in range(4):
                            for jg in range(8):
                                nc.sync.wait_ge(sT.h, sT.n)
                                nc.sync.dma_start(
                                    out=WKT[:].rearrange("p (c j) -> p c j", c=8),
                                    in_=w1qkvf.ap()[:, col0 + jg * 128:
                                                    col0 + (jg + 1) * 128]
                                        .rearrange("(c p) j -> p c j", p=128),
                                ).then_inc(sF.h, 16); sF.n += 16
                                nc.tensor.wait_ge(sF.h, sF.n)
                                b = psum.acquire()
                                for dch in range(DCH):
                                    mm = nc.tensor.matmul(
                                        psum.bank(b)[:, 0:512],
                                        WKT[:, dch * 128:(dch + 1) * 128],
                                        UPF[:, dch * T + tc * 512:
                                            dch * T + tc * 512 + 512],
                                        start=(dch == 0), stop=(dch == DCH - 1))
                                mm.then_inc(sT.h, 1); sT.n += 1
                                nc.vector.wait_ge(sT.h, sT.n)
                                nc.vector.wait_ge(sKD.h, sKD.n)
                                cp = nc.vector.tensor_copy(KSB[:],
                                                           psum.bank(b)[:, 0:512])
                                psum.consumed(b, cp)
                                p.vmark(sV)
                                nc.sync.wait_ge(sV.h, sV.n)
                                nc.sync.dma_start(
                                    out=dstd.ap()[fi, jg * 128:(jg + 1) * 128,
                                                  tc * 512:(tc + 1) * 512],
                                    in_=KSB[:]).then_inc(sKD.h, 16); sKD.n += 16

        if _EMIT_UPTO >= 5:
            with ExitStack() as ph:
                sF = p.sem("fld"); sKD = p.sem("kfd")
                for eng in (nc.sync, nc.vector, nc.scalar, nc.tensor):
                    eng.wait_ge(sF.h, sF.n)
                    eng.wait_ge(sKD.h, sKD.n)
                OQ, OKk, OKV, OPB, OOU = 0, T, 2 * T, 3 * T, 7 * T
                QPs = BUFA[:, OQ:OQ + T]
                KPs = BUFA[:, OKk:OKk + T]
                KVF = BUFA[:, OKV:OKV + T]
                OUTP = BUFA[:, OOU:OOU + T]
                pbslot = lambda hh, stp: BUFA[:, OPB + (hh * 8 + stp) * 512:
                                              OPB + (hh * 8 + stp) * 512 + 512]
                FAW = p.sb(ph, "FAW", [8, T])
                RBCW = p.sb(ph, "RBCW", [128, 512])
                DROW = p.sb(ph, "DROW", [2, 256])
                RB8 = p.sb(ph, "RB8", [8, 256])
                WPJ = p.sb(ph, "WPJ", [128, D])
                sPR = p.sem("prld")
                for pr in range(8):
                    nc.sync.wait_ge(sT.h, sT.n)
                    nc.sync.wait_ge(sV.h, sV.n)
                    nc.sync.dma_start(out=QPs, in_=scq.ap()[pr * 128:(pr + 1) * 128, :]
                                      ).then_inc(sPR.h, 16); sPR.n += 16
                    nc.sync.dma_start(out=KPs, in_=sck.ap()[pr * 128:(pr + 1) * 128, :]
                                      ).then_inc(sPR.h, 16); sPR.n += 16
                    nc.sync.dma_start(out=KVF, in_=sckf.ap()[0, pr * 128:(pr + 1) * 128, :]
                                      ).then_inc(sPR.h, 16); sPR.n += 16
                    bfa = [psum.reserve() for _ in range(4)]
                    for fi in range(FUT):
                        if fi > 0:
                            nc.sync.wait_ge(sV.h, sV.n)
                            nc.sync.dma_start(
                                out=KVF, in_=sckf.ap()[fi, pr * 128:(pr + 1) * 128, :]
                            ).then_inc(sPR.h, 16); sPR.n += 16
                        nc.vector.wait_ge(sPR.h, sPR.n)
                        nc.vector.wait_ge(sT.h, sT.n)
                        nc.vector.drain()
                        nc.vector.tensor_tensor(OUTP, QPs, KVF, op=OP.mult)
                        p.vmark(sV)
                        nc.tensor.wait_ge(sV.h, sV.n)
                        for tc in range(4):
                            nc.tensor.matmul(
                                psum.bank(bfa[tc])[0:8, 0:512],
                                SELH8[:, fi * 8:(fi + 1) * 8],
                                OUTP[:, tc * 512: tc * 512 + 512],
                                start=(fi == 0), stop=(fi == FUT - 1)
                            ).then_inc(sT.h, 1); sT.n += 1
                    nc.scalar.wait_ge(sT.h, sT.n)
                    for tc in range(4):
                        i = nc.scalar.activation(out=FAW[:, tc * 512: tc * 512 + 512],
                                                 in_=psum.bank(bfa[tc])[0:8, 0:512],
                                                 func=AF.Exp)
                        psum.consumed(bfa[tc], i)
                        psum.release(bfa[tc])
                    p.smark(sS)
                    nc.sync.wait_ge(sV.h, sV.n)
                    nc.sync.dma_start(
                        out=KVF.rearrange("p (k d) -> p k d", k=16),
                        in_=scvt.ap()[:, pr * 128:(pr + 1) * 128]
                            .rearrange("(k p) d -> p k d", p=128),
                    ).then_inc(sPR.h, 16); sPR.n += 16
                    nc.tensor.wait_ge(sPR.h, sPR.n)
                    nc.vector.wait_ge(sS.h, sS.n)
                    for qc in range(8):
                        nst = 2 * (qc + 1)
                        for stp in range(nst // 2):
                            bq = [psum.acquire(), psum.acquire()]
                            for half in range(2):
                                st = stp * 2 + half
                                for hh in range(2):
                                    mm = nc.tensor.matmul(
                                        psum.bank(bq[hh])[:, half * 256:(half + 1) * 256],
                                        KPs[64 * hh:64 * hh + 64, st * 128:(st + 1) * 128],
                                        QPs[64 * hh:64 * hh + 64, qc * 256:(qc + 1) * 256],
                                        start=True, stop=True,
                                        tile_position=(64 * hh, 0))
                            mm.then_inc(sT.h, 1); sT.n += 1
                            nc.scalar.wait_ge(sT.h, sT.n)
                            for hh in range(2):
                                i = nc.scalar.activation(out=pbslot(hh, stp),
                                                         in_=psum.bank(bq[hh])[:, 0:512],
                                                         func=AF.Exp, scale=0.125)
                                psum.consumed(bq[hh], i)
                        p.smark(sS)
                        nc.vector.wait_ge(sS.h, sS.n)
                        for hh in range(2):
                            slot = pbslot(hh, nst // 2 - 1)
                            nc.vector.drain()
                            nc.vector.tensor_tensor(slot, slot, MKC[:], op=OP.mult)
                        p.vmark(sV)
                        nc.tensor.wait_ge(sV.h, sV.n)
                        bav = [psum.reserve(), psum.reserve()]
                        bden = psum.reserve()
                        for st in range(nst):
                            stp, half = st // 2, st % 2
                            for hh in range(2):
                                ps = BUFA[:, OPB + (hh * 8 + stp) * 512 + half * 256:
                                          OPB + (hh * 8 + stp) * 512 + half * 256 + 256]
                                nc.tensor.matmul(
                                    psum.bank(bden)[0:2, 0:256],
                                    ONES2[:, hh * 2:hh * 2 + 2], ps,
                                    start=(st == 0 and hh == 0), stop=False)
                                nc.tensor.matmul(
                                    psum.bank(bav[hh])[64 * hh:64 * hh + 64, 0:256],
                                    KVF[:, st * 128 + 64 * hh: st * 128 + 64 * hh + 64],
                                    ps, start=(st == 0), stop=(st == nst - 1),
                                    tile_position=(0, 64 * hh))
                        nc.tensor.matmul(psum.bank(bden)[0:2, 0:256], SEL8D[:],
                                         FAW[:, qc * 256:(qc + 1) * 256],
                                         start=False, stop=True)
                        p.tmark(sT)
                        nc.vector.wait_ge(sT.h, sT.n)
                        rc = nc.vector.reciprocal(DROW[:], psum.bank(bden)[0:2, 0:256])
                        psum.consumed(bden, rc)
                        p.vmark(sV)
                        nc.tensor.wait_ge(sV.h, sV.n)
                        brb = psum.acquire()
                        nc.tensor.matmul(psum.bank(brb)[:, 0:256], SEL2T[:], DROW[:],
                                         start=True, stop=True)
                        br8 = psum.acquire()
                        nc.tensor.matmul(psum.bank(br8)[0:8, 0:256], SEL2TO8[:], DROW[:],
                                         start=True, stop=True).then_inc(sT.h, 1)
                        sT.n += 1
                        nc.vector.wait_ge(sT.h, sT.n)
                        cp = nc.vector.tensor_copy(RBCW[:, 0:256], psum.bank(brb)[:, 0:256])
                        psum.consumed(brb, cp)
                        cp8 = nc.vector.tensor_copy(RB8[:], psum.bank(br8)[0:8, 0:256])
                        psum.consumed(br8, cp8)
                        nc.vector.drain()
                        for hh in range(2):
                            i = nc.vector.tensor_tensor(
                                OUTP[64 * hh:64 * hh + 64, qc * 256:(qc + 1) * 256],
                                psum.bank(bav[hh])[64 * hh:64 * hh + 64, 0:256],
                                RBCW[64 * hh:64 * hh + 64, 0:256], op=OP.mult)
                            psum.consumed(bav[hh], i)
                        nc.vector.drain()
                        fsl = FAW[:, qc * 256:(qc + 1) * 256]
                        nc.vector.tensor_tensor(fsl, fsl, RB8[:], op=OP.mult)
                        p.vmark(sV)
                        psum.release(bav[0]); psum.release(bav[1])
                        psum.release(bden)
                    for fi in range(FUT):
                        nc.sync.wait_ge(sT.h, sT.n)
                        nc.sync.wait_ge(sV.h, sV.n)
                        nc.sync.dma_start(
                            out=KVF, in_=scvf.ap()[fi, pr * 128:(pr + 1) * 128, :]
                        ).then_inc(sPR.h, 16); sPR.n += 16
                        nc.tensor.wait_ge(sPR.h, sPR.n)
                        nc.tensor.wait_ge(sV.h, sV.n)
                        for qc in range(8):
                            bwb = psum.acquire()
                            nc.tensor.matmul(psum.bank(bwb)[:, 0:256],
                                             SELF8[:, fi * 128:(fi + 1) * 128],
                                             FAW[:, qc * 256:(qc + 1) * 256],
                                             start=True, stop=True).then_inc(sT.h, 1)
                            sT.n += 1
                            nc.vector.wait_ge(sT.h, sT.n)
                            cw = nc.vector.tensor_copy(RBCW[:, 256:512],
                                                       psum.bank(bwb)[:, 0:256])
                            psum.consumed(bwb, cw)
                            nc.vector.drain()
                            nc.vector.tensor_tensor(RBCW[:, 0:256], RBCW[:, 256:512],
                                                    KVF[:, qc * 256:(qc + 1) * 256],
                                                    op=OP.mult)
                            nc.vector.drain()
                            osl = OUTP[:, qc * 256:(qc + 1) * 256]
                            nc.vector.tensor_tensor(osl, osl, RBCW[:, 0:256], op=OP.add)
                            nc.vector.drain()
                        p.vmark(sV)
                    nc.sync.wait_ge(sT.h, sT.n)
                    nc.sync.dma_start(out=WPJ[:], in_=w1projp.ap()[pr]
                                      ).then_inc(sPR.h, 16); sPR.n += 16
                    nc.tensor.wait_ge(sPR.h, sPR.n)
                    nc.tensor.wait_ge(sV.h, sV.n)
                    for j in range(DCH):
                        for tc in range(4):
                            b = psum.acquire()
                            nc.tensor.matmul(psum.bank(b)[:, 0:512],
                                             WPJ[:, j * 128:(j + 1) * 128],
                                             OUTP[:, tc * 512: tc * 512 + 512],
                                             start=True, stop=True).then_inc(sT.h, 1)
                            sT.n += 1
                            nc.vector.wait_ge(sT.h, sT.n)
                            asl = BUFB[:, j * T + tc * 512: j * T + tc * 512 + 512]
                            nc.vector.drain()
                            i = nc.vector.tensor_tensor(asl, asl,
                                                        psum.bank(b)[:, 0:512],
                                                        op=OP.add)
                            psum.consumed(b, i)
                    p.vmark(sV)

        if _EMIT_UPTO >= 5:
            with ExitStack() as ph:
                sPR = p.sem("prld")
                for eng in (nc.vector, nc.scalar, nc.tensor, nc.sync):
                    eng.wait_ge(sPR.h, sPR.n)
                emit_layernorm(BUFB, BUFA, ph, gb=(LN2G, LN2B))
            with ExitStack() as ph:
                nc.sync.wait_ge(sV.h, sV.n)
                nc.sync.wait_ge(sS.h, sS.n)
                nc.sync.wait_ge(sT.h, sT.n)
                WFT = p.sb(ph, "WFT", [128, DCH * 1024])
                WF2T = p.sb(ph, "WF2T", [128, D])
                G1 = p.sb(ph, "G1", [128, 8 * 256])
                ACC2 = p.sb(ph, "ACC2", [128, 8 * 256])
                sFF = p.sem("ffw")
                pb2 = [psum.reserve() for _ in range(4)]
                for tc in range(8):
                    for fg in range(4):
                        nc.sync.wait_ge(sT.h, sT.n)
                        nc.sync.dma_start(
                            out=WFT[:].rearrange("p (c j) -> p c j", c=DCH),
                            in_=w1ff1.ap()[:, fg * 1024:(fg + 1) * 1024]
                                .rearrange("(c p) j -> p c j", p=128),
                        ).then_inc(sFF.h, 16); sFF.n += 16
                        nc.tensor.wait_ge(sFF.h, sFF.n)
                        nc.tensor.wait_ge(sV.h, sV.n)
                        for f8 in range(8):
                            b = psum.acquire()
                            for dch in range(DCH):
                                mm = nc.tensor.matmul(
                                    psum.bank(b)[:, 0:256],
                                    WFT[:, dch * 1024 + f8 * 128:
                                        dch * 1024 + (f8 + 1) * 128],
                                    BUFA[:, dch * T + tc * 256: dch * T + tc * 256 + 256],
                                    start=(dch == 0), stop=(dch == DCH - 1))
                            mm.then_inc(sT.h, 1); sT.n += 1
                            nc.scalar.wait_ge(sT.h, sT.n)
                            i = nc.scalar.activation(out=G1[:, f8 * 256:(f8 + 1) * 256],
                                                     in_=psum.bank(b)[:, 0:256],
                                                     func=AF.Gelu)
                            psum.consumed(b, i)
                        p.smark(sS)
                        nc.tensor.wait_ge(sS.h, sS.n)
                        for jhalf in range(2):
                            for f8 in range(8):
                                nc.sync.wait_ge(sT.h, sT.n)
                                nc.sync.dma_start(
                                    out=WF2T[:],
                                    in_=w1ff2.ap()[fg * 1024 + f8 * 128:
                                                   fg * 1024 + (f8 + 1) * 128, :],
                                ).then_inc(sFF.h, 16); sFF.n += 16
                                nc.tensor.wait_ge(sFF.h, sFF.n)
                                for j4 in range(4):
                                    mm = nc.tensor.matmul(
                                        psum.bank(pb2[j4])[:, 0:256],
                                        WF2T[:, (jhalf * 4 + j4) * 128:
                                             (jhalf * 4 + j4 + 1) * 128],
                                        G1[:, f8 * 256:(f8 + 1) * 256],
                                        start=(f8 == 0), stop=(f8 == 7))
                                mm.then_inc(sT.h, 1); sT.n += 1
                            nc.vector.wait_ge(sT.h, sT.n)
                            for j4 in range(4):
                                j8 = jhalf * 4 + j4
                                asl = ACC2[:, j8 * 256:(j8 + 1) * 256]
                                nc.vector.drain()
                                if fg == 0:
                                    i = nc.vector.tensor_copy(asl,
                                                              psum.bank(pb2[j4])[:, 0:256])
                                else:
                                    i = nc.vector.tensor_tensor(
                                        asl, asl, psum.bank(pb2[j4])[:, 0:256], op=OP.add)
                                psum.consumed(pb2[j4], i)
                            p.vmark(sV)
                            nc.tensor.wait_ge(sV.h, sV.n)
                    for j8 in range(8):
                        asl = BUFB[:, j8 * T + tc * 256: j8 * T + tc * 256 + 256]
                        nc.vector.drain()
                        nc.vector.tensor_tensor(asl, asl, ACC2[:, j8 * 256:(j8 + 1) * 256],
                                                op=OP.add)
                    p.vmark(sV)
                for b in pb2:
                    psum.release(b)

        if _EMIT_UPTO >= 6:
            with ExitStack() as ph:
                sFF = p.sem("ffw")
                for eng in (nc.sync, nc.vector, nc.scalar, nc.tensor):
                    eng.wait_ge(sFF.h, sFF.n)
                nc.sync.wait_ge(sCC.h, sCC.n)
                sSC = p.sem("scat")
                LO = p.sb(ph, "LO", [128, 6])
                THROW = p.sb(ph, "THROW", [1, 128])
                ph6 = ExitStack()
                XGT = p.sb(ph6, "XGT", [128, T])
                GLT = p.sb(ph6, "GLT", [128, T])
                GROW = p.sb(ph6, "GROW", [16, T])
                SELPM = p.sb(ph6, "SELPM", [128, 16])
                SELPMT = p.sb(ph6, "SELPMT", [16, 128])
                sGD = p.sem("gated")
                sGL = p.sem("glw")
                nc.sync.dma_start(out=SELPM[:], in_=selpm.ap()
                                  ).then_inc(sGD.h, 16); sGD.n += 16
                nc.sync.dma_start(out=SELPMT[:], in_=selpmt.ap()
                                  ).then_inc(sGD.h, 16); sGD.n += 16
                for r in range(NCORE):
                    btc = [psum.reserve() for _ in range(4)]
                    for dch in range(DCH):
                        nc.sync.wait_ge(sT.h, sT.n)
                        nc.sync.dma_start(
                            out=XGT[:],
                            in_=xg_all.ap()[r * D + dch * 128:
                                            r * D + (dch + 1) * 128, :],
                        ).then_inc(sGD.h, 16); sGD.n += 16
                        nc.tensor.wait_ge(sGD.h, sGD.n)
                        for tc in range(4):
                            mm = nc.tensor.matmul(
                                psum.bank(btc[tc])[0:16, 0:512],
                                GWT[:, dch * 16:(dch + 1) * 16],
                                XGT[:, tc * 512: tc * 512 + 512],
                                start=(dch == 0), stop=(dch == DCH - 1))
                        mm.then_inc(sT.h, 1); sT.n += 1
                    nc.vector.wait_ge(sT.h, sT.n)
                    nc.vector.wait_ge(sGL.h, sGL.n)
                    for tc in range(4):
                        cp = nc.vector.tensor_copy(GROW[:, tc * 512: tc * 512 + 512],
                                                   psum.bank(btc[tc])[0:16, 0:512])
                        psum.consumed(btc[tc], cp)
                        psum.release(btc[tc])
                    p.vmark(sV)
                    nc.sync.wait_ge(sV.h, sV.n)
                    nc.sync.dma_start(out=GLT[r * 16:(r + 1) * 16, :], in_=GROW[:]
                                      ).then_inc(sGL.h, 16); sGL.n += 16
                CROW = p.sb(ph6, "CROW", [1, 16])
                CCOL = p.sb(ph6, "CCOL", [16, 1])
                CNTB = p.sb(ph6, "CNTB", [128, 1])
                MSK = p.sb(ph6, "MSK", [128, T])
                nc.vector.wait_ge(sGL.h, sGL.n)
                nc.tensor.wait_ge(sGD.h, sGD.n)
                nc.vector.memset(LO[:, 0:1], -64.0)
                nc.vector.memset(LO[:, 1:2], 64.0)
                p.vmark(sV)
                for it in range(52):
                    nc.vector.drain()
                    nc.vector.tensor_tensor(LO[:, 2:3], LO[:, 0:1], LO[:, 1:2],
                                            op=OP.add)
                    nc.vector.drain()
                    nc.vector.tensor_scalar(out=LO[:, 2:3], in0=LO[:, 2:3],
                                            scalar1=0.5, scalar2=None, op0=OP.mult)
                    nc.vector.drain()
                    nc.vector.tensor_scalar(out=MSK[:], in0=GLT[:],
                                            scalar1=LO[:, 2:3], scalar2=None,
                                            op0=OP.is_ge)
                    nc.vector.drain()
                    nc.vector.reduce_sum(CNTB[:], MSK[:], axis=mybir.AxisListType.X)
                    p.vmark(sV)
                    nc.tensor.wait_ge(sV.h, sV.n)
                    b1 = psum.acquire()
                    nc.tensor.matmul(psum.bank(b1)[0:1, 0:16], CNTB[:], SELPM[:],
                                     start=True, stop=True).then_inc(sT.h, 1)
                    sT.n += 1
                    nc.vector.wait_ge(sT.h, sT.n)
                    c1 = nc.vector.tensor_copy(CROW[:], psum.bank(b1)[0:1, 0:16])
                    psum.consumed(b1, c1)
                    p.vmark(sV)
                    nc.tensor.wait_ge(sV.h, sV.n)
                    b2 = psum.acquire()
                    nc.tensor.transpose(psum.bank(b2)[0:16, 0:1], CROW[:],
                                        IDENT[0:1, 0:1]).then_inc(sT.h, 1)
                    sT.n += 1
                    nc.vector.wait_ge(sT.h, sT.n)
                    c2 = nc.vector.tensor_copy(CCOL[:], psum.bank(b2)[0:16, 0:1])
                    psum.consumed(b2, c2)
                    p.vmark(sV)
                    nc.tensor.wait_ge(sV.h, sV.n)
                    b3 = psum.acquire()
                    nc.tensor.matmul(psum.bank(b3)[:, 0:1], SELPMT[:], CCOL[:],
                                     start=True, stop=True).then_inc(sT.h, 1)
                    sT.n += 1
                    nc.vector.wait_ge(sT.h, sT.n)
                    c3 = nc.vector.tensor_copy(CNTB[:], psum.bank(b3)[:, 0:1])
                    psum.consumed(b3, c3)
                    nc.vector.drain()
                    nc.vector.tensor_scalar(out=LO[:, 3:4], in0=CNTB[:],
                                            scalar1=2048.0, scalar2=None,
                                            op0=OP.is_ge)
                    nc.vector.drain()
                    nc.vector.tensor_tensor(LO[:, 4:5], LO[:, 2:3], LO[:, 0:1],
                                            op=OP.subtract)
                    nc.vector.tensor_tensor(LO[:, 5:6], LO[:, 1:2], LO[:, 2:3],
                                            op=OP.subtract)
                    nc.vector.drain()
                    nc.vector.scalar_tensor_tensor(out=LO[:, 0:1], in0=LO[:, 4:5],
                                                   scalar=LO[:, 3:4], in1=LO[:, 0:1],
                                                   op0=OP.mult, op1=OP.add)
                    nc.vector.scalar_tensor_tensor(out=LO[:, 1:2], in0=LO[:, 5:6],
                                                   scalar=LO[:, 3:4], in1=LO[:, 2:3],
                                                   op0=OP.mult, op1=OP.add)
                p.vmark(sV)

                nc.tensor.wait_ge(sV.h, sV.n)
                bt0 = psum.acquire()
                nc.tensor.transpose(psum.bank(bt0)[0:1, 0:128], LO[:, 0:1],
                                    IDENT[:]).then_inc(sT.h, 1); sT.n += 1
                nc.vector.wait_ge(sT.h, sT.n)
                ct0 = nc.vector.tensor_copy(THROW[:], psum.bank(bt0)[0:1, 0:128])
                psum.consumed(bt0, ct0)
                p.vmark(sV)
                ph6.close()
                IOTAC = p.sb(ph, "IOTAC", [128, 384])
                IOTAR = p.sb(ph, "IOTAR", [128, 3])
                ROWS = BUFA[0:1, 0:5 * T]
                ZROW = p.sb(ph, "ZROW", [1, T])
                POSBC = p.sb(ph, "POSBC", [128, T])
                SIGBC = p.sb(ph, "SIGBC", [128, T])
                POSTX = p.sb(ph, "POSTX", [128, 16])
                OHT = p.sb(ph, "OHT", [128, 3 * T])
                OH1 = p.sb(ph, "OH1", [128, 384])
                XTEK = p.sb(ph, "XTEK", [128, D])
                WUG = p.sb(ph, "WUG", [128, D])
                DWN = p.sb(ph, "DWN", [128, D])
                DBB = BUFA[:, 32 * 384:32 * 384 + 1024]
                DBROW = BUFA[0:1, 32 * 384 + 1024:32 * 384 + 2048]
                UBF = p.sb(ph, "UBF", [128, 32])
                GBF = p.sb(ph, "GBF", [128, 32])
                EO = BUFA[:, 0:32 * 384]
                XIN = BUFA[:, 32 * 384:32 * 384 + 8 * 384]
                DVN = BUFA[:, 0:3 * 1024]
                TME = BUFA[:, 32 * 384 + 8 * 384: 32 * 384 + 8 * 384 + 384]
                sEW = p.sem("expw")
                sGA = p.sem("gath")
                nc.sync.wait_ge(sV.h, sV.n)
                nc.sync.dma_start(out=IOTAC[:], in_=iotac.ap()
                                  ).then_inc(sEW.h, 16); sEW.n += 16
                nc.sync.dma_start(out=IOTAR[:], in_=iotar.ap()
                                  ).then_inc(sEW.h, 16); sEW.n += 16
                nc.vector.wait_ge(sEW.h, sEW.n)
                nc.vector.memset(ZROW[:], 0.0)
                p.vmark(sV)
                rR = lambda i: ROWS[0:1, i * T:(i + 1) * T]
                for e in range(E):
                    nc.sync.wait_ge(sT.h, sT.n)
                    nc.sync.wait_ge(sV.h, sV.n)
                    nc.sync.dma_start(out=rR(4), in_=LLOG[e:e + 1, :]
                                      ).then_inc(sEW.h, 16); sEW.n += 16
                    nc.vector.wait_ge(sT.h, sT.n)
                    nc.vector.wait_ge(sEW.h, sEW.n)
                    nc.vector.drain()
                    nc.vector.tensor_scalar(out=rR(0), in0=rR(4),
                                            scalar1=THROW[0:1, e:e + 1],
                                            scalar2=None, op0=OP.is_ge)
                    nc.vector.drain()
                    nc.vector.tensor_tensor_scan(rR(1), rR(0), ZROW[:], 0.0,
                                                 op0=OP.add, op1=OP.add)
                    nc.vector.drain()
                    nc.vector.tensor_tensor(rR(2), rR(1), rR(0), op=OP.mult)
                    p.vmark(sV)
                    nc.scalar.wait_ge(sV.h, sV.n)
                    nc.scalar.activation(out=rR(3), in_=rR(4), func=AF.Sigmoid)
                    p.smark(sS)
                    nc.tensor.wait_ge(sS.h, sS.n)
                    bcast_row(rR(2), POSBC[:], T)
                    bcast_row(rR(3), SIGBC[:], T)
                    for k in range(16):
                        bx = psum.acquire()
                        nc.tensor.transpose(psum.bank(bx)[:, 0:1],
                                            ROWS[0:1, 2 * T + k * 128: 2 * T + (k + 1) * 128],
                                            IDENT[0:1, 0:1]).then_inc(sT.h, 1)
                        sT.n += 1
                        nc.vector.wait_ge(sT.h, sT.n)
                        cx = nc.vector.tensor_copy(POSTX[:, k:k + 1],
                                                   psum.bank(bx)[:, 0:1])
                        psum.consumed(bx, cx)
                    p.vmark(sV)
                    bg = [psum.reserve() for _ in range(8)]
                    for k in range(16):
                        nc.sync.wait_ge(sT.h, sT.n)
                        nc.sync.dma_start(out=XTEK[:],
                                          in_=scxt.ap()[k * 128:(k + 1) * 128, :]
                                          ).then_inc(sGA.h, 16); sGA.n += 16
                        nc.vector.wait_ge(sT.h, sT.n)
                        nc.vector.wait_ge(sV.h, sV.n)
                        nc.vector.drain()
                        nc.vector.tensor_scalar(out=OH1[:], in0=IOTAC[:],
                                                scalar1=POSTX[:, k:k + 1],
                                                scalar2=None, op0=OP.is_equal)
                        p.vmark(sV)
                        nc.tensor.wait_ge(sGA.h, sGA.n)
                        nc.tensor.wait_ge(sV.h, sV.n)
                        for dch in range(DCH):
                            mm = nc.tensor.matmul(
                                psum.bank(bg[dch])[:, 0:384],
                                XTEK[:, dch * 128:(dch + 1) * 128], OH1[:],
                                start=(k == 0), stop=(k == 15))
                        mm.then_inc(sT.h, 1); sT.n += 1
                    nc.vector.wait_ge(sT.h, sT.n)
                    nc.vector.wait_ge(sSC.h, sSC.n)
                    for dch in range(DCH):
                        cx = nc.vector.tensor_copy(
                            XIN[:, dch * 384:(dch + 1) * 384],
                            psum.bank(bg[dch])[:, 0:384])
                        psum.consumed(bg[dch], cx)
                        psum.release(bg[dch])
                    p.vmark(sV)
                    nc.sync.wait_ge(sS.h, sS.n)
                    nc.sync.wait_ge(sV.h, sV.n)
                    nc.sync.dma_start(out=UBF[:], in_=upbf.ap()[e]
                                      ).then_inc(sEW.h, 16); sEW.n += 16
                    nc.sync.dma_start(out=GBF[:], in_=gbf.ap()[e]
                                      ).then_inc(sEW.h, 16); sEW.n += 16
                    bdown = [psum.reserve() for _ in range(6)]
                    for fc in range(32):
                        nc.sync.wait_ge(sT.h, sT.n)
                        nc.sync.dma_start(
                            out=WUG[:].rearrange("p (c j) -> p c j", c=8),
                            in_=upwt.ap()[e, :, fc * 128:(fc + 1) * 128]
                                .rearrange("(c p) j -> p c j", p=128),
                        ).then_inc(sEW.h, 16); sEW.n += 16
                        nc.tensor.wait_ge(sEW.h, sEW.n)
                        nc.tensor.wait_ge(sV.h, sV.n)
                        bup = psum.acquire()
                        for dch in range(DCH):
                            mm = nc.tensor.matmul(
                                psum.bank(bup)[:, 0:384],
                                WUG[:, dch * 128:(dch + 1) * 128],
                                XIN[:, dch * 384: dch * 384 + 384],
                                start=(dch == 0), stop=(dch == DCH - 1))
                        mm.then_inc(sT.h, 1); sT.n += 1
                        nc.sync.wait_ge(sT.h, sT.n)
                        nc.sync.dma_start(
                            out=WUG[:].rearrange("p (c j) -> p c j", c=8),
                            in_=gwt2.ap()[e, :, fc * 128:(fc + 1) * 128]
                                .rearrange("(c p) j -> p c j", p=128),
                        ).then_inc(sEW.h, 16); sEW.n += 16
                        nc.tensor.wait_ge(sEW.h, sEW.n)
                        bgw = psum.acquire()
                        for dch in range(DCH):
                            mm = nc.tensor.matmul(
                                psum.bank(bgw)[:, 0:384],
                                WUG[:, dch * 128:(dch + 1) * 128],
                                XIN[:, dch * 384: dch * 384 + 384],
                                start=(dch == 0), stop=(dch == DCH - 1))
                        mm.then_inc(sT.h, 1); sT.n += 1
                        nc.scalar.wait_ge(sT.h, sT.n)
                        nc.scalar.wait_ge(sSC.h, sSC.n)
                        i1 = nc.scalar.activation(out=TME, in_=psum.bank(bgw)[:, 0:384],
                                                  func=AF.Silu,
                                                  bias=GBF[:, fc:fc + 1], scale=1.0)
                        psum.consumed(bgw, i1)
                        esl = EO[:, fc * 384:(fc + 1) * 384]
                        i2 = nc.scalar.activation(out=esl, in_=psum.bank(bup)[:, 0:384],
                                                  func=AF.Identity,
                                                  bias=UBF[:, fc:fc + 1], scale=1.0)
                        psum.consumed(bup, i2)
                        p.smark(sS)
                        nc.vector.wait_ge(sS.h, sS.n)
                        nc.vector.drain()
                        nc.vector.tensor_tensor(esl, esl, TME, op=OP.mult)
                        p.vmark(sV)
                        nc.sync.wait_ge(sT.h, sT.n)
                        nc.sync.dma_start(
                            out=DWN[:],
                            in_=dwnt.ap()[e, fc * 128:(fc + 1) * 128, :],
                        ).then_inc(sEW.h, 16); sEW.n += 16
                        nc.tensor.wait_ge(sEW.h, sEW.n)
                        nc.tensor.wait_ge(sV.h, sV.n)
                        for tt in range(3):
                            for dh in range(2):
                                mm = nc.tensor.matmul(
                                    psum.bank(bdown[tt * 2 + dh])[:, 0:512],
                                    EO[:, fc * 384 + tt * 128: fc * 384 + (tt + 1) * 128],
                                    DWN[:, dh * 512:(dh + 1) * 512],
                                    start=(fc == 0), stop=(fc == 31))
                        mm.then_inc(sT.h, 1); sT.n += 1
                    nc.sync.wait_ge(sT.h, sT.n)
                    nc.sync.dma_start(out=DBROW, in_=dbw.ap()[e:e + 1, :]
                                      ).then_inc(sEW.h, 16); sEW.n += 16
                    nc.tensor.wait_ge(sEW.h, sEW.n)
                    for half in range(2):
                        bd = psum.acquire()
                        nc.tensor.matmul(psum.bank(bd)[:, 0:512], ONESR[:],
                                         DBROW[:, half * 512:(half + 1) * 512],
                                         start=True, stop=True).then_inc(sT.h, 1)
                        sT.n += 1
                        nc.vector.wait_ge(sT.h, sT.n)
                        cd = nc.vector.tensor_copy(DBB[:, half * 512:(half + 1) * 512],
                                                   psum.bank(bd)[:, 0:512])
                        psum.consumed(bd, cd)
                    p.vmark(sV)
                    nc.vector.wait_ge(sT.h, sT.n)
                    for tt in range(3):
                        for dh in range(2):
                            bb = bdown[tt * 2 + dh]
                            dsl = DVN[:, tt * 1024 + dh * 512: tt * 1024 + (dh + 1) * 512]
                            nc.vector.drain()
                            i = nc.vector.tensor_tensor(dsl, psum.bank(bb)[:, 0:512],
                                                        DBB[:, dh * 512:(dh + 1) * 512],
                                                        op=OP.add)
                            psum.consumed(bb, i)
                            psum.release(bb)
                    for tt in range(3):
                        osl = OHT[:, tt * T:(tt + 1) * T]
                        nc.vector.drain()
                        nc.vector.tensor_scalar(out=osl, in0=POSBC[:],
                                                scalar1=IOTAR[:, tt:tt + 1],
                                                scalar2=None, op0=OP.is_equal)
                        nc.vector.drain()
                        nc.vector.tensor_tensor(osl, osl, SIGBC[:], op=OP.mult)
                    p.vmark(sV)
                    nc.tensor.wait_ge(sV.h, sV.n)
                    for dch in range(DCH):
                        for tc in range(4):
                            bb = psum.acquire()
                            for tt in range(3):
                                mm = nc.tensor.matmul(
                                    psum.bank(bb)[:, 0:512],
                                    DVN[:, tt * 1024 + dch * 128:
                                        tt * 1024 + dch * 128 + 128],
                                    OHT[:, tt * T + tc * 512: tt * T + tc * 512 + 512],
                                    start=(tt == 0), stop=(tt == 2))
                            mm.then_inc(sT.h, 1); sT.n += 1
                            nc.vector.wait_ge(sT.h, sT.n)
                            asl = BUFB[:, dch * T + tc * 512: dch * T + tc * 512 + 512]
                            nc.vector.drain()
                            i = nc.vector.tensor_tensor(asl, asl,
                                                        psum.bank(bb)[:, 0:512],
                                                        op=OP.add)
                            psum.consumed(bb, i)
                    p.vmark(sV)
                    nc.vector.drain().then_inc(sSC.h, 1); sSC.n += 1

        if _EMIT_UPTO >= 8:
            with ExitStack() as ph:
                sEW = p.sem("expw"); sGA = p.sem("gath"); sSC = p.sem("scat")
                for eng in (nc.sync, nc.vector, nc.scalar, nc.tensor):
                    eng.wait_ge(sEW.h, sEW.n)
                    eng.wait_ge(sGA.h, sGA.n)
                    eng.wait_ge(sSC.h, sSC.n)
                XN2S = p.sb(ph, "XN2S", [128, DCH * 512])
                WSH = p.sb(ph, "WSH", [128, D])
                GS = BUFA[:, 0:16 * 512]
                US = BUFA[:, 16 * 512:32 * 512]
                sSH = p.sem("shw")
                for tc in range(4):
                    nc.sync.wait_ge(sT.h, sT.n)
                    nc.sync.dma_start(
                        out=XN2S[:].rearrange("p (c t) -> p c t", c=DCH),
                        in_=xg_in.ap()[:, tc * 512:(tc + 1) * 512]
                            .rearrange("(c p) t -> p c t", p=128),
                    ).then_inc(sSH.h, 16); sSH.n += 16
                    for (wsrc, dst, fn) in ((shgt, GS, AF.Silu), (shut, US, AF.Identity)):
                        for fc in range(16):
                            nc.sync.wait_ge(sT.h, sT.n)
                            nc.sync.dma_start(
                                out=WSH[:].rearrange("p (c j) -> p c j", c=8),
                                in_=wsrc.ap()[:, fc * 128:(fc + 1) * 128]
                                    .rearrange("(c p) j -> p c j", p=128),
                            ).then_inc(sSH.h, 16); sSH.n += 16
                            nc.tensor.wait_ge(sSH.h, sSH.n)
                            nc.tensor.wait_ge(sV.h, sV.n)
                            bb = psum.acquire()
                            for dch in range(DCH):
                                mm = nc.tensor.matmul(
                                    psum.bank(bb)[:, 0:512],
                                    WSH[:, dch * 128:(dch + 1) * 128],
                                    XN2S[:, dch * 512: dch * 512 + 512],
                                    start=(dch == 0), stop=(dch == DCH - 1))
                            mm.then_inc(sT.h, 1); sT.n += 1
                            nc.scalar.wait_ge(sT.h, sT.n)
                            i = nc.scalar.activation(out=dst[:, fc * 512:(fc + 1) * 512],
                                                     in_=psum.bank(bb)[:, 0:512],
                                                     func=fn)
                            psum.consumed(bb, i)
                    p.smark(sS)
                    nc.vector.wait_ge(sS.h, sS.n)
                    nc.vector.drain()
                    nc.vector.tensor_tensor(GS, GS, US, op=OP.mult)
                    p.vmark(sV)
                    bsh = [psum.reserve() for _ in range(8)]
                    for fc in range(16):
                        nc.sync.wait_ge(sT.h, sT.n)
                        nc.sync.dma_start(
                            out=WSH[:].rearrange("p (c j) -> p c j", c=8),
                            in_=shdt.ap()[fc * 128:(fc + 1) * 128, :]
                                .rearrange("f (c j) -> f c j", c=8),
                        ).then_inc(sSH.h, 16); sSH.n += 16
                        nc.tensor.wait_ge(sSH.h, sSH.n)
                        nc.tensor.wait_ge(sV.h, sV.n)
                        for jg in range(8):
                            mm = nc.tensor.matmul(
                                psum.bank(bsh[jg])[:, 0:512],
                                WSH[:, jg * 128:(jg + 1) * 128],
                                GS[:, fc * 512:(fc + 1) * 512]
                                if False else BUFA[:, fc * 512:(fc + 1) * 512],
                                start=(fc == 0), stop=(fc == 15))
                        mm.then_inc(sT.h, 1); sT.n += 1
                    nc.vector.wait_ge(sT.h, sT.n)
                    for jg in range(8):
                        asl = BUFB[:, jg * T + tc * 512: jg * T + tc * 512 + 512]
                        nc.vector.drain()
                        i = nc.vector.tensor_tensor(asl, asl,
                                                    psum.bank(bsh[jg])[:, 0:512],
                                                    op=OP.add)
                        psum.consumed(bsh[jg], i)
                        psum.release(bsh[jg])
                    p.vmark(sV)

        if _EMIT_UPTO >= 9:
            with ExitStack() as ph:
                sSH = p.sem("shw"); sSC = p.sem("scat")
                for eng in (nc.sync, nc.vector, nc.scalar, nc.tensor):
                    eng.wait_ge(sSH.h, sSH.n)
                    eng.wait_ge(sSC.h, sSC.n)
                nc.sync.wait_ge(sT.h, sT.n)
                nc.sync.wait_ge(sV.h, sV.n)
                nc.sync.wait_ge(sS.h, sS.n)
                sXR = p.sem("xrl")
                nc.sync.dma_start(out=BUFA[:], in_=sc_x.ap()
                                  ).then_inc(sXR.h, 16); sXR.n += 16
                nc.vector.wait_ge(sXR.h, sXR.n)
                for dch in range(DCH):
                    nc.vector.drain()
                    asl = BUFA[:, dch * T:(dch + 1) * T]
                    nc.vector.tensor_tensor(asl, asl, BUFB[:, dch * T:(dch + 1) * T],
                                            op=OP.add)
                p.vmark(sV)
                nc.tensor.wait_ge(sV.h, sV.n)
                OT = p.sb(ph, "OT", [128, D])
                for k in range(16):
                    nc.vector.wait_ge(sO.h, sO.n)
                    for dch in range(DCH):
                        bx = psum.acquire()
                        nc.tensor.transpose(
                            psum.bank(bx)[:, 0:128],
                            BUFA[:, dch * T + k * 128: dch * T + k * 128 + 128],
                            IDENT[:]).then_inc(sT.h, 1)
                        sT.n += 1
                        nc.vector.wait_ge(sT.h, sT.n)
                        cp = nc.vector.tensor_copy(OT[:, dch * 128:(dch + 1) * 128],
                                                   psum.bank(bx)[:, 0:128])
                        psum.consumed(bx, cp)
                    p.vmark(sV)
                    nc.sync.wait_ge(sV.h, sV.n)
                    nc.sync.dma_start(out=out_t.ap()[k * 128:(k + 1) * 128, :],
                                      in_=OT[:]).then_inc(sO.h, 16); sO.n += 16

        if _DEBUG_DUMP is not None:
            name = _DEBUG_DUMP[0]
            src = {"BUFA": BUFA, "BUFB": BUFB}[name]
            dbg = p.dout("dbg", [128, DCH * T])
            for s in (sV, sS, sT):
                nc.sync.wait_ge(s.h, s.n)
            nc.sync.dma_start(out=dbg.ap(), in_=src[:]).then_inc(sO.h, 16); sO.n += 16

        if _EMIT_UPTO < 9:
            with ExitStack() as ph:
                OT = p.sb(ph, "OT", [128, D])
                nc.tensor.wait_ge(sV.h, sV.n)
                nc.tensor.wait_ge(sS.h, sS.n)
                for k in range(16):
                    nc.vector.wait_ge(sO.h, sO.n)
                    for dch in range(DCH):
                        b = psum.acquire()
                        nc.tensor.transpose(
                            psum.bank(b)[:, 0:128],
                            BUFA[:, dch * T + k * 128: dch * T + k * 128 + 128],
                            IDENT[:]).then_inc(sT.h, 1)
                        sT.n += 1
                        nc.vector.wait_ge(sT.h, sT.n)
                        cp = nc.vector.tensor_copy(OT[:, dch * 128: dch * 128 + 128],
                                                   psum.bank(b)[:, 0:128])
                        psum.consumed(b, cp)
                    p.vmark(sV)
                    nc.sync.wait_ge(sV.h, sV.n)
                    nc.sync.dma_start(out=out_t.ap()[k * 128:(k + 1) * 128, :],
                                      in_=OT[:]).then_inc(sO.h, 16); sO.n += 16
        nc.sync.wait_ge(sO.h, sO.n)

    return p


def _host_inputs(inputs):
    f = np.float32
    c = np.ascontiguousarray
    w_in = inputs["L0_attn_w_in"]
    wq, wk, wv = w_in[0:D], w_in[D:2 * D], w_in[2 * D:3 * D]
    w0qkvp = np.empty((8, D, 384), f)
    for pr in range(8):
        r = slice(128 * pr, 128 * pr + 128)
        w0qkvp[pr, :, 0:128] = wq[r].T
        w0qkvp[pr, :, 128:256] = wk[r].T
        w0qkvp[pr, :, 256:384] = wv[r].T
    b_in = inputs["L0_attn_b_in"]
    b0qkv = np.zeros((128, 24), f)
    for pr in range(8):
        b0qkv[:, pr] = b_in[128 * pr:128 * pr + 128]
        b0qkv[:, 8 + pr] = b_in[D + 128 * pr:D + 128 * pr + 128]
        b0qkv[:, 16 + pr] = b_in[2 * D + 128 * pr:2 * D + 128 * pr + 128]
    w_out = inputs["L0_attn_w_out"]
    w0outp = np.empty((8, 128, D), f)
    for pr in range(8):
        w0outp[pr] = w_out.T[128 * pr:128 * pr + 128, :]
    b0out = c(inputs["L0_attn_b_out"].reshape(DCH, 128).T)

    com = {
        "ident": np.eye(128, dtype=f),
        "ones_col": np.ones((128, 1), f),
        "ones_row": np.ones((1, 128), f),
        "mask_c": np.concatenate([
            (np.arange(128)[:, None] <= np.arange(256)[None, :]).astype(f),
            (np.arange(128)[:, None] <= (np.arange(256)[None, :] - 128)).astype(f)],
            axis=1),
        "ones2": np.array([[1, 0, 0, 1]], f).repeat(128, 0),
        "sel2t": np.concatenate([np.ones((2, 64), f) * np.array([[1], [0]], f),
                                 np.ones((2, 64), f) * np.array([[0], [1]], f)],
                                axis=1),
        "w0qkvp": w0qkvp, "b0qkv": b0qkv, "w0outp": w0outp, "b0out": b0out,
        "w0r": c(inputs["L0_ffn_W_r"].T), "w0k": c(inputs["L0_ffn_W_k"].T),
        "w0v": c(inputs["L0_ffn_W_v"].T),
        "mur": c(inputs["L0_ffn_mu_r"].reshape(DCH, 128).T),
        "muk": c(inputs["L0_ffn_mu_k"].reshape(DCH, 128).T),
    }
    com["w1qkvf"] = c(inputs["L1_attn_W_qkvf"].T)
    com["w1conv"] = c(inputs["L1_attn_W_conv"].transpose(2, 0, 1))
    wproj_t = inputs["L1_attn_W_proj"].T
    w1projp = np.empty((8, 128, D), f)
    for pr in range(8):
        w1projp[pr] = wproj_t[128 * pr:128 * pr + 128, :]
    com["w1projp"] = w1projp
    com["w1ff1"] = c(inputs["L1_attn_W_ff1"])
    com["w1ff2"] = c(inputs["L1_attn_W_ff2"])
    for nm, key in (("ln1g", "L1_attn_ln1_g"), ("ln1b", "L1_attn_ln1_b"),
                    ("ln2g", "L1_attn_ln2_g"), ("ln2b", "L1_attn_ln2_b")):
        com[nm] = c(inputs[key].reshape(DCH, 128).T)
    selh8 = np.zeros((128, 32), f)
    for fi in range(4):
        for hh in range(2):
            selh8[64 * hh:64 * hh + 64, fi * 8 + hh * 4 + fi] = 1.0
    com["selh8"] = selh8
    sel8d = np.zeros((8, 2), f)
    sel8d[0:4, 0] = 1.0
    sel8d[4:8, 1] = 1.0
    com["sel8d"] = sel8d
    sel2to8 = np.zeros((2, 8), f)
    sel2to8[0, 0:4] = 1.0
    sel2to8[1, 4:8] = 1.0
    com["sel2to8"] = sel2to8
    self8 = np.zeros((8, 4 * 128), f)
    for fi in range(4):
        self8[fi, fi * 128: fi * 128 + 64] = 1.0
        self8[4 + fi, fi * 128 + 64: fi * 128 + 128] = 1.0
    com["self8"] = self8
    com["gwt"] = c(inputs["L1_ffn_gate_w"].T)
    iot = (np.arange(16)[:, None] * 128 + np.arange(128)[None, :]).astype(f)
    com["iotap1"] = iot + 1.0
    com["iotac"] = np.broadcast_to(np.arange(1, 385, dtype=f)[None, :], (128, 384)).copy()
    com["iotar"] = (np.arange(3)[None, :] * 128 + np.arange(128)[:, None] + 1).astype(f)
    com["selpm"] = (np.arange(128)[:, None] % 16 == np.arange(16)[None, :]).astype(f)
    com["selpmt"] = c(com["selpm"].T)
    com["upwt"] = c(inputs["L1_ffn_up_w"].transpose(0, 2, 1))
    com["gwt2"] = c(inputs["L1_ffn_gw"].transpose(0, 2, 1))
    com["dwnt"] = c(inputs["L1_ffn_down_w"].transpose(0, 2, 1))
    com["upbf"] = c(inputs["L1_ffn_up_b"].reshape(E, 32, 128).transpose(0, 2, 1))
    com["gbf"] = c(inputs["L1_ffn_gb"].reshape(E, 32, 128).transpose(0, 2, 1))
    com["dbw"] = c(inputs["L1_ffn_down_b"])
    com["shgt"] = c(inputs["L1_ffn_sh_gate_w"].T)
    com["shut"] = c(inputs["L1_ffn_sh_up_w"].T)
    com["shdt"] = c(inputs["L1_ffn_sh_down_w"].T)
    x = inputs["L0_attn_x"]
    maps = []
    for core in range(NCORE):
        m = dict(com)
        m["x_in"] = c(x[core])
        maps.append(m)
    return maps


def kernel(**inputs):
    maps = _host_inputs(inputs)
    p = build_program()
    res = run_bass_kernel_spmd(p.nc, maps, core_ids=list(range(NCORE)))
    out = np.stack([res.results[i]["out"] for i in range(NCORE)], axis=0)
    return out.astype(np.float32)


# version 05
# speedup vs baseline: 1.2354x; 1.2354x over previous
import os
import numpy as np
from contextlib import ExitStack
import concourse.bass as bass
import concourse.mybir as mybir
from concourse.bass_utils import run_bass_kernel_spmd
from concourse.alu_op_type import AluOpType as OP

F32 = mybir.dt.float32
I16 = mybir.dt.int16
I32 = mybir.dt.int32
AF = mybir.ActivationFunctionType

D = 1024
T = 2048
NCORE = 8
DCH = D // 128
H = 16
HS = 64
FUT = 4
E = 16
FF = 4096
SHD = 2048
CAP = 384
EPS = 1e-5

_EMIT_UPTO = 99
_DEBUG_DUMP = None


class Sem:
    __slots__ = ("h", "n", "name")

    def __init__(self, stack, nc, name):
        self.h = stack.enter_context(nc.semaphore(name))
        self.n = 0
        self.name = name


class Prog:
    def __init__(self):
        self.nc = bass.Bass("TRN2", target_bir_lowering=False,
                            debug=bool(os.environ.get("BASSDBG")))
        self.stack = ExitStack()
        self.sems = {}
        self.dram = {}

    def sem(self, name):
        if name not in self.sems:
            self.sems[name] = Sem(self.stack, self.nc, name)
        return self.sems[name]

    _names = None

    def sb(self, stack, name, shape, dtype=F32):
        if self._names is None:
            self._names = {}
        n = self._names.get(name, 0)
        self._names[name] = n + 1
        if n:
            name = f"{name}_{n}"
        return stack.enter_context(self.nc.sbuf_tensor(name, list(shape), dtype))

    def din(self, name, shape, dtype=F32):
        t = self.nc.dram_tensor(name, list(shape), dtype, kind="ExternalInput")
        self.dram[name] = t
        return t

    def dout(self, name, shape, dtype=F32):
        t = self.nc.dram_tensor(name, list(shape), dtype, kind="ExternalOutput")
        self.dram[name] = t
        return t

    def dtmp(self, name, shape, dtype=F32):
        t = self.nc.dram_tensor(name, list(shape), dtype)
        self.dram[name] = t
        return t

    def vmark(self, sem):
        self.nc.vector.drain().then_inc(sem.h, 1); sem.n += 1

    def smark(self, sem):
        self.nc.scalar.drain().then_inc(sem.h, 1); sem.n += 1

    def tmark(self, sem):
        self.nc.tensor.drain().then_inc(sem.h, 1); sem.n += 1


class PsumPool:

    def __init__(self, p, stack):
        self.p = p
        self.banks = [stack.enter_context(p.nc.psum_tensor(f"pb{i}", [128, 512], F32))
                      for i in range(8)]
        self.sems = [p.sem(f"pbs{i}") for i in range(8)]
        self.pending = [0] * 8
        self.rr = 0

    def __post(self):
        pass

    def acquire(self, eng="tensor"):
        for _ in range(8):
            i = self.rr
            self.rr = (self.rr + 1) % 8
            if i not in getattr(self, "reserved", ()):
                break
        if self.pending[i] > 0:
            getattr(self.p.nc, eng).wait_ge(self.sems[i].h, self.pending[i])
        return i

    def reserve(self, eng="tensor"):
        if not hasattr(self, "reserved"):
            self.reserved = set()
        i = self.acquire(eng)
        self.reserved.add(i)
        return i

    def release(self, i):
        self.reserved.discard(i)

    def bank(self, i):
        return self.banks[i]

    def consumed(self, i, instr):
        instr.then_inc(self.sems[i].h, 1)
        self.sems[i].n += 1
        self.pending[i] = self.sems[i].n


class DblW:

    def __init__(self, p, ph, name, cols, dtype=F32):
        self.p = p
        self.bufs = [p.sb(ph, name + "0", [128, cols], dtype),
                     p.sb(ph, name + "1", [128, cols], dtype)]
        self.sems = [p.sem(name + "s0"), p.sem(name + "s1")]
        self.freed = [0, 0]
        self.n = 0
        self.c = 0

    def issue(self, in_ap, rearr_c=None):
        nc = self.p.nc
        i = self.n % 2
        sT = self.p.sem("ten")
        nc.sync.wait_ge(sT.h, self.freed[i])
        out = (self.bufs[i][:].rearrange("p (c j) -> p c j", c=rearr_c)
               if rearr_c else self.bufs[i][:])
        nc.sync.dma_start(out=out, in_=in_ap).then_inc(self.sems[i].h, 16)
        self.sems[i].n += 16
        self.n += 1

    def consume(self):
        k = self.c
        self.c += 1
        i = k % 2
        self.p.nc.tensor.wait_ge(self.sems[i].h, (k // 2 + 1) * 16)
        return self.bufs[i]

    def done(self):
        self.freed[(self.c - 1) % 2] = self.p.sem("ten").n


def build_program():
    p = Prog()
    nc = p.nc

    x_in = p.din("x_in", [T, D])
    out_t = p.dout("out", [T, D])
    ident = p.din("ident", [128, 128])
    ones = p.din("ones_col", [128, 1])
    onesr = p.din("ones_row", [1, 128])
    maskc = p.din("mask_c", [128, 512])
    ones2 = p.din("ones2", [128, 4])
    sel2t = p.din("sel2t", [2, 128])
    w0qkvp = p.din("w0qkvp", [8, D, 384])
    b0qkv = p.din("b0qkv", [128, 24])
    w0outp = p.din("w0outp", [8, 128, D])
    b0out = p.din("b0out", [128, 8])
    w0r = p.din("w0r", [D, D])
    w0k = p.din("w0k", [D, D])
    w0v = p.din("w0v", [D, D])
    mur = p.din("mur", [128, 8])
    muk = p.din("muk", [128, 8])
    w1qkvf = p.din("w1qkvf", [D, 4 * D])
    w1conv = p.din("w1conv", [FUT, 64, 64])
    w1projp = p.din("w1projp", [8, 128, D])
    w1ff1 = p.din("w1ff1", [D, FF])
    w1ff2 = p.din("w1ff2", [FF, D])
    ln1g = p.din("ln1g", [128, 8])
    ln1b = p.din("ln1b", [128, 8])
    ln2g = p.din("ln2g", [128, 8])
    ln2b = p.din("ln2b", [128, 8])
    selh8 = p.din("selh8", [128, 32])
    sel8d = p.din("sel8d", [8, 2])
    sel2to8 = p.din("sel2to8", [2, 8])
    self8 = p.din("self8", [8, FUT * 128])
    gwt = p.din("gwt", [D, E])
    iotap1 = p.din("iotap1", [16, 128])
    iotac = p.din("iotac", [128, 384])
    iotar = p.din("iotar", [128, 3])
    selpm = p.din("selpm", [128, 16])
    selpmt = p.din("selpmt", [16, 128])
    upwt = p.din("upwt", [E, D, FF])
    gwt2 = p.din("gwt2", [E, D, FF])
    dwnt = p.din("dwnt", [E, FF, D])
    upbf = p.din("upbf", [E, 128, 32])
    gbf = p.din("gbf", [E, 128, 32])
    dbw = p.din("dbw", [E, D])
    shgt = p.din("shgt", [D, SHD])
    shut = p.din("shut", [D, SHD])
    shdt = p.din("shdt", [SHD, D])
    scq = p.dtmp("scq", [D, T])
    sck = p.dtmp("sck", [D, T])
    scf = p.dtmp("scf", [D, T])
    scvt = p.dtmp("scvt", [T, D])
    sckf = p.dtmp("sckf", [FUT, D, T])
    scvf = p.dtmp("scvf", [FUT, D, T])
    sc_x = p.dtmp("sc_x", [128, DCH * T])
    xg_in = p.dtmp("xg_in", [D, T])
    xg_all = p.dtmp("xg_all", [NCORE * D, T])
    scxt = p.dtmp("scxt", [T, D])

    with p.stack:
        psum = PsumPool(p, p.stack)
        BUFA = p.sb(p.stack, "BUFA", [128, DCH * T])
        BUFB = p.sb(p.stack, "BUFB", [128, DCH * T])
        IDENT = p.sb(p.stack, "IDENT", [128, 128])
        ONES = p.sb(p.stack, "ONES", [128, 1])
        ONESR = p.sb(p.stack, "ONESR", [1, 128])
        MKC = p.sb(p.stack, "MKC", [128, 512])
        ONES2 = p.sb(p.stack, "ONES2", [128, 4])
        SEL2T = p.sb(p.stack, "SEL2T", [2, 128])
        BQKV = p.sb(p.stack, "BQKV", [128, 24])
        BOUT = p.sb(p.stack, "BOUT", [128, 8])
        MUR = p.sb(p.stack, "MUR", [128, 8])
        MUK = p.sb(p.stack, "MUK", [128, 8])
        OMR = p.sb(p.stack, "OMR", [128, 8])
        OMK = p.sb(p.stack, "OMK", [128, 8])
        GWT = p.sb(p.stack, "GWT", [128, DCH * E])

        sL = p.sem("ld")
        sW = p.sem("wload")
        sV = p.sem("vec")
        sS = p.sem("sca")
        sT = p.sem("ten")
        sO = p.sem("outd")

        for (dst, src) in [(IDENT, ident.ap()), (ONES, ones.ap()),
                           (ONESR, onesr.ap()), (MKC, maskc.ap()),
                           (ONES2, ones2.ap()), (SEL2T, sel2t.ap()),
                           (BQKV, b0qkv.ap()), (BOUT, b0out.ap()),
                           (MUR, mur.ap()), (MUK, muk.ap())]:
            nc.sync.dma_start(out=dst[:], in_=src).then_inc(sL.h, 16); sL.n += 16
        nc.sync.dma_start(out=GWT[:].rearrange("p (c e) -> p c e", c=DCH),
                          in_=gwt.ap().rearrange("(c p) e -> p c e", p=128)
                          ).then_inc(sL.h, 16); sL.n += 16

        def bcast_row(row_ap, dst_ap, n, nparts=128):
            for c0 in range(0, n, 512):
                w = min(512, n - c0)
                b = psum.acquire()
                nc.tensor.matmul(psum.bank(b)[0:nparts, 0:w],
                                 ONESR[:, 0:nparts], row_ap[:, c0:c0 + w],
                                 start=True, stop=True).then_inc(sT.h, 1)
                sT.n += 1
                nc.vector.wait_ge(sT.h, sT.n)
                cp = nc.vector.tensor_copy(dst_ap[:, c0:c0 + w],
                                           psum.bank(b)[0:nparts, 0:w])
                psum.consumed(b, cp)

        def transpose_fm(src_ap_fn, dst_ap_fn, ktiles, dchunks, consumer="vector"):
            for k in range(ktiles):
                for dch in range(dchunks):
                    b = psum.acquire()
                    nc.tensor.transpose(psum.bank(b)[:, 0:128],
                                        src_ap_fn(k, dch), IDENT[:]
                                        ).then_inc(sT.h, 1)
                    sT.n += 1
                    nc.vector.wait_ge(sT.h, sT.n)
                    cp = nc.vector.tensor_copy(dst_ap_fn(k, dch),
                                               psum.bank(b)[:, 0:128])
                    psum.consumed(b, cp)

        nc.sync.dma_start(
            out=BUFB[:].rearrange("p (k d) -> p k d", k=16),
            in_=x_in.ap().rearrange("(k p) d -> p k d", p=128),
        ).then_inc(sL.h, 16); sL.n += 16
        nc.tensor.wait_ge(sL.h, sL.n)
        transpose_fm(
            lambda k, dch: BUFB[:, k * D + dch * 128: k * D + dch * 128 + 128],
            lambda k, dch: BUFA[:, dch * T + k * 128: dch * T + k * 128 + 128],
            16, DCH)
        p.vmark(sV)

        def emit_layernorm(src, dst, ph, gb=None):
            SQ = p.sb(ph, "SQ", [128, DCH * 512])
            STAT = p.sb(ph, "STAT", [1, 4 * 512])
            BC0 = p.sb(ph, "BC0", [128, 512])
            BC1 = p.sb(ph, "BC1", [128, 512])
            nc.scalar.wait_ge(sV.h, sV.n)
            nc.tensor.wait_ge(sV.h, sV.n)
            for tc in range(4):
                bs = psum.acquire(); bq = psum.acquire()
                nc.scalar.wait_ge(sT.h, sT.n)
                for dch in range(DCH):
                    sl = src[:, dch * T + tc * 512: dch * T + tc * 512 + 512]
                    nc.scalar.activation(out=SQ[:, dch * 512: dch * 512 + 512],
                                         in_=sl, func=AF.Square).then_inc(sS.h, 1)
                    sS.n += 1
                    nc.tensor.matmul(psum.bank(bs)[0:1, :], ONES[:], sl,
                                     start=(dch == 0), stop=(dch == DCH - 1))
                    nc.tensor.wait_ge(sS.h, sS.n)
                    mmq = nc.tensor.matmul(psum.bank(bq)[0:1, :], ONES[:],
                                           SQ[:, dch * 512: dch * 512 + 512],
                                           start=(dch == 0), stop=(dch == DCH - 1))
                    if dch == DCH - 1:
                        mmq.then_inc(sT.h, 1); sT.n += 1
                nc.vector.wait_ge(sT.h, sT.n)
                nc.vector.wait_ge(sV.h, sV.n)
                srow = lambda i: STAT[0:1, i * 512: i * 512 + 512]
                mrow, vrow, trow = srow(2), srow(0), srow(1)
                c0 = nc.vector.tensor_scalar(out=mrow, in0=psum.bank(bs)[0:1, :],
                                             scalar1=1.0 / D, scalar2=None,
                                             op0=OP.mult)
                psum.consumed(bs, c0)
                c1 = nc.vector.tensor_scalar(out=vrow, in0=psum.bank(bq)[0:1, :],
                                             scalar1=1.0 / D, scalar2=None,
                                             op0=OP.mult)
                psum.consumed(bq, c1)
                nc.vector.drain()
                nc.vector.tensor_tensor(trow, mrow, mrow, op=OP.mult)
                nc.vector.drain()
                nc.vector.tensor_tensor(vrow, vrow, trow, op=OP.subtract)
                nc.vector.drain()
                nc.vector.tensor_scalar(out=vrow, in0=vrow, scalar1=EPS,
                                        scalar2=None, op0=OP.add)
                p.vmark(sV)
                nc.scalar.wait_ge(sV.h, sV.n)
                nc.scalar.activation(out=trow, in_=vrow, func=AF.Sqrt)
                p.smark(sS)
                nc.vector.wait_ge(sS.h, sS.n)
                nc.vector.reciprocal(srow(3), trow)
                p.vmark(sV)
                nc.tensor.wait_ge(sV.h, sV.n)
                bcast_row(srow(2), BC0[:], 512)
                bcast_row(srow(3), BC1[:], 512)
                nc.vector.drain()
                for dch in range(DCH):
                    sl = src[:, dch * T + tc * 512: dch * T + tc * 512 + 512]
                    dl = dst[:, dch * T + tc * 512: dch * T + tc * 512 + 512]
                    nc.vector.tensor_tensor(dl, sl, BC0[:], op=OP.subtract)
                    nc.vector.drain()
                    nc.vector.tensor_tensor(dl, dl, BC1[:], op=OP.mult)
                    nc.vector.drain()
                    if gb is not None:
                        g, bb = gb
                        nc.vector.tensor_scalar(out=dl, in0=dl,
                                                scalar1=g[:, dch:dch + 1],
                                                scalar2=bb[:, dch:dch + 1],
                                                op0=OP.mult, op1=OP.add)
                        nc.vector.drain()
                p.vmark(sV)

        if _EMIT_UPTO >= 1:
            with ExitStack() as ph:
                emit_layernorm(BUFA, BUFB, ph)

        if _EMIT_UPTO >= 2:
            with ExitStack() as ph:
                XR = p.sb(ph, "XR", [128, DCH * 512])
                RR = p.sb(ph, "RR", [128, DCH * 512])
                K2 = p.sb(ph, "K2", [128, DCH * 512])
                WT = [p.sb(ph, f"WTf{i}", [128, D]) for i in range(2)]
                TMPc = p.sb(ph, "TMPc", [128, 512])
                nc.vector.wait_ge(sL.h, sL.n)
                for (m, o) in ((MUR, OMR), (MUK, OMK)):
                    nc.vector.tensor_scalar(out=o[:], in0=m[:], scalar1=-1.0,
                                            scalar2=1.0, op0=OP.mult, op1=OP.add)
                p.vmark(sV)

                def mix(dst, mu, om, tc):
                    nc.vector.wait_ge(sT.h, sT.n)
                    for dch in range(DCH):
                        base = dch * T + tc * 512
                        dsl = dst[:, dch * 512: dch * 512 + 512]
                        cur = BUFB[:, base: base + 512]
                        nc.vector.drain()
                        nc.vector.tensor_scalar(out=TMPc[:], in0=cur,
                                                scalar1=mu[:, dch: dch + 1],
                                                scalar2=None, op0=OP.mult)
                        nc.vector.drain()
                        if tc == 0:
                            prev = BUFB[:, base: base + 511]
                            nc.vector.scalar_tensor_tensor(
                                out=dst[:, dch * 512 + 1: dch * 512 + 512],
                                in0=prev, scalar=om[:, dch: dch + 1],
                                in1=TMPc[:, 1:512], op0=OP.mult, op1=OP.add)
                            nc.vector.tensor_copy(dst[:, dch * 512: dch * 512 + 1],
                                                  TMPc[:, 0:1])
                        else:
                            prev = BUFB[:, base - 1: base + 511]
                            nc.vector.scalar_tensor_tensor(
                                out=dsl, in0=prev, scalar=om[:, dch: dch + 1],
                                in1=TMPc[:], op0=OP.mult, op1=OP.add)
                    p.vmark(sV)

                def stream_mm(wdram, rhs_buf, act_out):
                    banks = [psum.acquire() for _ in range(8)]
                    for dch in range(DCH):
                        wbuf = WT[dch % 2]
                        nc.sync.wait_ge(sT.h, sT.n)
                        nc.sync.dma_start(out=wbuf[:],
                                          in_=wdram.ap()[dch * 128: dch * 128 + 128, :]
                                          ).then_inc(sW.h, 16)
                        sW.n += 16
                        nc.tensor.wait_ge(sW.h, sW.n)
                        for j in range(8):
                            nc.tensor.matmul(
                                psum.bank(banks[j])[:, 0:512],
                                wbuf[:, j * 128: j * 128 + 128],
                                rhs_buf[:, dch * 512: dch * 512 + 512],
                                start=(dch == 0), stop=(dch == DCH - 1))
                        p.tmark(sT)
                    for j in range(8):
                        act_out(j, banks[j])

                for tc in range(4):
                    mix(XR, MUR, OMR, tc)
                    nc.tensor.wait_ge(sV.h, sV.n)

                    def act_r(j, b):
                        nc.scalar.wait_ge(sT.h, sT.n)
                        i = nc.scalar.activation(out=RR[:, j * 512: j * 512 + 512],
                                                 in_=psum.bank(b)[:, 0:512],
                                                 func=AF.Sigmoid)
                        psum.consumed(b, i)
                    stream_mm(w0r, XR, act_r)
                    p.smark(sS)

                    mix(XR, MUK, OMK, tc)
                    nc.tensor.wait_ge(sV.h, sV.n)

                    def act_k(j, b):
                        nc.scalar.wait_ge(sT.h, sT.n)
                        sl = K2[:, j * 512: j * 512 + 512]
                        nc.scalar.activation(out=sl, in_=psum.bank(b)[:, 0:512],
                                             func=AF.Relu)
                        nc.scalar.drain()
                        i = nc.scalar.activation(out=sl, in_=sl, func=AF.Square)
                        psum.consumed(b, i)
                    stream_mm(w0k, XR, act_k)
                    p.smark(sS)
                    nc.tensor.wait_ge(sS.h, sS.n)

                    def act_v(j, b):
                        nc.vector.wait_ge(sT.h, sT.n)
                        xsl = BUFA[:, j * T + tc * 512: j * T + tc * 512 + 512]
                        rsl = RR[:, j * 512: j * 512 + 512]
                        nc.vector.drain()
                        nc.vector.tensor_tensor(TMPc[:], psum.bank(b)[:, 0:512],
                                                rsl, op=OP.mult)
                        nc.vector.drain()
                        i = nc.vector.tensor_tensor(xsl, xsl, TMPc[:], op=OP.add)
                        psum.consumed(b, i)
                    stream_mm(w0v, K2, act_v)
                    p.vmark(sV)

        if _EMIT_UPTO >= 3:
            with ExitStack() as ph:
                WQB = p.sb(ph, "WQB", [128, D])
                WOUT = p.sb(ph, "WOUT", [128, D])
                QP = p.sb(ph, "QP", [128, T])
                KP = p.sb(ph, "KP", [128, T])
                VXP = p.sb(ph, "VXP", [128, T])
                PBUF = p.sb(ph, "PBUF", [128, 2 * 8 * 512])
                OUTP = p.sb(ph, "OUTP", [128, T])
                RBC = p.sb(ph, "RBC", [128, 256])
                DROW = p.sb(ph, "DROW", [2, 256])
                sP = p.sem("pairw")

                nc.vector.wait_ge(sL.h, sL.n)
                nc.vector.wait_ge(sV.h, sV.n)
                for dch in range(DCH):
                    nc.vector.drain()
                    xs = BUFA[:, dch * T:(dch + 1) * T]
                    nc.vector.tensor_scalar(out=xs, in0=xs,
                                            scalar1=BOUT[:, dch:dch + 1],
                                            scalar2=None, op0=OP.add)
                p.vmark(sV)

                for pr in range(8):
                    nc.sync.wait_ge(sT.h, sT.n)
                    nc.sync.dma_start(out=WOUT[:], in_=w0outp.ap()[pr]
                                      ).then_inc(sP.h, 16); sP.n += 16
                    for blk, DEST in ((0, QP), (1, KP), (2, VXP)):
                        nc.sync.wait_ge(sT.h, sT.n)
                        nc.sync.dma_start(
                            out=WQB[:].rearrange("p (c j) -> p c j", c=8),
                            in_=w0qkvp.ap()[pr, :, blk * 128:(blk + 1) * 128]
                                .rearrange("(c p) j -> p c j", p=128),
                        ).then_inc(sP.h, 16); sP.n += 16
                        nc.tensor.wait_ge(sP.h, sP.n)
                        nc.tensor.wait_ge(sV.h, sV.n)
                        for tc in range(4):
                            b = psum.acquire()
                            for dch in range(DCH):
                                mm = nc.tensor.matmul(
                                    psum.bank(b)[:, 0:512],
                                    WQB[:, dch * 128:(dch + 1) * 128],
                                    BUFB[:, dch * T + tc * 512: dch * T + tc * 512 + 512],
                                    start=(dch == 0), stop=(dch == DCH - 1))
                            mm.then_inc(sT.h, 1); sT.n += 1
                            nc.scalar.wait_ge(sT.h, sT.n)
                            i = nc.scalar.activation(
                                out=DEST[:, tc * 512: tc * 512 + 512],
                                in_=psum.bank(b)[:, 0:512], func=AF.Identity,
                                bias=BQKV[:, blk * 8 + pr: blk * 8 + pr + 1],
                                scale=1.0)
                            psum.consumed(b, i)
                    p.smark(sS)
                    nc.tensor.wait_ge(sS.h, sS.n)
                    for k in range(16):
                        b = psum.acquire()
                        nc.tensor.transpose(psum.bank(b)[:, 0:128],
                                            VXP[:, k * 128:(k + 1) * 128],
                                            IDENT[:]).then_inc(sT.h, 1)
                        sT.n += 1
                        nc.vector.wait_ge(sT.h, sT.n)
                        cp = nc.vector.tensor_copy(VXP[:, k * 128:(k + 1) * 128],
                                                   psum.bank(b)[:, 0:128])
                        psum.consumed(b, cp)
                    p.vmark(sV)
                    nc.tensor.wait_ge(sV.h, sV.n)

                    for qc in range(8):
                        nst = 2 * (qc + 1)
                        for stp in range(nst // 2):
                            bq = [psum.acquire(), psum.acquire()]
                            for half in range(2):
                                st = stp * 2 + half
                                for hh in range(2):
                                    mm = nc.tensor.matmul(
                                        psum.bank(bq[hh])[:, half * 256:(half + 1) * 256],
                                        KP[64 * hh:64 * hh + 64, st * 128:(st + 1) * 128],
                                        QP[64 * hh:64 * hh + 64, qc * 256:(qc + 1) * 256],
                                        start=True, stop=True,
                                        tile_position=(64 * hh, 0))
                            mm.then_inc(sT.h, 1); sT.n += 1
                            nc.scalar.wait_ge(sT.h, sT.n)
                            for hh in range(2):
                                slot = PBUF[:, (hh * 8 + stp) * 512:
                                            (hh * 8 + stp) * 512 + 512]
                                i = nc.scalar.activation(out=slot,
                                                         in_=psum.bank(bq[hh])[:, 0:512],
                                                         func=AF.Exp, scale=0.125)
                                psum.consumed(bq[hh], i)
                        p.smark(sS)
                        nc.vector.wait_ge(sS.h, sS.n)
                        for hh in range(2):
                            slot = PBUF[:, (hh * 8 + nst // 2 - 1) * 512:
                                        (hh * 8 + nst // 2 - 1) * 512 + 512]
                            nc.vector.drain()
                            nc.vector.tensor_tensor(slot, slot, MKC[:], op=OP.mult)
                        p.vmark(sV)
                        nc.tensor.wait_ge(sV.h, sV.n)
                        bav = [psum.reserve(), psum.reserve()]
                        bden = psum.reserve()
                        for st in range(nst):
                            stp, half = st // 2, st % 2
                            for hh in range(2):
                                ps = PBUF[:, (hh * 8 + stp) * 512 + half * 256:
                                          (hh * 8 + stp) * 512 + half * 256 + 256]
                                mmd = nc.tensor.matmul(
                                    psum.bank(bden)[0:2, 0:256],
                                    ONES2[:, hh * 2:hh * 2 + 2], ps,
                                    start=(st == 0 and hh == 0),
                                    stop=(st == nst - 1 and hh == 1))
                                nc.tensor.matmul(
                                    psum.bank(bav[hh])[64 * hh:64 * hh + 64, 0:256],
                                    VXP[:, st * 128 + 64 * hh: st * 128 + 64 * hh + 64],
                                    ps, start=(st == 0), stop=(st == nst - 1),
                                    tile_position=(0, 64 * hh))
                        p.tmark(sT)
                        nc.vector.wait_ge(sT.h, sT.n)
                        rc = nc.vector.reciprocal(DROW[:], psum.bank(bden)[0:2, 0:256])
                        psum.consumed(bden, rc)
                        p.vmark(sV)
                        nc.tensor.wait_ge(sV.h, sV.n)
                        brb = psum.acquire()
                        nc.tensor.matmul(psum.bank(brb)[:, 0:256], SEL2T[:], DROW[:],
                                         start=True, stop=True).then_inc(sT.h, 1)
                        sT.n += 1
                        nc.vector.wait_ge(sT.h, sT.n)
                        cp = nc.vector.tensor_copy(RBC[:], psum.bank(brb)[:, 0:256])
                        psum.consumed(brb, cp)
                        nc.vector.drain()
                        for hh in range(2):
                            i = nc.vector.tensor_tensor(
                                OUTP[64 * hh:64 * hh + 64, qc * 256:(qc + 1) * 256],
                                psum.bank(bav[hh])[64 * hh:64 * hh + 64, 0:256],
                                RBC[64 * hh:64 * hh + 64, :], op=OP.mult)
                            psum.consumed(bav[hh], i)
                        p.vmark(sV)
                        psum.release(bav[0]); psum.release(bav[1])
                        psum.release(bden)

                    nc.tensor.wait_ge(sV.h, sV.n)
                    for j in range(DCH):
                        for tc in range(4):
                            b = psum.acquire()
                            nc.tensor.matmul(psum.bank(b)[:, 0:512],
                                             WOUT[:, j * 128:(j + 1) * 128],
                                             OUTP[:, tc * 512: tc * 512 + 512],
                                             start=True, stop=True).then_inc(sT.h, 1)
                            sT.n += 1
                            nc.vector.wait_ge(sT.h, sT.n)
                            xsl = BUFA[:, j * T + tc * 512: j * T + tc * 512 + 512]
                            nc.vector.drain()
                            i = nc.vector.tensor_tensor(xsl, xsl,
                                                        psum.bank(b)[:, 0:512],
                                                        op=OP.add)
                            psum.consumed(b, i)
                    p.vmark(sV)

        sX = p.sem("xsp")
        sCC = p.sem("cc")
        LLOG = p.sb(p.stack, "LLOG", [16, T])
        LN1G = p.sb(p.stack, "LN1G", [128, 8])
        LN1B = p.sb(p.stack, "LN1B", [128, 8])
        LN2G = p.sb(p.stack, "LN2G", [128, 8])
        LN2B = p.sb(p.stack, "LN2B", [128, 8])
        SELH8 = p.sb(p.stack, "SELH8", [128, 32])
        SEL8D = p.sb(p.stack, "SEL8D", [8, 2])
        SEL2TO8 = p.sb(p.stack, "SEL2TO8", [2, 8])
        SELF8 = p.sb(p.stack, "SELF8", [8, FUT * 128])
        sL2 = p.sem("ld2")
        for (dst, srcap) in [(LN1G, ln1g.ap()), (LN1B, ln1b.ap()),
                             (LN2G, ln2g.ap()), (LN2B, ln2b.ap()),
                             (SELH8, selh8.ap()), (SEL8D, sel8d.ap()),
                             (SEL2TO8, sel2to8.ap()), (SELF8, self8.ap())]:
            nc.scalar.dma_start(out=dst[:], in_=srcap).then_inc(sL2.h, 16)
            sL2.n += 16
        if _EMIT_UPTO >= 4:
            with ExitStack() as ph:
                emit_layernorm(BUFA, BUFB, ph)
            with ExitStack() as ph:
                nc.tensor.wait_ge(sV.h, sV.n)
                nc.tensor.wait_ge(sL.h, sL.n)
                for tc in range(4):
                    b = psum.acquire()
                    for dch in range(DCH):
                        mm = nc.tensor.matmul(psum.bank(b)[0:16, 0:512],
                                              GWT[:, dch * 16:(dch + 1) * 16],
                                              BUFB[:, dch * T + tc * 512:
                                                   dch * T + tc * 512 + 512],
                                              start=(dch == 0), stop=(dch == DCH - 1))
                    mm.then_inc(sT.h, 1); sT.n += 1
                    nc.vector.wait_ge(sT.h, sT.n)
                    cp = nc.vector.tensor_copy(LLOG[:, tc * 512: tc * 512 + 512],
                                               psum.bank(b)[0:16, 0:512])
                    psum.consumed(b, cp)
                p.vmark(sV)
                nc.sync.wait_ge(sV.h, sV.n)
                nc.sync.dma_start(
                    out=xg_in.ap().rearrange("(c p) t -> p c t", p=128),
                    in_=BUFB[:].rearrange("p (c t) -> p c t", c=DCH),
                ).then_inc(sX.h, 16); sX.n += 16
                nc.sync.dma_start(out=sc_x.ap(), in_=BUFA[:]
                                  ).then_inc(sX.h, 16); sX.n += 16
                nc.gpsimd.wait_ge(sX.h, sX.n)
                nc.gpsimd.collective_compute(
                    "AllGather", OP.bypass,
                    replica_groups=[list(range(NCORE))],
                    ins=[xg_in.ap().opt()], outs=[xg_all.ap().opt()],
                ).then_inc(sCC.h, 1); sCC.n += 1
                XTT = p.sb(ph, "XTT", [128, D])
                sXT = p.sem("xtew")
                for k in range(16):
                    nc.vector.wait_ge(sXT.h, sXT.n)
                    for dch in range(DCH):
                        b = psum.acquire()
                        nc.tensor.transpose(psum.bank(b)[:, 0:128],
                                            BUFB[:, dch * T + k * 128:
                                                 dch * T + k * 128 + 128],
                                            IDENT[:]).then_inc(sT.h, 1)
                        sT.n += 1
                        nc.vector.wait_ge(sT.h, sT.n)
                        cp = nc.vector.tensor_copy(
                            XTT[:, dch * 128:(dch + 1) * 128],
                            psum.bank(b)[:, 0:128])
                        psum.consumed(b, cp)
                    p.vmark(sV)
                    nc.sync.wait_ge(sV.h, sV.n)
                    nc.sync.dma_start(out=scxt.ap()[k * 128:(k + 1) * 128, :],
                                      in_=XTT[:]).then_inc(sXT.h, 16)
                    sXT.n += 16

        if _EMIT_UPTO >= 5:
            with ExitStack() as ph:
                for eng in (nc.vector, nc.scalar, nc.tensor):
                    eng.wait_ge(sX.h, sX.n)
                    eng.wait_ge(sXT.h, sXT.n)
                    eng.wait_ge(sL2.h, sL2.n)
                emit_layernorm(BUFB, BUFA, ph, gb=(LN1G, LN1B))
            with ExitStack() as ph:
                nc.sync.wait_ge(sV.h, sV.n)
                nc.sync.wait_ge(sS.h, sS.n)
                nc.sync.wait_ge(sT.h, sT.n)
                WQT = p.sb(ph, "WQT", [128, D])
                QSB = p.sb(ph, "QSB", [128, 512])
                TSB = p.sb(ph, "TSB", [128, 512])
                sQW = p.sem("qkvw")
                sQD = p.sem("qkvd")
                for jj in range(32):
                    stream = jj // 8; pr = jj % 8
                    nc.sync.wait_ge(sT.h, sT.n)
                    nc.sync.dma_start(
                        out=WQT[:].rearrange("p (c j) -> p c j", c=8),
                        in_=w1qkvf.ap()[:, jj * 128:(jj + 1) * 128]
                            .rearrange("(c p) j -> p c j", p=128),
                    ).then_inc(sQW.h, 16); sQW.n += 16
                    nc.tensor.wait_ge(sQW.h, sQW.n)
                    nc.tensor.wait_ge(sV.h, sV.n)
                    for tc in range(4):
                        b = psum.acquire()
                        for dch in range(DCH):
                            mm = nc.tensor.matmul(
                                psum.bank(b)[:, 0:512], WQT[:, dch * 128:(dch + 1) * 128],
                                BUFA[:, dch * T + tc * 512: dch * T + tc * 512 + 512],
                                start=(dch == 0), stop=(dch == DCH - 1))
                        mm.then_inc(sT.h, 1); sT.n += 1
                        nc.vector.wait_ge(sT.h, sT.n)
                        nc.vector.wait_ge(sQD.h, sQD.n)
                        cp = nc.vector.tensor_copy(QSB[:], psum.bank(b)[:, 0:512])
                        psum.consumed(b, cp)
                        p.vmark(sV)
                        if stream != 2:
                            dstd = (scq, sck, None, scf)[stream]
                            nc.sync.wait_ge(sV.h, sV.n)
                            nc.sync.dma_start(
                                out=dstd.ap()[pr * 128:(pr + 1) * 128,
                                              tc * 512:(tc + 1) * 512],
                                in_=QSB[:]).then_inc(sQD.h, 16); sQD.n += 16
                        else:
                            nc.tensor.wait_ge(sV.h, sV.n)
                            for kk in range(4):
                                b2 = psum.acquire()
                                nc.tensor.transpose(psum.bank(b2)[:, 0:128],
                                                    QSB[:, kk * 128:(kk + 1) * 128],
                                                    IDENT[:]).then_inc(sT.h, 1)
                                sT.n += 1
                                nc.vector.wait_ge(sT.h, sT.n)
                                cp2 = nc.vector.tensor_copy(
                                    TSB[:, kk * 128:(kk + 1) * 128],
                                    psum.bank(b2)[:, 0:128])
                                psum.consumed(b2, cp2)
                            p.vmark(sV)
                            nc.sync.wait_ge(sV.h, sV.n)
                            nc.sync.dma_start(
                                out=scvt.ap()[tc * 512:(tc + 1) * 512,
                                              pr * 128:(pr + 1) * 128]
                                    .rearrange("(k p) d -> p k d", p=128),
                                in_=TSB[:].rearrange("p (k d) -> p k d", k=4),
                            ).then_inc(sQD.h, 16); sQD.n += 16
            with ExitStack() as ph:
                sQW = p.sem("qkvw"); sQD = p.sem("qkvd")
                for eng in (nc.sync, nc.vector, nc.scalar, nc.tensor):
                    eng.wait_ge(sQW.h, sQW.n)
                    eng.wait_ge(sQD.h, sQD.n)
                FSB = p.sb(ph, "FSB", [128, T])
                WCV2 = p.sb(ph, "WCV2", [128, 64])
                WKT = p.sb(ph, "WKT", [128, D])
                KSB = p.sb(ph, "KSB", [128, 512])
                sF = p.sem("fld")
                sKD = p.sem("kfd")
                UPF = BUFA
                for fi in range(FUT):
                    nc.sync.wait_ge(sT.h, sT.n)
                    nc.sync.dma_start(out=WCV2[0:64, :], in_=w1conv.ap()[fi]
                                      ).then_inc(sF.h, 16); sF.n += 16
                    nc.sync.dma_start(out=WCV2[64:128, :], in_=w1conv.ap()[fi]
                                      ).then_inc(sF.h, 16); sF.n += 16
                    for hp in range(8):
                        nc.sync.wait_ge(sT.h, sT.n)
                        nc.sync.dma_start(out=FSB[:],
                                          in_=scf.ap()[hp * 128:(hp + 1) * 128, :]
                                          ).then_inc(sF.h, 16); sF.n += 16
                        nc.tensor.wait_ge(sF.h, sF.n)
                        nc.tensor.wait_ge(sQD.h, sQD.n)
                        for tc in range(4):
                            b0 = psum.acquire(); b1 = psum.acquire()
                            nc.tensor.matmul(psum.bank(b0)[0:64, 0:512],
                                             WCV2[0:64, :],
                                             FSB[0:64, tc * 512: tc * 512 + 512],
                                             start=True, stop=True)
                            mm1 = nc.tensor.matmul(psum.bank(b1)[64:128, 0:512],
                                                   WCV2[64:128, :],
                                                   FSB[64:128, tc * 512: tc * 512 + 512],
                                                   start=True, stop=True,
                                                   tile_position=(64, 64))
                            mm1.then_inc(sT.h, 1); sT.n += 1
                            nc.vector.wait_ge(sT.h, sT.n)
                            dsl0 = UPF[0:64, hp * T + tc * 512: hp * T + tc * 512 + 512]
                            dsl1 = UPF[64:128, hp * T + tc * 512: hp * T + tc * 512 + 512]
                            c0 = nc.vector.tensor_copy(dsl0, psum.bank(b0)[0:64, 0:512])
                            psum.consumed(b0, c0)
                            c1 = nc.vector.tensor_copy(dsl1, psum.bank(b1)[64:128, 0:512])
                            psum.consumed(b1, c1)
                    p.vmark(sV)
                    nc.tensor.wait_ge(sV.h, sV.n)
                    for (col0, dstd) in ((D, sckf), (2 * D, scvf)):
                        for tc in range(4):
                            for jg in range(8):
                                nc.sync.wait_ge(sT.h, sT.n)
                                nc.sync.dma_start(
                                    out=WKT[:].rearrange("p (c j) -> p c j", c=8),
                                    in_=w1qkvf.ap()[:, col0 + jg * 128:
                                                    col0 + (jg + 1) * 128]
                                        .rearrange("(c p) j -> p c j", p=128),
                                ).then_inc(sF.h, 16); sF.n += 16
                                nc.tensor.wait_ge(sF.h, sF.n)
                                b = psum.acquire()
                                for dch in range(DCH):
                                    mm = nc.tensor.matmul(
                                        psum.bank(b)[:, 0:512],
                                        WKT[:, dch * 128:(dch + 1) * 128],
                                        UPF[:, dch * T + tc * 512:
                                            dch * T + tc * 512 + 512],
                                        start=(dch == 0), stop=(dch == DCH - 1))
                                mm.then_inc(sT.h, 1); sT.n += 1
                                nc.vector.wait_ge(sT.h, sT.n)
                                nc.vector.wait_ge(sKD.h, sKD.n)
                                cp = nc.vector.tensor_copy(KSB[:],
                                                           psum.bank(b)[:, 0:512])
                                psum.consumed(b, cp)
                                p.vmark(sV)
                                nc.sync.wait_ge(sV.h, sV.n)
                                nc.sync.dma_start(
                                    out=dstd.ap()[fi, jg * 128:(jg + 1) * 128,
                                                  tc * 512:(tc + 1) * 512],
                                    in_=KSB[:]).then_inc(sKD.h, 16); sKD.n += 16

        if _EMIT_UPTO >= 5:
            with ExitStack() as ph:
                sF = p.sem("fld"); sKD = p.sem("kfd")
                for eng in (nc.sync, nc.vector, nc.scalar, nc.tensor):
                    eng.wait_ge(sF.h, sF.n)
                    eng.wait_ge(sKD.h, sKD.n)
                OQ, OKk, OKV, OPB, OOU = 0, T, 2 * T, 3 * T, 7 * T
                QPs = BUFA[:, OQ:OQ + T]
                KPs = BUFA[:, OKk:OKk + T]
                KVF = BUFA[:, OKV:OKV + T]
                OUTP = BUFA[:, OOU:OOU + T]
                pbslot = lambda hh, stp: BUFA[:, OPB + (hh * 8 + stp) * 512:
                                              OPB + (hh * 8 + stp) * 512 + 512]
                FAW = p.sb(ph, "FAW", [8, T])
                RBCW = p.sb(ph, "RBCW", [128, 512])
                DROW = p.sb(ph, "DROW", [2, 256])
                RB8 = p.sb(ph, "RB8", [8, 256])
                WPJ = p.sb(ph, "WPJ", [128, D])
                sPR = p.sem("prld")
                for pr in range(8):
                    nc.sync.wait_ge(sT.h, sT.n)
                    nc.sync.wait_ge(sV.h, sV.n)
                    nc.sync.dma_start(out=QPs, in_=scq.ap()[pr * 128:(pr + 1) * 128, :]
                                      ).then_inc(sPR.h, 16); sPR.n += 16
                    nc.sync.dma_start(out=KPs, in_=sck.ap()[pr * 128:(pr + 1) * 128, :]
                                      ).then_inc(sPR.h, 16); sPR.n += 16
                    nc.sync.dma_start(out=KVF, in_=sckf.ap()[0, pr * 128:(pr + 1) * 128, :]
                                      ).then_inc(sPR.h, 16); sPR.n += 16
                    bfa = [psum.reserve() for _ in range(4)]
                    for fi in range(FUT):
                        if fi > 0:
                            nc.sync.wait_ge(sV.h, sV.n)
                            nc.sync.dma_start(
                                out=KVF, in_=sckf.ap()[fi, pr * 128:(pr + 1) * 128, :]
                            ).then_inc(sPR.h, 16); sPR.n += 16
                        nc.vector.wait_ge(sPR.h, sPR.n)
                        nc.vector.wait_ge(sT.h, sT.n)
                        nc.vector.drain()
                        nc.vector.tensor_tensor(OUTP, QPs, KVF, op=OP.mult)
                        p.vmark(sV)
                        nc.tensor.wait_ge(sV.h, sV.n)
                        for tc in range(4):
                            nc.tensor.matmul(
                                psum.bank(bfa[tc])[0:8, 0:512],
                                SELH8[:, fi * 8:(fi + 1) * 8],
                                OUTP[:, tc * 512: tc * 512 + 512],
                                start=(fi == 0), stop=(fi == FUT - 1)
                            ).then_inc(sT.h, 1); sT.n += 1
                    nc.scalar.wait_ge(sT.h, sT.n)
                    for tc in range(4):
                        i = nc.scalar.activation(out=FAW[:, tc * 512: tc * 512 + 512],
                                                 in_=psum.bank(bfa[tc])[0:8, 0:512],
                                                 func=AF.Exp)
                        psum.consumed(bfa[tc], i)
                        psum.release(bfa[tc])
                    p.smark(sS)
                    nc.sync.wait_ge(sV.h, sV.n)
                    nc.sync.dma_start(
                        out=KVF.rearrange("p (k d) -> p k d", k=16),
                        in_=scvt.ap()[:, pr * 128:(pr + 1) * 128]
                            .rearrange("(k p) d -> p k d", p=128),
                    ).then_inc(sPR.h, 16); sPR.n += 16
                    nc.tensor.wait_ge(sPR.h, sPR.n)
                    nc.vector.wait_ge(sS.h, sS.n)
                    for qc in range(8):
                        nst = 2 * (qc + 1)
                        for stp in range(nst // 2):
                            bq = [psum.acquire(), psum.acquire()]
                            for half in range(2):
                                st = stp * 2 + half
                                for hh in range(2):
                                    mm = nc.tensor.matmul(
                                        psum.bank(bq[hh])[:, half * 256:(half + 1) * 256],
                                        KPs[64 * hh:64 * hh + 64, st * 128:(st + 1) * 128],
                                        QPs[64 * hh:64 * hh + 64, qc * 256:(qc + 1) * 256],
                                        start=True, stop=True,
                                        tile_position=(64 * hh, 0))
                            mm.then_inc(sT.h, 1); sT.n += 1
                            nc.scalar.wait_ge(sT.h, sT.n)
                            for hh in range(2):
                                i = nc.scalar.activation(out=pbslot(hh, stp),
                                                         in_=psum.bank(bq[hh])[:, 0:512],
                                                         func=AF.Exp, scale=0.125)
                                psum.consumed(bq[hh], i)
                        p.smark(sS)
                        nc.vector.wait_ge(sS.h, sS.n)
                        for hh in range(2):
                            slot = pbslot(hh, nst // 2 - 1)
                            nc.vector.drain()
                            nc.vector.tensor_tensor(slot, slot, MKC[:], op=OP.mult)
                        p.vmark(sV)
                        nc.tensor.wait_ge(sV.h, sV.n)
                        bav = [psum.reserve(), psum.reserve()]
                        bden = psum.reserve()
                        for st in range(nst):
                            stp, half = st // 2, st % 2
                            for hh in range(2):
                                ps = BUFA[:, OPB + (hh * 8 + stp) * 512 + half * 256:
                                          OPB + (hh * 8 + stp) * 512 + half * 256 + 256]
                                nc.tensor.matmul(
                                    psum.bank(bden)[0:2, 0:256],
                                    ONES2[:, hh * 2:hh * 2 + 2], ps,
                                    start=(st == 0 and hh == 0), stop=False)
                                nc.tensor.matmul(
                                    psum.bank(bav[hh])[64 * hh:64 * hh + 64, 0:256],
                                    KVF[:, st * 128 + 64 * hh: st * 128 + 64 * hh + 64],
                                    ps, start=(st == 0), stop=(st == nst - 1),
                                    tile_position=(0, 64 * hh))
                        nc.tensor.matmul(psum.bank(bden)[0:2, 0:256], SEL8D[:],
                                         FAW[:, qc * 256:(qc + 1) * 256],
                                         start=False, stop=True)
                        p.tmark(sT)
                        nc.vector.wait_ge(sT.h, sT.n)
                        rc = nc.vector.reciprocal(DROW[:], psum.bank(bden)[0:2, 0:256])
                        psum.consumed(bden, rc)
                        p.vmark(sV)
                        nc.tensor.wait_ge(sV.h, sV.n)
                        brb = psum.acquire()
                        nc.tensor.matmul(psum.bank(brb)[:, 0:256], SEL2T[:], DROW[:],
                                         start=True, stop=True)
                        br8 = psum.acquire()
                        nc.tensor.matmul(psum.bank(br8)[0:8, 0:256], SEL2TO8[:], DROW[:],
                                         start=True, stop=True).then_inc(sT.h, 1)
                        sT.n += 1
                        nc.vector.wait_ge(sT.h, sT.n)
                        cp = nc.vector.tensor_copy(RBCW[:, 0:256], psum.bank(brb)[:, 0:256])
                        psum.consumed(brb, cp)
                        cp8 = nc.vector.tensor_copy(RB8[:], psum.bank(br8)[0:8, 0:256])
                        psum.consumed(br8, cp8)
                        nc.vector.drain()
                        for hh in range(2):
                            i = nc.vector.tensor_tensor(
                                OUTP[64 * hh:64 * hh + 64, qc * 256:(qc + 1) * 256],
                                psum.bank(bav[hh])[64 * hh:64 * hh + 64, 0:256],
                                RBCW[64 * hh:64 * hh + 64, 0:256], op=OP.mult)
                            psum.consumed(bav[hh], i)
                        nc.vector.drain()
                        fsl = FAW[:, qc * 256:(qc + 1) * 256]
                        nc.vector.tensor_tensor(fsl, fsl, RB8[:], op=OP.mult)
                        p.vmark(sV)
                        psum.release(bav[0]); psum.release(bav[1])
                        psum.release(bden)
                    for fi in range(FUT):
                        nc.sync.wait_ge(sT.h, sT.n)
                        nc.sync.wait_ge(sV.h, sV.n)
                        nc.sync.dma_start(
                            out=KVF, in_=scvf.ap()[fi, pr * 128:(pr + 1) * 128, :]
                        ).then_inc(sPR.h, 16); sPR.n += 16
                        nc.tensor.wait_ge(sPR.h, sPR.n)
                        nc.tensor.wait_ge(sV.h, sV.n)
                        for qc in range(8):
                            bwb = psum.acquire()
                            nc.tensor.matmul(psum.bank(bwb)[:, 0:256],
                                             SELF8[:, fi * 128:(fi + 1) * 128],
                                             FAW[:, qc * 256:(qc + 1) * 256],
                                             start=True, stop=True).then_inc(sT.h, 1)
                            sT.n += 1
                            nc.vector.wait_ge(sT.h, sT.n)
                            cw = nc.vector.tensor_copy(RBCW[:, 256:512],
                                                       psum.bank(bwb)[:, 0:256])
                            psum.consumed(bwb, cw)
                            nc.vector.drain()
                            nc.vector.tensor_tensor(RBCW[:, 0:256], RBCW[:, 256:512],
                                                    KVF[:, qc * 256:(qc + 1) * 256],
                                                    op=OP.mult)
                            nc.vector.drain()
                            osl = OUTP[:, qc * 256:(qc + 1) * 256]
                            nc.vector.tensor_tensor(osl, osl, RBCW[:, 0:256], op=OP.add)
                            nc.vector.drain()
                        p.vmark(sV)
                    nc.sync.wait_ge(sT.h, sT.n)
                    nc.sync.dma_start(out=WPJ[:], in_=w1projp.ap()[pr]
                                      ).then_inc(sPR.h, 16); sPR.n += 16
                    nc.tensor.wait_ge(sPR.h, sPR.n)
                    nc.tensor.wait_ge(sV.h, sV.n)
                    for j in range(DCH):
                        for tc in range(4):
                            b = psum.acquire()
                            nc.tensor.matmul(psum.bank(b)[:, 0:512],
                                             WPJ[:, j * 128:(j + 1) * 128],
                                             OUTP[:, tc * 512: tc * 512 + 512],
                                             start=True, stop=True).then_inc(sT.h, 1)
                            sT.n += 1
                            nc.vector.wait_ge(sT.h, sT.n)
                            asl = BUFB[:, j * T + tc * 512: j * T + tc * 512 + 512]
                            nc.vector.drain()
                            i = nc.vector.tensor_tensor(asl, asl,
                                                        psum.bank(b)[:, 0:512],
                                                        op=OP.add)
                            psum.consumed(b, i)
                    p.vmark(sV)

        if _EMIT_UPTO >= 5:
            with ExitStack() as ph:
                sPR = p.sem("prld")
                for eng in (nc.vector, nc.scalar, nc.tensor, nc.sync):
                    eng.wait_ge(sPR.h, sPR.n)
                emit_layernorm(BUFB, BUFA, ph, gb=(LN2G, LN2B))
            with ExitStack() as ph:
                nc.sync.wait_ge(sV.h, sV.n)
                nc.sync.wait_ge(sS.h, sS.n)
                nc.sync.wait_ge(sT.h, sT.n)
                WFT = p.sb(ph, "WFT", [128, DCH * 1024])
                WF2T = p.sb(ph, "WF2T", [128, D])
                G1 = p.sb(ph, "G1", [128, 8 * 256])
                ACC2 = p.sb(ph, "ACC2", [128, 8 * 256])
                sFF = p.sem("ffw")
                pb2 = [psum.reserve() for _ in range(4)]
                for tc in range(8):
                    for fg in range(4):
                        nc.sync.wait_ge(sT.h, sT.n)
                        nc.sync.dma_start(
                            out=WFT[:].rearrange("p (c j) -> p c j", c=DCH),
                            in_=w1ff1.ap()[:, fg * 1024:(fg + 1) * 1024]
                                .rearrange("(c p) j -> p c j", p=128),
                        ).then_inc(sFF.h, 16); sFF.n += 16
                        nc.tensor.wait_ge(sFF.h, sFF.n)
                        nc.tensor.wait_ge(sV.h, sV.n)
                        for f8 in range(8):
                            b = psum.acquire()
                            for dch in range(DCH):
                                mm = nc.tensor.matmul(
                                    psum.bank(b)[:, 0:256],
                                    WFT[:, dch * 1024 + f8 * 128:
                                        dch * 1024 + (f8 + 1) * 128],
                                    BUFA[:, dch * T + tc * 256: dch * T + tc * 256 + 256],
                                    start=(dch == 0), stop=(dch == DCH - 1))
                            mm.then_inc(sT.h, 1); sT.n += 1
                            nc.scalar.wait_ge(sT.h, sT.n)
                            i = nc.scalar.activation(out=G1[:, f8 * 256:(f8 + 1) * 256],
                                                     in_=psum.bank(b)[:, 0:256],
                                                     func=AF.Gelu)
                            psum.consumed(b, i)
                        p.smark(sS)
                        nc.tensor.wait_ge(sS.h, sS.n)
                        for jhalf in range(2):
                            for f8 in range(8):
                                nc.sync.wait_ge(sT.h, sT.n)
                                nc.sync.dma_start(
                                    out=WF2T[:],
                                    in_=w1ff2.ap()[fg * 1024 + f8 * 128:
                                                   fg * 1024 + (f8 + 1) * 128, :],
                                ).then_inc(sFF.h, 16); sFF.n += 16
                                nc.tensor.wait_ge(sFF.h, sFF.n)
                                for j4 in range(4):
                                    mm = nc.tensor.matmul(
                                        psum.bank(pb2[j4])[:, 0:256],
                                        WF2T[:, (jhalf * 4 + j4) * 128:
                                             (jhalf * 4 + j4 + 1) * 128],
                                        G1[:, f8 * 256:(f8 + 1) * 256],
                                        start=(f8 == 0), stop=(f8 == 7))
                                mm.then_inc(sT.h, 1); sT.n += 1
                            nc.vector.wait_ge(sT.h, sT.n)
                            for j4 in range(4):
                                j8 = jhalf * 4 + j4
                                asl = ACC2[:, j8 * 256:(j8 + 1) * 256]
                                nc.vector.drain()
                                if fg == 0:
                                    i = nc.vector.tensor_copy(asl,
                                                              psum.bank(pb2[j4])[:, 0:256])
                                else:
                                    i = nc.vector.tensor_tensor(
                                        asl, asl, psum.bank(pb2[j4])[:, 0:256], op=OP.add)
                                psum.consumed(pb2[j4], i)
                            p.vmark(sV)
                            nc.tensor.wait_ge(sV.h, sV.n)
                    for j8 in range(8):
                        asl = BUFB[:, j8 * T + tc * 256: j8 * T + tc * 256 + 256]
                        nc.vector.drain()
                        nc.vector.tensor_tensor(asl, asl, ACC2[:, j8 * 256:(j8 + 1) * 256],
                                                op=OP.add)
                    p.vmark(sV)
                for b in pb2:
                    psum.release(b)

        if _EMIT_UPTO >= 6:
            with ExitStack() as ph:
                sFF = p.sem("ffw")
                for eng in (nc.sync, nc.vector, nc.scalar, nc.tensor):
                    eng.wait_ge(sFF.h, sFF.n)
                nc.sync.wait_ge(sCC.h, sCC.n)
                sSC = p.sem("scat")
                LO = p.sb(ph, "LO", [128, 6])
                THROW = p.sb(ph, "THROW", [1, 128])
                ph6 = ExitStack()
                XGT = p.sb(ph6, "XGT", [128, T])
                GLT = p.sb(ph6, "GLT", [128, T])
                GROW = p.sb(ph6, "GROW", [16, T])
                SELPM = p.sb(ph6, "SELPM", [128, 16])
                SELPMT = p.sb(ph6, "SELPMT", [16, 128])
                sGD = p.sem("gated")
                sGL = p.sem("glw")
                nc.sync.dma_start(out=SELPM[:], in_=selpm.ap()
                                  ).then_inc(sGD.h, 16); sGD.n += 16
                nc.sync.dma_start(out=SELPMT[:], in_=selpmt.ap()
                                  ).then_inc(sGD.h, 16); sGD.n += 16
                for r in range(NCORE):
                    btc = [psum.reserve() for _ in range(4)]
                    for dch in range(DCH):
                        nc.sync.wait_ge(sT.h, sT.n)
                        nc.sync.dma_start(
                            out=XGT[:],
                            in_=xg_all.ap()[r * D + dch * 128:
                                            r * D + (dch + 1) * 128, :],
                        ).then_inc(sGD.h, 16); sGD.n += 16
                        nc.tensor.wait_ge(sGD.h, sGD.n)
                        for tc in range(4):
                            mm = nc.tensor.matmul(
                                psum.bank(btc[tc])[0:16, 0:512],
                                GWT[:, dch * 16:(dch + 1) * 16],
                                XGT[:, tc * 512: tc * 512 + 512],
                                start=(dch == 0), stop=(dch == DCH - 1))
                        mm.then_inc(sT.h, 1); sT.n += 1
                    nc.vector.wait_ge(sT.h, sT.n)
                    nc.vector.wait_ge(sGL.h, sGL.n)
                    for tc in range(4):
                        cp = nc.vector.tensor_copy(GROW[:, tc * 512: tc * 512 + 512],
                                                   psum.bank(btc[tc])[0:16, 0:512])
                        psum.consumed(btc[tc], cp)
                        psum.release(btc[tc])
                    p.vmark(sV)
                    nc.sync.wait_ge(sV.h, sV.n)
                    nc.sync.dma_start(out=GLT[r * 16:(r + 1) * 16, :], in_=GROW[:]
                                      ).then_inc(sGL.h, 16); sGL.n += 16
                CROW = p.sb(ph6, "CROW", [1, 16])
                CCOL = p.sb(ph6, "CCOL", [16, 1])
                CNTB = p.sb(ph6, "CNTB", [128, 1])
                MSK = p.sb(ph6, "MSK", [128, T])
                nc.vector.wait_ge(sGL.h, sGL.n)
                nc.tensor.wait_ge(sGD.h, sGD.n)
                nc.vector.memset(LO[:, 0:1], -64.0)
                nc.vector.memset(LO[:, 1:2], 64.0)
                p.vmark(sV)
                for it in range(52):
                    nc.vector.drain()
                    nc.vector.tensor_tensor(LO[:, 2:3], LO[:, 0:1], LO[:, 1:2],
                                            op=OP.add)
                    nc.vector.drain()
                    nc.vector.tensor_scalar(out=LO[:, 2:3], in0=LO[:, 2:3],
                                            scalar1=0.5, scalar2=None, op0=OP.mult)
                    nc.vector.drain()
                    nc.vector.tensor_scalar(out=MSK[:], in0=GLT[:],
                                            scalar1=LO[:, 2:3], scalar2=None,
                                            op0=OP.is_ge)
                    nc.vector.drain()
                    nc.vector.reduce_sum(CNTB[:], MSK[:], axis=mybir.AxisListType.X)
                    p.vmark(sV)
                    nc.tensor.wait_ge(sV.h, sV.n)
                    b1 = psum.acquire()
                    nc.tensor.matmul(psum.bank(b1)[0:1, 0:16], CNTB[:], SELPM[:],
                                     start=True, stop=True).then_inc(sT.h, 1)
                    sT.n += 1
                    nc.vector.wait_ge(sT.h, sT.n)
                    c1 = nc.vector.tensor_copy(CROW[:], psum.bank(b1)[0:1, 0:16])
                    psum.consumed(b1, c1)
                    p.vmark(sV)
                    nc.tensor.wait_ge(sV.h, sV.n)
                    b2 = psum.acquire()
                    nc.tensor.transpose(psum.bank(b2)[0:16, 0:1], CROW[:],
                                        IDENT[0:1, 0:1]).then_inc(sT.h, 1)
                    sT.n += 1
                    nc.vector.wait_ge(sT.h, sT.n)
                    c2 = nc.vector.tensor_copy(CCOL[:], psum.bank(b2)[0:16, 0:1])
                    psum.consumed(b2, c2)
                    p.vmark(sV)
                    nc.tensor.wait_ge(sV.h, sV.n)
                    b3 = psum.acquire()
                    nc.tensor.matmul(psum.bank(b3)[:, 0:1], SELPMT[:], CCOL[:],
                                     start=True, stop=True).then_inc(sT.h, 1)
                    sT.n += 1
                    nc.vector.wait_ge(sT.h, sT.n)
                    c3 = nc.vector.tensor_copy(CNTB[:], psum.bank(b3)[:, 0:1])
                    psum.consumed(b3, c3)
                    nc.vector.drain()
                    nc.vector.tensor_scalar(out=LO[:, 3:4], in0=CNTB[:],
                                            scalar1=2048.0, scalar2=None,
                                            op0=OP.is_ge)
                    nc.vector.drain()
                    nc.vector.tensor_tensor(LO[:, 4:5], LO[:, 2:3], LO[:, 0:1],
                                            op=OP.subtract)
                    nc.vector.tensor_tensor(LO[:, 5:6], LO[:, 1:2], LO[:, 2:3],
                                            op=OP.subtract)
                    nc.vector.drain()
                    nc.vector.scalar_tensor_tensor(out=LO[:, 0:1], in0=LO[:, 4:5],
                                                   scalar=LO[:, 3:4], in1=LO[:, 0:1],
                                                   op0=OP.mult, op1=OP.add)
                    nc.vector.scalar_tensor_tensor(out=LO[:, 1:2], in0=LO[:, 5:6],
                                                   scalar=LO[:, 3:4], in1=LO[:, 2:3],
                                                   op0=OP.mult, op1=OP.add)
                p.vmark(sV)

                nc.tensor.wait_ge(sV.h, sV.n)
                bt0 = psum.acquire()
                nc.tensor.transpose(psum.bank(bt0)[0:1, 0:128], LO[:, 0:1],
                                    IDENT[:]).then_inc(sT.h, 1); sT.n += 1
                nc.vector.wait_ge(sT.h, sT.n)
                ct0 = nc.vector.tensor_copy(THROW[:], psum.bank(bt0)[0:1, 0:128])
                psum.consumed(bt0, ct0)
                p.vmark(sV)
                ph6.close()
                IOTAC = p.sb(ph, "IOTAC", [128, 384])
                IOTAR = p.sb(ph, "IOTAR", [128, 3])
                ROWS = BUFA[0:1, 0:5 * T]
                ZROW = BUFA[0:1, 5 * T:6 * T]
                POSBC = p.sb(ph, "POSBC", [128, T])
                SIGBC = p.sb(ph, "SIGBC", [128, T])
                POSTX = p.sb(ph, "POSTX", [128, 16])
                OHT = p.sb(ph, "OHT", [128, 2 * T])
                OH1 = BUFA[:, 32 * 384 + 8 * 384: 32 * 384 + 8 * 384 + 384]
                xtk = DblW(p, ph, "XTK", D)
                wug = DblW(p, ph, "WUG", D)
                wdw = DblW(p, ph, "WDW", D)
                DBB = BUFA[:, 32 * 384:32 * 384 + 1024]
                DBROW = BUFA[0:1, 32 * 384 + 1024:32 * 384 + 2048]
                UBF = p.sb(ph, "UBF", [128, 32])
                GBF = p.sb(ph, "GBF", [128, 32])
                EO = BUFA[:, 0:32 * 384]
                XIN = BUFA[:, 32 * 384:32 * 384 + 8 * 384]
                DVN = BUFA[:, 0:3 * 1024]
                TME = BUFA[:, 32 * 384 + 8 * 384: 32 * 384 + 8 * 384 + 384]
                sEW = p.sem("expw")
                sGA = p.sem("gath")
                nc.sync.wait_ge(sV.h, sV.n)
                nc.sync.dma_start(out=IOTAC[:], in_=iotac.ap()
                                  ).then_inc(sEW.h, 16); sEW.n += 16
                nc.sync.dma_start(out=IOTAR[:], in_=iotar.ap()
                                  ).then_inc(sEW.h, 16); sEW.n += 16
                nc.vector.wait_ge(sEW.h, sEW.n)
                rR = lambda i: ROWS[0:1, i * T:(i + 1) * T]
                for e in range(E):
                    nc.sync.wait_ge(sT.h, sT.n)
                    nc.sync.wait_ge(sV.h, sV.n)
                    sLL = p.sem("llrow")
                    nc.sync.dma_start(out=rR(4), in_=LLOG[e:e + 1, :]
                                      ).then_inc(sLL.h, 16); sLL.n += 16
                    nc.vector.wait_ge(sT.h, sT.n)
                    nc.vector.wait_ge(sLL.h, sLL.n)
                    nc.vector.memset(ZROW, 0.0)
                    nc.vector.drain()
                    nc.vector.tensor_scalar(out=rR(0), in0=rR(4),
                                            scalar1=THROW[0:1, e:e + 1],
                                            scalar2=None, op0=OP.is_ge)
                    nc.vector.drain()
                    nc.vector.tensor_tensor_scan(rR(1), rR(0), ZROW, 0.0,
                                                 op0=OP.add, op1=OP.add)
                    nc.vector.drain()
                    nc.vector.tensor_tensor(rR(2), rR(1), rR(0), op=OP.mult)
                    p.vmark(sV)
                    nc.scalar.wait_ge(sV.h, sV.n)
                    nc.scalar.activation(out=rR(3), in_=rR(4), func=AF.Sigmoid)
                    p.smark(sS)
                    nc.tensor.wait_ge(sS.h, sS.n)
                    bcast_row(rR(2), POSBC[:], T)
                    bcast_row(rR(3), SIGBC[:], T)
                    for k in range(16):
                        bx = psum.acquire()
                        nc.tensor.transpose(psum.bank(bx)[:, 0:1],
                                            ROWS[0:1, 2 * T + k * 128: 2 * T + (k + 1) * 128],
                                            IDENT[0:1, 0:1]).then_inc(sT.h, 1)
                        sT.n += 1
                        nc.vector.wait_ge(sT.h, sT.n)
                        cx = nc.vector.tensor_copy(POSTX[:, k:k + 1],
                                                   psum.bank(bx)[:, 0:1])
                        psum.consumed(bx, cx)
                    p.vmark(sV)
                    bg = [psum.reserve() for _ in range(8)]
                    xtk.issue(scxt.ap()[0:128, :])
                    for k in range(16):
                        nc.vector.wait_ge(sT.h, sT.n)
                        nc.vector.wait_ge(sV.h, sV.n)
                        nc.vector.drain()
                        nc.vector.tensor_scalar(out=OH1[:], in0=IOTAC[:],
                                                scalar1=POSTX[:, k:k + 1],
                                                scalar2=None, op0=OP.is_equal)
                        p.vmark(sV)
                        XTEK = xtk.consume()
                        nc.tensor.wait_ge(sV.h, sV.n)
                        for dch in range(DCH):
                            mm = nc.tensor.matmul(
                                psum.bank(bg[dch])[:, 0:384],
                                XTEK[:, dch * 128:(dch + 1) * 128], OH1[:],
                                start=(k == 0), stop=(k == 15))
                        mm.then_inc(sT.h, 1); sT.n += 1
                        xtk.done()
                        if k < 15:
                            xtk.issue(scxt.ap()[(k + 1) * 128:(k + 2) * 128, :])
                    nc.vector.wait_ge(sT.h, sT.n)
                    nc.vector.wait_ge(sSC.h, sSC.n)
                    for dch in range(DCH):
                        cx = nc.vector.tensor_copy(
                            XIN[:, dch * 384:(dch + 1) * 384],
                            psum.bank(bg[dch])[:, 0:384])
                        psum.consumed(bg[dch], cx)
                        psum.release(bg[dch])
                    p.vmark(sV)
                    nc.sync.wait_ge(sS.h, sS.n)
                    nc.sync.wait_ge(sV.h, sV.n)
                    sUB = p.sem("ubgb")
                    nc.sync.dma_start(out=UBF[:], in_=upbf.ap()[e]
                                      ).then_inc(sUB.h, 16); sUB.n += 16
                    nc.sync.dma_start(out=GBF[:], in_=gbf.ap()[e]
                                      ).then_inc(sUB.h, 16); sUB.n += 16
                    nc.scalar.wait_ge(sUB.h, sUB.n)
                    bdown = [psum.reserve() for _ in range(6)]
                    wug.issue(upwt.ap()[e, :, 0:128]
                              .rearrange("(c p) j -> p c j", p=128), rearr_c=8)
                    wug.issue(gwt2.ap()[e, :, 0:128]
                              .rearrange("(c p) j -> p c j", p=128), rearr_c=8)
                    wdw.issue(dwnt.ap()[e, 0:128, :])
                    for fc in range(32):
                        WUG = wug.consume()
                        nc.tensor.wait_ge(sV.h, sV.n)
                        bup = psum.acquire()
                        for dch in range(DCH):
                            mm = nc.tensor.matmul(
                                psum.bank(bup)[:, 0:384],
                                WUG[:, dch * 128:(dch + 1) * 128],
                                XIN[:, dch * 384: dch * 384 + 384],
                                start=(dch == 0), stop=(dch == DCH - 1))
                        mm.then_inc(sT.h, 1); sT.n += 1
                        wug.done()
                        if fc < 31:
                            wug.issue(upwt.ap()[e, :, (fc + 1) * 128:(fc + 2) * 128]
                                      .rearrange("(c p) j -> p c j", p=128), rearr_c=8)
                        WUG = wug.consume()
                        bgw = psum.acquire()
                        for dch in range(DCH):
                            mm = nc.tensor.matmul(
                                psum.bank(bgw)[:, 0:384],
                                WUG[:, dch * 128:(dch + 1) * 128],
                                XIN[:, dch * 384: dch * 384 + 384],
                                start=(dch == 0), stop=(dch == DCH - 1))
                        mm.then_inc(sT.h, 1); sT.n += 1
                        wug.done()
                        if fc < 31:
                            wug.issue(gwt2.ap()[e, :, (fc + 1) * 128:(fc + 2) * 128]
                                      .rearrange("(c p) j -> p c j", p=128), rearr_c=8)
                        nc.scalar.wait_ge(sT.h, sT.n)
                        nc.scalar.wait_ge(sSC.h, sSC.n)
                        i1 = nc.scalar.activation(out=TME, in_=psum.bank(bgw)[:, 0:384],
                                                  func=AF.Silu,
                                                  bias=GBF[:, fc:fc + 1], scale=1.0)
                        psum.consumed(bgw, i1)
                        esl = EO[:, fc * 384:(fc + 1) * 384]
                        i2 = nc.scalar.activation(out=esl, in_=psum.bank(bup)[:, 0:384],
                                                  func=AF.Identity,
                                                  bias=UBF[:, fc:fc + 1], scale=1.0)
                        psum.consumed(bup, i2)
                        p.smark(sS)
                        nc.vector.wait_ge(sS.h, sS.n)
                        nc.vector.drain()
                        nc.vector.tensor_tensor(esl, esl, TME, op=OP.mult)
                        p.vmark(sV)
                        DWN = wdw.consume()
                        nc.tensor.wait_ge(sV.h, sV.n)
                        for tt in range(3):
                            for dh in range(2):
                                mm = nc.tensor.matmul(
                                    psum.bank(bdown[tt * 2 + dh])[:, 0:512],
                                    EO[:, fc * 384 + tt * 128: fc * 384 + (tt + 1) * 128],
                                    DWN[:, dh * 512:(dh + 1) * 512],
                                    start=(fc == 0), stop=(fc == 31))
                        mm.then_inc(sT.h, 1); sT.n += 1
                        wdw.done()
                        if fc < 31:
                            wdw.issue(dwnt.ap()[e, (fc + 1) * 128:(fc + 2) * 128, :])
                    nc.sync.wait_ge(sT.h, sT.n)
                    sDB = p.sem("dbrow")
                    nc.sync.dma_start(out=DBROW, in_=dbw.ap()[e:e + 1, :]
                                      ).then_inc(sDB.h, 16); sDB.n += 16
                    nc.tensor.wait_ge(sDB.h, sDB.n)
                    for half in range(2):
                        bd = psum.acquire()
                        nc.tensor.matmul(psum.bank(bd)[:, 0:512], ONESR[:],
                                         DBROW[:, half * 512:(half + 1) * 512],
                                         start=True, stop=True).then_inc(sT.h, 1)
                        sT.n += 1
                        nc.vector.wait_ge(sT.h, sT.n)
                        cd = nc.vector.tensor_copy(DBB[:, half * 512:(half + 1) * 512],
                                                   psum.bank(bd)[:, 0:512])
                        psum.consumed(bd, cd)
                    p.vmark(sV)
                    nc.vector.wait_ge(sT.h, sT.n)
                    for tt in range(3):
                        for dh in range(2):
                            bb = bdown[tt * 2 + dh]
                            dsl = DVN[:, tt * 1024 + dh * 512: tt * 1024 + (dh + 1) * 512]
                            nc.vector.drain()
                            i = nc.vector.tensor_tensor(dsl, psum.bank(bb)[:, 0:512],
                                                        DBB[:, dh * 512:(dh + 1) * 512],
                                                        op=OP.add)
                            psum.consumed(bb, i)
                            psum.release(bb)
                    for tts in ((0, 1), (2,)):
                        nc.vector.wait_ge(sT.h, sT.n)
                        for j, tt in enumerate(tts):
                            osl = OHT[:, j * T:(j + 1) * T]
                            nc.vector.drain()
                            nc.vector.tensor_scalar(out=osl, in0=POSBC[:],
                                                    scalar1=IOTAR[:, tt:tt + 1],
                                                    scalar2=None, op0=OP.is_equal)
                            nc.vector.drain()
                            nc.vector.tensor_tensor(osl, osl, SIGBC[:], op=OP.mult)
                        p.vmark(sV)
                        nc.tensor.wait_ge(sV.h, sV.n)
                        for dch in range(DCH):
                            for tc in range(4):
                                bb = psum.acquire()
                                for j, tt in enumerate(tts):
                                    mm = nc.tensor.matmul(
                                        psum.bank(bb)[:, 0:512],
                                        DVN[:, tt * 1024 + dch * 128:
                                            tt * 1024 + dch * 128 + 128],
                                        OHT[:, j * T + tc * 512: j * T + tc * 512 + 512],
                                        start=(j == 0), stop=(j == len(tts) - 1))
                                mm.then_inc(sT.h, 1); sT.n += 1
                                nc.vector.wait_ge(sT.h, sT.n)
                                asl = BUFB[:, dch * T + tc * 512:
                                           dch * T + tc * 512 + 512]
                                nc.vector.drain()
                                i = nc.vector.tensor_tensor(asl, asl,
                                                            psum.bank(bb)[:, 0:512],
                                                            op=OP.add)
                                psum.consumed(bb, i)
                        p.vmark(sV)
                    nc.vector.drain().then_inc(sSC.h, 1); sSC.n += 1

        if _EMIT_UPTO >= 8:
            with ExitStack() as ph:
                sEW = p.sem("expw"); sGA = p.sem("gath"); sSC = p.sem("scat")
                for eng in (nc.sync, nc.vector, nc.scalar, nc.tensor):
                    eng.wait_ge(sEW.h, sEW.n)
                    eng.wait_ge(sGA.h, sGA.n)
                    eng.wait_ge(sSC.h, sSC.n)
                XN2S = p.sb(ph, "XN2S", [128, DCH * 512])
                WSH = p.sb(ph, "WSH", [128, D])
                GS = BUFA[:, 0:16 * 512]
                US = BUFA[:, 16 * 512:32 * 512]
                sSH = p.sem("shw")
                for tc in range(4):
                    nc.sync.wait_ge(sT.h, sT.n)
                    nc.sync.dma_start(
                        out=XN2S[:].rearrange("p (c t) -> p c t", c=DCH),
                        in_=xg_in.ap()[:, tc * 512:(tc + 1) * 512]
                            .rearrange("(c p) t -> p c t", p=128),
                    ).then_inc(sSH.h, 16); sSH.n += 16
                    for (wsrc, dst, fn) in ((shgt, GS, AF.Silu), (shut, US, AF.Identity)):
                        for fc in range(16):
                            nc.sync.wait_ge(sT.h, sT.n)
                            nc.sync.dma_start(
                                out=WSH[:].rearrange("p (c j) -> p c j", c=8),
                                in_=wsrc.ap()[:, fc * 128:(fc + 1) * 128]
                                    .rearrange("(c p) j -> p c j", p=128),
                            ).then_inc(sSH.h, 16); sSH.n += 16
                            nc.tensor.wait_ge(sSH.h, sSH.n)
                            nc.tensor.wait_ge(sV.h, sV.n)
                            bb = psum.acquire()
                            for dch in range(DCH):
                                mm = nc.tensor.matmul(
                                    psum.bank(bb)[:, 0:512],
                                    WSH[:, dch * 128:(dch + 1) * 128],
                                    XN2S[:, dch * 512: dch * 512 + 512],
                                    start=(dch == 0), stop=(dch == DCH - 1))
                            mm.then_inc(sT.h, 1); sT.n += 1
                            nc.scalar.wait_ge(sT.h, sT.n)
                            i = nc.scalar.activation(out=dst[:, fc * 512:(fc + 1) * 512],
                                                     in_=psum.bank(bb)[:, 0:512],
                                                     func=fn)
                            psum.consumed(bb, i)
                    p.smark(sS)
                    nc.vector.wait_ge(sS.h, sS.n)
                    nc.vector.drain()
                    nc.vector.tensor_tensor(GS, GS, US, op=OP.mult)
                    p.vmark(sV)
                    bsh = [psum.reserve() for _ in range(8)]
                    for fc in range(16):
                        nc.sync.wait_ge(sT.h, sT.n)
                        nc.sync.dma_start(
                            out=WSH[:].rearrange("p (c j) -> p c j", c=8),
                            in_=shdt.ap()[fc * 128:(fc + 1) * 128, :]
                                .rearrange("f (c j) -> f c j", c=8),
                        ).then_inc(sSH.h, 16); sSH.n += 16
                        nc.tensor.wait_ge(sSH.h, sSH.n)
                        nc.tensor.wait_ge(sV.h, sV.n)
                        for jg in range(8):
                            mm = nc.tensor.matmul(
                                psum.bank(bsh[jg])[:, 0:512],
                                WSH[:, jg * 128:(jg + 1) * 128],
                                GS[:, fc * 512:(fc + 1) * 512]
                                if False else BUFA[:, fc * 512:(fc + 1) * 512],
                                start=(fc == 0), stop=(fc == 15))
                        mm.then_inc(sT.h, 1); sT.n += 1
                    nc.vector.wait_ge(sT.h, sT.n)
                    for jg in range(8):
                        asl = BUFB[:, jg * T + tc * 512: jg * T + tc * 512 + 512]
                        nc.vector.drain()
                        i = nc.vector.tensor_tensor(asl, asl,
                                                    psum.bank(bsh[jg])[:, 0:512],
                                                    op=OP.add)
                        psum.consumed(bsh[jg], i)
                        psum.release(bsh[jg])
                    p.vmark(sV)

        if _EMIT_UPTO >= 9:
            with ExitStack() as ph:
                sSH = p.sem("shw"); sSC = p.sem("scat")
                for eng in (nc.sync, nc.vector, nc.scalar, nc.tensor):
                    eng.wait_ge(sSH.h, sSH.n)
                    eng.wait_ge(sSC.h, sSC.n)
                nc.sync.wait_ge(sT.h, sT.n)
                nc.sync.wait_ge(sV.h, sV.n)
                nc.sync.wait_ge(sS.h, sS.n)
                sXR = p.sem("xrl")
                nc.sync.dma_start(out=BUFA[:], in_=sc_x.ap()
                                  ).then_inc(sXR.h, 16); sXR.n += 16
                nc.vector.wait_ge(sXR.h, sXR.n)
                for dch in range(DCH):
                    nc.vector.drain()
                    asl = BUFA[:, dch * T:(dch + 1) * T]
                    nc.vector.tensor_tensor(asl, asl, BUFB[:, dch * T:(dch + 1) * T],
                                            op=OP.add)
                p.vmark(sV)
                nc.tensor.wait_ge(sV.h, sV.n)
                OT = p.sb(ph, "OT", [128, D])
                for k in range(16):
                    nc.vector.wait_ge(sO.h, sO.n)
                    for dch in range(DCH):
                        bx = psum.acquire()
                        nc.tensor.transpose(
                            psum.bank(bx)[:, 0:128],
                            BUFA[:, dch * T + k * 128: dch * T + k * 128 + 128],
                            IDENT[:]).then_inc(sT.h, 1)
                        sT.n += 1
                        nc.vector.wait_ge(sT.h, sT.n)
                        cp = nc.vector.tensor_copy(OT[:, dch * 128:(dch + 1) * 128],
                                                   psum.bank(bx)[:, 0:128])
                        psum.consumed(bx, cp)
                    p.vmark(sV)
                    nc.sync.wait_ge(sV.h, sV.n)
                    nc.sync.dma_start(out=out_t.ap()[k * 128:(k + 1) * 128, :],
                                      in_=OT[:]).then_inc(sO.h, 16); sO.n += 16

        if _DEBUG_DUMP is not None:
            name = _DEBUG_DUMP[0]
            src = {"BUFA": BUFA, "BUFB": BUFB}[name]
            dbg = p.dout("dbg", [128, DCH * T])
            for s in (sV, sS, sT):
                nc.sync.wait_ge(s.h, s.n)
            nc.sync.dma_start(out=dbg.ap(), in_=src[:]).then_inc(sO.h, 16); sO.n += 16

        if _EMIT_UPTO < 9:
            with ExitStack() as ph:
                OT = p.sb(ph, "OT", [128, D])
                nc.tensor.wait_ge(sV.h, sV.n)
                nc.tensor.wait_ge(sS.h, sS.n)
                for k in range(16):
                    nc.vector.wait_ge(sO.h, sO.n)
                    for dch in range(DCH):
                        b = psum.acquire()
                        nc.tensor.transpose(
                            psum.bank(b)[:, 0:128],
                            BUFA[:, dch * T + k * 128: dch * T + k * 128 + 128],
                            IDENT[:]).then_inc(sT.h, 1)
                        sT.n += 1
                        nc.vector.wait_ge(sT.h, sT.n)
                        cp = nc.vector.tensor_copy(OT[:, dch * 128: dch * 128 + 128],
                                                   psum.bank(b)[:, 0:128])
                        psum.consumed(b, cp)
                    p.vmark(sV)
                    nc.sync.wait_ge(sV.h, sV.n)
                    nc.sync.dma_start(out=out_t.ap()[k * 128:(k + 1) * 128, :],
                                      in_=OT[:]).then_inc(sO.h, 16); sO.n += 16
        nc.sync.wait_ge(sO.h, sO.n)

    return p


def _host_inputs(inputs):
    f = np.float32
    c = np.ascontiguousarray
    w_in = inputs["L0_attn_w_in"]
    wq, wk, wv = w_in[0:D], w_in[D:2 * D], w_in[2 * D:3 * D]
    w0qkvp = np.empty((8, D, 384), f)
    for pr in range(8):
        r = slice(128 * pr, 128 * pr + 128)
        w0qkvp[pr, :, 0:128] = wq[r].T
        w0qkvp[pr, :, 128:256] = wk[r].T
        w0qkvp[pr, :, 256:384] = wv[r].T
    b_in = inputs["L0_attn_b_in"]
    b0qkv = np.zeros((128, 24), f)
    for pr in range(8):
        b0qkv[:, pr] = b_in[128 * pr:128 * pr + 128]
        b0qkv[:, 8 + pr] = b_in[D + 128 * pr:D + 128 * pr + 128]
        b0qkv[:, 16 + pr] = b_in[2 * D + 128 * pr:2 * D + 128 * pr + 128]
    w_out = inputs["L0_attn_w_out"]
    w0outp = np.empty((8, 128, D), f)
    for pr in range(8):
        w0outp[pr] = w_out.T[128 * pr:128 * pr + 128, :]
    b0out = c(inputs["L0_attn_b_out"].reshape(DCH, 128).T)

    com = {
        "ident": np.eye(128, dtype=f),
        "ones_col": np.ones((128, 1), f),
        "ones_row": np.ones((1, 128), f),
        "mask_c": np.concatenate([
            (np.arange(128)[:, None] <= np.arange(256)[None, :]).astype(f),
            (np.arange(128)[:, None] <= (np.arange(256)[None, :] - 128)).astype(f)],
            axis=1),
        "ones2": np.array([[1, 0, 0, 1]], f).repeat(128, 0),
        "sel2t": np.concatenate([np.ones((2, 64), f) * np.array([[1], [0]], f),
                                 np.ones((2, 64), f) * np.array([[0], [1]], f)],
                                axis=1),
        "w0qkvp": w0qkvp, "b0qkv": b0qkv, "w0outp": w0outp, "b0out": b0out,
        "w0r": c(inputs["L0_ffn_W_r"].T), "w0k": c(inputs["L0_ffn_W_k"].T),
        "w0v": c(inputs["L0_ffn_W_v"].T),
        "mur": c(inputs["L0_ffn_mu_r"].reshape(DCH, 128).T),
        "muk": c(inputs["L0_ffn_mu_k"].reshape(DCH, 128).T),
    }
    com["w1qkvf"] = c(inputs["L1_attn_W_qkvf"].T)
    com["w1conv"] = c(inputs["L1_attn_W_conv"].transpose(2, 0, 1))
    wproj_t = inputs["L1_attn_W_proj"].T
    w1projp = np.empty((8, 128, D), f)
    for pr in range(8):
        w1projp[pr] = wproj_t[128 * pr:128 * pr + 128, :]
    com["w1projp"] = w1projp
    com["w1ff1"] = c(inputs["L1_attn_W_ff1"])
    com["w1ff2"] = c(inputs["L1_attn_W_ff2"])
    for nm, key in (("ln1g", "L1_attn_ln1_g"), ("ln1b", "L1_attn_ln1_b"),
                    ("ln2g", "L1_attn_ln2_g"), ("ln2b", "L1_attn_ln2_b")):
        com[nm] = c(inputs[key].reshape(DCH, 128).T)
    selh8 = np.zeros((128, 32), f)
    for fi in range(4):
        for hh in range(2):
            selh8[64 * hh:64 * hh + 64, fi * 8 + hh * 4 + fi] = 1.0
    com["selh8"] = selh8
    sel8d = np.zeros((8, 2), f)
    sel8d[0:4, 0] = 1.0
    sel8d[4:8, 1] = 1.0
    com["sel8d"] = sel8d
    sel2to8 = np.zeros((2, 8), f)
    sel2to8[0, 0:4] = 1.0
    sel2to8[1, 4:8] = 1.0
    com["sel2to8"] = sel2to8
    self8 = np.zeros((8, 4 * 128), f)
    for fi in range(4):
        self8[fi, fi * 128: fi * 128 + 64] = 1.0
        self8[4 + fi, fi * 128 + 64: fi * 128 + 128] = 1.0
    com["self8"] = self8
    com["gwt"] = c(inputs["L1_ffn_gate_w"].T)
    iot = (np.arange(16)[:, None] * 128 + np.arange(128)[None, :]).astype(f)
    com["iotap1"] = iot + 1.0
    com["iotac"] = np.broadcast_to(np.arange(1, 385, dtype=f)[None, :], (128, 384)).copy()
    com["iotar"] = (np.arange(3)[None, :] * 128 + np.arange(128)[:, None] + 1).astype(f)
    com["selpm"] = (np.arange(128)[:, None] % 16 == np.arange(16)[None, :]).astype(f)
    com["selpmt"] = c(com["selpm"].T)
    com["upwt"] = c(inputs["L1_ffn_up_w"].transpose(0, 2, 1))
    com["gwt2"] = c(inputs["L1_ffn_gw"].transpose(0, 2, 1))
    com["dwnt"] = c(inputs["L1_ffn_down_w"].transpose(0, 2, 1))
    com["upbf"] = c(inputs["L1_ffn_up_b"].reshape(E, 32, 128).transpose(0, 2, 1))
    com["gbf"] = c(inputs["L1_ffn_gb"].reshape(E, 32, 128).transpose(0, 2, 1))
    com["dbw"] = c(inputs["L1_ffn_down_b"])
    com["shgt"] = c(inputs["L1_ffn_sh_gate_w"].T)
    com["shut"] = c(inputs["L1_ffn_sh_up_w"].T)
    com["shdt"] = c(inputs["L1_ffn_sh_down_w"].T)
    x = inputs["L0_attn_x"]
    maps = []
    for core in range(NCORE):
        m = dict(com)
        m["x_in"] = c(x[core])
        maps.append(m)
    return maps


def kernel(**inputs):
    maps = _host_inputs(inputs)
    p = build_program()
    res = run_bass_kernel_spmd(p.nc, maps, core_ids=list(range(NCORE)))
    out = np.stack([res.results[i]["out"] for i in range(NCORE)], axis=0)
    return out.astype(np.float32)


# version 06
# speedup vs baseline: 1.3602x; 1.1010x over previous
import os
import numpy as np
from contextlib import ExitStack
import concourse.bass as bass
import concourse.mybir as mybir
from concourse.bass_utils import run_bass_kernel_spmd
from concourse.alu_op_type import AluOpType as OP

F32 = mybir.dt.float32
I16 = mybir.dt.int16
I32 = mybir.dt.int32
AF = mybir.ActivationFunctionType

D = 1024
T = 2048
NCORE = 8
DCH = D // 128
H = 16
HS = 64
FUT = 4
E = 16
FF = 4096
SHD = 2048
CAP = 384
EPS = 1e-5

_EMIT_UPTO = 99
_DEBUG_DUMP = None


class Sem:
    __slots__ = ("h", "n", "name")

    def __init__(self, stack, nc, name):
        self.h = stack.enter_context(nc.semaphore(name))
        self.n = 0
        self.name = name


class Prog:
    def __init__(self):
        self.nc = bass.Bass("TRN2", target_bir_lowering=False,
                            debug=bool(os.environ.get("BASSDBG")))
        self.stack = ExitStack()
        self.sems = {}
        self.dram = {}

    def sem(self, name):
        if name not in self.sems:
            self.sems[name] = Sem(self.stack, self.nc, name)
        return self.sems[name]

    _names = None

    def sb(self, stack, name, shape, dtype=F32):
        if self._names is None:
            self._names = {}
        n = self._names.get(name, 0)
        self._names[name] = n + 1
        if n:
            name = f"{name}_{n}"
        return stack.enter_context(self.nc.sbuf_tensor(name, list(shape), dtype))

    def din(self, name, shape, dtype=F32):
        t = self.nc.dram_tensor(name, list(shape), dtype, kind="ExternalInput")
        self.dram[name] = t
        return t

    def dout(self, name, shape, dtype=F32):
        t = self.nc.dram_tensor(name, list(shape), dtype, kind="ExternalOutput")
        self.dram[name] = t
        return t

    def dtmp(self, name, shape, dtype=F32):
        t = self.nc.dram_tensor(name, list(shape), dtype)
        self.dram[name] = t
        return t

    def vmark(self, sem):
        self.nc.vector.drain().then_inc(sem.h, 1); sem.n += 1

    def smark(self, sem):
        self.nc.scalar.drain().then_inc(sem.h, 1); sem.n += 1

    def tmark(self, sem):
        self.nc.tensor.drain().then_inc(sem.h, 1); sem.n += 1


class PsumPool:

    def __init__(self, p, stack):
        self.p = p
        self.banks = [stack.enter_context(p.nc.psum_tensor(f"pb{i}", [128, 512], F32))
                      for i in range(8)]
        self.sems = [p.sem(f"pbs{i}") for i in range(8)]
        self.pending = [0] * 8
        self.rr = 0

    def __post(self):
        pass

    def acquire(self, eng="tensor"):
        for _ in range(8):
            i = self.rr
            self.rr = (self.rr + 1) % 8
            if i not in getattr(self, "reserved", ()):
                break
        if self.pending[i] > 0:
            getattr(self.p.nc, eng).wait_ge(self.sems[i].h, self.pending[i])
        return i

    def reserve(self, eng="tensor"):
        if not hasattr(self, "reserved"):
            self.reserved = set()
        i = self.acquire(eng)
        self.reserved.add(i)
        return i

    def release(self, i):
        self.reserved.discard(i)

    def bank(self, i):
        return self.banks[i]

    def consumed(self, i, instr):
        instr.then_inc(self.sems[i].h, 1)
        self.sems[i].n += 1
        self.pending[i] = self.sems[i].n


class DblW:

    def __init__(self, p, ph, name, cols, dtype=F32):
        self.p = p
        self.bufs = [p.sb(ph, name + "0", [128, cols], dtype),
                     p.sb(ph, name + "1", [128, cols], dtype)]
        self.sems = [p.sem(name + "s0"), p.sem(name + "s1")]
        self.freed = [0, 0]
        self.n = 0
        self.c = 0

    def issue(self, in_ap, rearr_c=None):
        nc = self.p.nc
        i = self.n % 2
        sT = self.p.sem("ten")
        nc.sync.wait_ge(sT.h, self.freed[i])
        out = (self.bufs[i][:].rearrange("p (c j) -> p c j", c=rearr_c)
               if rearr_c else self.bufs[i][:])
        nc.sync.dma_start(out=out, in_=in_ap).then_inc(self.sems[i].h, 16)
        self.sems[i].n += 16
        self.n += 1

    def consume(self):
        k = self.c
        self.c += 1
        i = k % 2
        self.p.nc.tensor.wait_ge(self.sems[i].h, (k // 2 + 1) * 16)
        return self.bufs[i]

    def done(self):
        self.freed[(self.c - 1) % 2] = self.p.sem("ten").n


def build_program():
    p = Prog()
    nc = p.nc

    x_in = p.din("x_in", [T, D])
    out_t = p.dout("out", [T, D])
    ident = p.din("ident", [128, 128])
    ones = p.din("ones_col", [128, 1])
    onesr = p.din("ones_row", [1, 128])
    maskc = p.din("mask_c", [128, 512])
    ones2 = p.din("ones2", [128, 4])
    sel2t = p.din("sel2t", [2, 128])
    w0qkvp = p.din("w0qkvp", [8, D, 384])
    b0qkv = p.din("b0qkv", [128, 24])
    w0outp = p.din("w0outp", [8, 128, D])
    b0out = p.din("b0out", [128, 8])
    w0r = p.din("w0r", [D, D])
    w0k = p.din("w0k", [D, D])
    w0v = p.din("w0v", [D, D])
    mur = p.din("mur", [128, 8])
    muk = p.din("muk", [128, 8])
    w1qkvf = p.din("w1qkvf", [D, 4 * D])
    w1conv = p.din("w1conv", [FUT, 64, 64])
    w1projp = p.din("w1projp", [8, 128, D])
    w1ff1 = p.din("w1ff1", [D, FF])
    w1ff2 = p.din("w1ff2", [FF, D])
    ln1g = p.din("ln1g", [128, 8])
    ln1b = p.din("ln1b", [128, 8])
    ln2g = p.din("ln2g", [128, 8])
    ln2b = p.din("ln2b", [128, 8])
    selh8 = p.din("selh8", [128, 32])
    sel8d = p.din("sel8d", [8, 2])
    sel2to8 = p.din("sel2to8", [2, 8])
    self8 = p.din("self8", [8, FUT * 128])
    gwt = p.din("gwt", [D, E])
    iotap1 = p.din("iotap1", [16, 128])
    iotac = p.din("iotac", [128, 384])
    iotar = p.din("iotar", [128, 3])
    selpm = p.din("selpm", [128, 16])
    selpmt = p.din("selpmt", [16, 128])
    upwt = p.din("upwt", [E, D, FF])
    gwt2 = p.din("gwt2", [E, D, FF])
    dwnt = p.din("dwnt", [E, FF, D])
    upbf = p.din("upbf", [E, 128, 32])
    gbf = p.din("gbf", [E, 128, 32])
    dbw = p.din("dbw", [E, D])
    shgt = p.din("shgt", [D, SHD])
    shut = p.din("shut", [D, SHD])
    shdt = p.din("shdt", [SHD, D])
    scq = p.dtmp("scq", [D, T])
    sck = p.dtmp("sck", [D, T])
    scf = p.dtmp("scf", [D, T])
    scvt = p.dtmp("scvt", [T, D])
    sckf = p.dtmp("sckf", [FUT, D, T])
    scvf = p.dtmp("scvf", [FUT, D, T])
    sc_x = p.dtmp("sc_x", [128, DCH * T])
    xg_in = p.dtmp("xg_in", [D, T])
    xg_all = p.dtmp("xg_all", [NCORE * D, T])
    scxt = p.dtmp("scxt", [T, D])

    with p.stack:
        psum = PsumPool(p, p.stack)
        BUFA = p.sb(p.stack, "BUFA", [128, DCH * T])
        BUFB = p.sb(p.stack, "BUFB", [128, DCH * T])
        IDENT = p.sb(p.stack, "IDENT", [128, 128])
        ONES = p.sb(p.stack, "ONES", [128, 1])
        ONESR = p.sb(p.stack, "ONESR", [1, 128])
        MKC = p.sb(p.stack, "MKC", [128, 512])
        ONES2 = p.sb(p.stack, "ONES2", [128, 4])
        SEL2T = p.sb(p.stack, "SEL2T", [2, 128])
        BQKV = p.sb(p.stack, "BQKV", [128, 24])
        BOUT = p.sb(p.stack, "BOUT", [128, 8])
        MUR = p.sb(p.stack, "MUR", [128, 8])
        MUK = p.sb(p.stack, "MUK", [128, 8])
        OMR = p.sb(p.stack, "OMR", [128, 8])
        OMK = p.sb(p.stack, "OMK", [128, 8])
        GWT = p.sb(p.stack, "GWT", [128, DCH * E])

        sL = p.sem("ld")
        sW = p.sem("wload")
        sV = p.sem("vec")
        sS = p.sem("sca")
        sT = p.sem("ten")
        sO = p.sem("outd")

        for (dst, src) in [(IDENT, ident.ap()), (ONES, ones.ap()),
                           (ONESR, onesr.ap()), (MKC, maskc.ap()),
                           (ONES2, ones2.ap()), (SEL2T, sel2t.ap()),
                           (BQKV, b0qkv.ap()), (BOUT, b0out.ap()),
                           (MUR, mur.ap()), (MUK, muk.ap())]:
            nc.sync.dma_start(out=dst[:], in_=src).then_inc(sL.h, 16); sL.n += 16
        nc.sync.dma_start(out=GWT[:].rearrange("p (c e) -> p c e", c=DCH),
                          in_=gwt.ap().rearrange("(c p) e -> p c e", p=128)
                          ).then_inc(sL.h, 16); sL.n += 16

        def bcast_row(row_ap, dst_ap, n, nparts=128):
            for c0 in range(0, n, 512):
                w = min(512, n - c0)
                b = psum.acquire()
                nc.tensor.matmul(psum.bank(b)[0:nparts, 0:w],
                                 ONESR[:, 0:nparts], row_ap[:, c0:c0 + w],
                                 start=True, stop=True).then_inc(sT.h, 1)
                sT.n += 1
                nc.vector.wait_ge(sT.h, sT.n)
                cp = nc.vector.tensor_copy(dst_ap[:, c0:c0 + w],
                                           psum.bank(b)[0:nparts, 0:w])
                psum.consumed(b, cp)

        def transpose_fm(src_ap_fn, dst_ap_fn, ktiles, dchunks, consumer="vector"):
            for k in range(ktiles):
                for dch in range(dchunks):
                    b = psum.acquire()
                    nc.tensor.transpose(psum.bank(b)[:, 0:128],
                                        src_ap_fn(k, dch), IDENT[:]
                                        ).then_inc(sT.h, 1)
                    sT.n += 1
                    nc.vector.wait_ge(sT.h, sT.n)
                    cp = nc.vector.tensor_copy(dst_ap_fn(k, dch),
                                               psum.bank(b)[:, 0:128])
                    psum.consumed(b, cp)

        nc.sync.dma_start(
            out=BUFB[:].rearrange("p (k d) -> p k d", k=16),
            in_=x_in.ap().rearrange("(k p) d -> p k d", p=128),
        ).then_inc(sL.h, 16); sL.n += 16
        nc.tensor.wait_ge(sL.h, sL.n)
        transpose_fm(
            lambda k, dch: BUFB[:, k * D + dch * 128: k * D + dch * 128 + 128],
            lambda k, dch: BUFA[:, dch * T + k * 128: dch * T + k * 128 + 128],
            16, DCH)
        p.vmark(sV)

        def emit_layernorm(src, dst, ph, gb=None):
            SQ = p.sb(ph, "SQ", [128, DCH * 512])
            STAT = p.sb(ph, "STAT", [1, 4 * 512])
            BC0 = p.sb(ph, "BC0", [128, 512])
            BC1 = p.sb(ph, "BC1", [128, 512])
            nc.scalar.wait_ge(sV.h, sV.n)
            nc.tensor.wait_ge(sV.h, sV.n)
            for tc in range(4):
                bs = psum.acquire(); bq = psum.acquire()
                nc.scalar.wait_ge(sT.h, sT.n)
                for dch in range(DCH):
                    sl = src[:, dch * T + tc * 512: dch * T + tc * 512 + 512]
                    nc.scalar.activation(out=SQ[:, dch * 512: dch * 512 + 512],
                                         in_=sl, func=AF.Square).then_inc(sS.h, 1)
                    sS.n += 1
                    nc.tensor.matmul(psum.bank(bs)[0:1, :], ONES[:], sl,
                                     start=(dch == 0), stop=(dch == DCH - 1))
                    nc.tensor.wait_ge(sS.h, sS.n)
                    mmq = nc.tensor.matmul(psum.bank(bq)[0:1, :], ONES[:],
                                           SQ[:, dch * 512: dch * 512 + 512],
                                           start=(dch == 0), stop=(dch == DCH - 1))
                    if dch == DCH - 1:
                        mmq.then_inc(sT.h, 1); sT.n += 1
                nc.vector.wait_ge(sT.h, sT.n)
                nc.vector.wait_ge(sV.h, sV.n)
                srow = lambda i: STAT[0:1, i * 512: i * 512 + 512]
                mrow, vrow, trow = srow(2), srow(0), srow(1)
                c0 = nc.vector.tensor_scalar(out=mrow, in0=psum.bank(bs)[0:1, :],
                                             scalar1=1.0 / D, scalar2=None,
                                             op0=OP.mult)
                psum.consumed(bs, c0)
                c1 = nc.vector.tensor_scalar(out=vrow, in0=psum.bank(bq)[0:1, :],
                                             scalar1=1.0 / D, scalar2=None,
                                             op0=OP.mult)
                psum.consumed(bq, c1)
                nc.vector.drain()
                nc.vector.tensor_tensor(trow, mrow, mrow, op=OP.mult)
                nc.vector.drain()
                nc.vector.tensor_tensor(vrow, vrow, trow, op=OP.subtract)
                nc.vector.drain()
                nc.vector.tensor_scalar(out=vrow, in0=vrow, scalar1=EPS,
                                        scalar2=None, op0=OP.add)
                p.vmark(sV)
                nc.scalar.wait_ge(sV.h, sV.n)
                nc.scalar.activation(out=trow, in_=vrow, func=AF.Sqrt)
                p.smark(sS)
                nc.vector.wait_ge(sS.h, sS.n)
                nc.vector.reciprocal(srow(3), trow)
                p.vmark(sV)
                nc.tensor.wait_ge(sV.h, sV.n)
                bcast_row(srow(2), BC0[:], 512)
                bcast_row(srow(3), BC1[:], 512)
                nc.vector.drain()
                for dch in range(DCH):
                    sl = src[:, dch * T + tc * 512: dch * T + tc * 512 + 512]
                    dl = dst[:, dch * T + tc * 512: dch * T + tc * 512 + 512]
                    nc.vector.tensor_tensor(dl, sl, BC0[:], op=OP.subtract)
                    nc.vector.drain()
                    nc.vector.tensor_tensor(dl, dl, BC1[:], op=OP.mult)
                    nc.vector.drain()
                    if gb is not None:
                        g, bb = gb
                        nc.vector.tensor_scalar(out=dl, in0=dl,
                                                scalar1=g[:, dch:dch + 1],
                                                scalar2=bb[:, dch:dch + 1],
                                                op0=OP.mult, op1=OP.add)
                        nc.vector.drain()
                p.vmark(sV)

        if _EMIT_UPTO >= 1:
            with ExitStack() as ph:
                emit_layernorm(BUFA, BUFB, ph)

        if _EMIT_UPTO >= 2:
            with ExitStack() as ph:
                XR = p.sb(ph, "XR", [128, DCH * 512])
                RR = p.sb(ph, "RR", [128, DCH * 512])
                K2 = p.sb(ph, "K2", [128, DCH * 512])
                wtf = DblW(p, ph, "WTF", D)
                TMPc = p.sb(ph, "TMPc", [128, 512])
                nc.vector.wait_ge(sL.h, sL.n)
                for (m, o) in ((MUR, OMR), (MUK, OMK)):
                    nc.vector.tensor_scalar(out=o[:], in0=m[:], scalar1=-1.0,
                                            scalar2=1.0, op0=OP.mult, op1=OP.add)
                p.vmark(sV)

                def mix(dst, mu, om, tc):
                    nc.vector.wait_ge(sT.h, sT.n)
                    for dch in range(DCH):
                        base = dch * T + tc * 512
                        dsl = dst[:, dch * 512: dch * 512 + 512]
                        cur = BUFB[:, base: base + 512]
                        nc.vector.drain()
                        nc.vector.tensor_scalar(out=TMPc[:], in0=cur,
                                                scalar1=mu[:, dch: dch + 1],
                                                scalar2=None, op0=OP.mult)
                        nc.vector.drain()
                        if tc == 0:
                            prev = BUFB[:, base: base + 511]
                            nc.vector.scalar_tensor_tensor(
                                out=dst[:, dch * 512 + 1: dch * 512 + 512],
                                in0=prev, scalar=om[:, dch: dch + 1],
                                in1=TMPc[:, 1:512], op0=OP.mult, op1=OP.add)
                            nc.vector.tensor_copy(dst[:, dch * 512: dch * 512 + 1],
                                                  TMPc[:, 0:1])
                        else:
                            prev = BUFB[:, base - 1: base + 511]
                            nc.vector.scalar_tensor_tensor(
                                out=dsl, in0=prev, scalar=om[:, dch: dch + 1],
                                in1=TMPc[:], op0=OP.mult, op1=OP.add)
                    p.vmark(sV)

                def stream_mm(wdram, rhs_buf, act_out):
                    banks = [psum.acquire() for _ in range(8)]
                    wtf.issue(wdram.ap()[0:128, :])
                    for dch in range(DCH):
                        wbuf = wtf.consume()
                        for j in range(8):
                            nc.tensor.matmul(
                                psum.bank(banks[j])[:, 0:512],
                                wbuf[:, j * 128: j * 128 + 128],
                                rhs_buf[:, dch * 512: dch * 512 + 512],
                                start=(dch == 0), stop=(dch == DCH - 1))
                        p.tmark(sT)
                        wtf.done()
                        if dch < DCH - 1:
                            wtf.issue(wdram.ap()[(dch + 1) * 128:(dch + 2) * 128, :])
                    for j in range(8):
                        act_out(j, banks[j])

                for tc in range(4):
                    mix(XR, MUR, OMR, tc)
                    nc.tensor.wait_ge(sV.h, sV.n)

                    def act_r(j, b):
                        nc.scalar.wait_ge(sT.h, sT.n)
                        i = nc.scalar.activation(out=RR[:, j * 512: j * 512 + 512],
                                                 in_=psum.bank(b)[:, 0:512],
                                                 func=AF.Sigmoid)
                        psum.consumed(b, i)
                    stream_mm(w0r, XR, act_r)
                    p.smark(sS)

                    mix(XR, MUK, OMK, tc)
                    nc.tensor.wait_ge(sV.h, sV.n)

                    def act_k(j, b):
                        nc.scalar.wait_ge(sT.h, sT.n)
                        sl = K2[:, j * 512: j * 512 + 512]
                        nc.scalar.activation(out=sl, in_=psum.bank(b)[:, 0:512],
                                             func=AF.Relu)
                        nc.scalar.drain()
                        i = nc.scalar.activation(out=sl, in_=sl, func=AF.Square)
                        psum.consumed(b, i)
                    stream_mm(w0k, XR, act_k)
                    p.smark(sS)
                    nc.tensor.wait_ge(sS.h, sS.n)

                    def act_v(j, b):
                        nc.vector.wait_ge(sT.h, sT.n)
                        xsl = BUFA[:, j * T + tc * 512: j * T + tc * 512 + 512]
                        rsl = RR[:, j * 512: j * 512 + 512]
                        nc.vector.drain()
                        nc.vector.tensor_tensor(TMPc[:], psum.bank(b)[:, 0:512],
                                                rsl, op=OP.mult)
                        nc.vector.drain()
                        i = nc.vector.tensor_tensor(xsl, xsl, TMPc[:], op=OP.add)
                        psum.consumed(b, i)
                    stream_mm(w0v, K2, act_v)
                    p.vmark(sV)

        if _EMIT_UPTO >= 3:
            with ExitStack() as ph:
                WQB = p.sb(ph, "WQB", [128, D])
                WOUT = p.sb(ph, "WOUT", [128, D])
                QP = p.sb(ph, "QP", [128, T])
                KP = p.sb(ph, "KP", [128, T])
                VXP = p.sb(ph, "VXP", [128, T])
                PBUF = p.sb(ph, "PBUF", [128, 2 * 8 * 512])
                OUTP = p.sb(ph, "OUTP", [128, T])
                RBC = p.sb(ph, "RBC", [128, 256])
                DROW = p.sb(ph, "DROW", [2, 256])
                sP = p.sem("pairw")

                nc.vector.wait_ge(sL.h, sL.n)
                nc.vector.wait_ge(sV.h, sV.n)
                for dch in range(DCH):
                    nc.vector.drain()
                    xs = BUFA[:, dch * T:(dch + 1) * T]
                    nc.vector.tensor_scalar(out=xs, in0=xs,
                                            scalar1=BOUT[:, dch:dch + 1],
                                            scalar2=None, op0=OP.add)
                p.vmark(sV)

                for pr in range(8):
                    nc.sync.wait_ge(sT.h, sT.n)
                    nc.sync.dma_start(out=WOUT[:], in_=w0outp.ap()[pr]
                                      ).then_inc(sP.h, 16); sP.n += 16
                    for blk, DEST in ((0, QP), (1, KP), (2, VXP)):
                        nc.sync.wait_ge(sT.h, sT.n)
                        nc.sync.dma_start(
                            out=WQB[:].rearrange("p (c j) -> p c j", c=8),
                            in_=w0qkvp.ap()[pr, :, blk * 128:(blk + 1) * 128]
                                .rearrange("(c p) j -> p c j", p=128),
                        ).then_inc(sP.h, 16); sP.n += 16
                        nc.tensor.wait_ge(sP.h, sP.n)
                        nc.tensor.wait_ge(sV.h, sV.n)
                        for tc in range(4):
                            b = psum.acquire()
                            for dch in range(DCH):
                                mm = nc.tensor.matmul(
                                    psum.bank(b)[:, 0:512],
                                    WQB[:, dch * 128:(dch + 1) * 128],
                                    BUFB[:, dch * T + tc * 512: dch * T + tc * 512 + 512],
                                    start=(dch == 0), stop=(dch == DCH - 1))
                            mm.then_inc(sT.h, 1); sT.n += 1
                            nc.scalar.wait_ge(sT.h, sT.n)
                            i = nc.scalar.activation(
                                out=DEST[:, tc * 512: tc * 512 + 512],
                                in_=psum.bank(b)[:, 0:512], func=AF.Identity,
                                bias=BQKV[:, blk * 8 + pr: blk * 8 + pr + 1],
                                scale=1.0)
                            psum.consumed(b, i)
                    p.smark(sS)
                    nc.tensor.wait_ge(sS.h, sS.n)
                    for k in range(16):
                        b = psum.acquire()
                        nc.tensor.transpose(psum.bank(b)[:, 0:128],
                                            VXP[:, k * 128:(k + 1) * 128],
                                            IDENT[:]).then_inc(sT.h, 1)
                        sT.n += 1
                        nc.vector.wait_ge(sT.h, sT.n)
                        cp = nc.vector.tensor_copy(VXP[:, k * 128:(k + 1) * 128],
                                                   psum.bank(b)[:, 0:128])
                        psum.consumed(b, cp)
                    p.vmark(sV)
                    nc.tensor.wait_ge(sV.h, sV.n)

                    for qc in range(8):
                        nst = 2 * (qc + 1)
                        for stp in range(nst // 2):
                            bq = [psum.acquire(), psum.acquire()]
                            for half in range(2):
                                st = stp * 2 + half
                                for hh in range(2):
                                    mm = nc.tensor.matmul(
                                        psum.bank(bq[hh])[:, half * 256:(half + 1) * 256],
                                        KP[64 * hh:64 * hh + 64, st * 128:(st + 1) * 128],
                                        QP[64 * hh:64 * hh + 64, qc * 256:(qc + 1) * 256],
                                        start=True, stop=True,
                                        tile_position=(64 * hh, 0))
                            mm.then_inc(sT.h, 1); sT.n += 1
                            nc.scalar.wait_ge(sT.h, sT.n)
                            for hh in range(2):
                                slot = PBUF[:, (hh * 8 + stp) * 512:
                                            (hh * 8 + stp) * 512 + 512]
                                i = nc.scalar.activation(out=slot,
                                                         in_=psum.bank(bq[hh])[:, 0:512],
                                                         func=AF.Exp, scale=0.125)
                                psum.consumed(bq[hh], i)
                        p.smark(sS)
                        nc.vector.wait_ge(sS.h, sS.n)
                        for hh in range(2):
                            slot = PBUF[:, (hh * 8 + nst // 2 - 1) * 512:
                                        (hh * 8 + nst // 2 - 1) * 512 + 512]
                            nc.vector.drain()
                            nc.vector.tensor_tensor(slot, slot, MKC[:], op=OP.mult)
                        p.vmark(sV)
                        nc.tensor.wait_ge(sV.h, sV.n)
                        bav = [psum.reserve(), psum.reserve()]
                        bden = psum.reserve()
                        for st in range(nst):
                            stp, half = st // 2, st % 2
                            for hh in range(2):
                                ps = PBUF[:, (hh * 8 + stp) * 512 + half * 256:
                                          (hh * 8 + stp) * 512 + half * 256 + 256]
                                mmd = nc.tensor.matmul(
                                    psum.bank(bden)[0:2, 0:256],
                                    ONES2[:, hh * 2:hh * 2 + 2], ps,
                                    start=(st == 0 and hh == 0),
                                    stop=(st == nst - 1 and hh == 1))
                                nc.tensor.matmul(
                                    psum.bank(bav[hh])[64 * hh:64 * hh + 64, 0:256],
                                    VXP[:, st * 128 + 64 * hh: st * 128 + 64 * hh + 64],
                                    ps, start=(st == 0), stop=(st == nst - 1),
                                    tile_position=(0, 64 * hh))
                        p.tmark(sT)
                        nc.vector.wait_ge(sT.h, sT.n)
                        rc = nc.vector.reciprocal(DROW[:], psum.bank(bden)[0:2, 0:256])
                        psum.consumed(bden, rc)
                        p.vmark(sV)
                        nc.tensor.wait_ge(sV.h, sV.n)
                        brb = psum.acquire()
                        nc.tensor.matmul(psum.bank(brb)[:, 0:256], SEL2T[:], DROW[:],
                                         start=True, stop=True).then_inc(sT.h, 1)
                        sT.n += 1
                        nc.vector.wait_ge(sT.h, sT.n)
                        cp = nc.vector.tensor_copy(RBC[:], psum.bank(brb)[:, 0:256])
                        psum.consumed(brb, cp)
                        nc.vector.drain()
                        for hh in range(2):
                            i = nc.vector.tensor_tensor(
                                OUTP[64 * hh:64 * hh + 64, qc * 256:(qc + 1) * 256],
                                psum.bank(bav[hh])[64 * hh:64 * hh + 64, 0:256],
                                RBC[64 * hh:64 * hh + 64, :], op=OP.mult)
                            psum.consumed(bav[hh], i)
                        p.vmark(sV)
                        psum.release(bav[0]); psum.release(bav[1])
                        psum.release(bden)

                    nc.tensor.wait_ge(sV.h, sV.n)
                    for j in range(DCH):
                        for tc in range(4):
                            b = psum.acquire()
                            nc.tensor.matmul(psum.bank(b)[:, 0:512],
                                             WOUT[:, j * 128:(j + 1) * 128],
                                             OUTP[:, tc * 512: tc * 512 + 512],
                                             start=True, stop=True).then_inc(sT.h, 1)
                            sT.n += 1
                            nc.vector.wait_ge(sT.h, sT.n)
                            xsl = BUFA[:, j * T + tc * 512: j * T + tc * 512 + 512]
                            nc.vector.drain()
                            i = nc.vector.tensor_tensor(xsl, xsl,
                                                        psum.bank(b)[:, 0:512],
                                                        op=OP.add)
                            psum.consumed(b, i)
                    p.vmark(sV)

        sX = p.sem("xsp")
        sCC = p.sem("cc")
        LLOG = p.sb(p.stack, "LLOG", [16, T])
        LN1G = p.sb(p.stack, "LN1G", [128, 8])
        LN1B = p.sb(p.stack, "LN1B", [128, 8])
        LN2G = p.sb(p.stack, "LN2G", [128, 8])
        LN2B = p.sb(p.stack, "LN2B", [128, 8])
        SELH8 = p.sb(p.stack, "SELH8", [128, 32])
        SEL8D = p.sb(p.stack, "SEL8D", [8, 2])
        SEL2TO8 = p.sb(p.stack, "SEL2TO8", [2, 8])
        SELF8 = p.sb(p.stack, "SELF8", [8, FUT * 128])
        sL2 = p.sem("ld2")
        for (dst, srcap) in [(LN1G, ln1g.ap()), (LN1B, ln1b.ap()),
                             (LN2G, ln2g.ap()), (LN2B, ln2b.ap()),
                             (SELH8, selh8.ap()), (SEL8D, sel8d.ap()),
                             (SEL2TO8, sel2to8.ap()), (SELF8, self8.ap())]:
            nc.scalar.dma_start(out=dst[:], in_=srcap).then_inc(sL2.h, 16)
            sL2.n += 16
        if _EMIT_UPTO >= 4:
            with ExitStack() as ph:
                emit_layernorm(BUFA, BUFB, ph)
            with ExitStack() as ph:
                nc.tensor.wait_ge(sV.h, sV.n)
                nc.tensor.wait_ge(sL.h, sL.n)
                for tc in range(4):
                    b = psum.acquire()
                    for dch in range(DCH):
                        mm = nc.tensor.matmul(psum.bank(b)[0:16, 0:512],
                                              GWT[:, dch * 16:(dch + 1) * 16],
                                              BUFB[:, dch * T + tc * 512:
                                                   dch * T + tc * 512 + 512],
                                              start=(dch == 0), stop=(dch == DCH - 1))
                    mm.then_inc(sT.h, 1); sT.n += 1
                    nc.vector.wait_ge(sT.h, sT.n)
                    cp = nc.vector.tensor_copy(LLOG[:, tc * 512: tc * 512 + 512],
                                               psum.bank(b)[0:16, 0:512])
                    psum.consumed(b, cp)
                p.vmark(sV)
                nc.sync.wait_ge(sV.h, sV.n)
                nc.sync.dma_start(
                    out=xg_in.ap().rearrange("(c p) t -> p c t", p=128),
                    in_=BUFB[:].rearrange("p (c t) -> p c t", c=DCH),
                ).then_inc(sX.h, 16); sX.n += 16
                nc.sync.dma_start(out=sc_x.ap(), in_=BUFA[:]
                                  ).then_inc(sX.h, 16); sX.n += 16
                nc.gpsimd.wait_ge(sX.h, sX.n)
                nc.gpsimd.collective_compute(
                    "AllGather", OP.bypass,
                    replica_groups=[list(range(NCORE))],
                    ins=[xg_in.ap().opt()], outs=[xg_all.ap().opt()],
                ).then_inc(sCC.h, 1); sCC.n += 1
                XTT = p.sb(ph, "XTT", [128, D])
                sXT = p.sem("xtew")
                for k in range(16):
                    nc.vector.wait_ge(sXT.h, sXT.n)
                    for dch in range(DCH):
                        b = psum.acquire()
                        nc.tensor.transpose(psum.bank(b)[:, 0:128],
                                            BUFB[:, dch * T + k * 128:
                                                 dch * T + k * 128 + 128],
                                            IDENT[:]).then_inc(sT.h, 1)
                        sT.n += 1
                        nc.vector.wait_ge(sT.h, sT.n)
                        cp = nc.vector.tensor_copy(
                            XTT[:, dch * 128:(dch + 1) * 128],
                            psum.bank(b)[:, 0:128])
                        psum.consumed(b, cp)
                    p.vmark(sV)
                    nc.sync.wait_ge(sV.h, sV.n)
                    nc.sync.dma_start(out=scxt.ap()[k * 128:(k + 1) * 128, :],
                                      in_=XTT[:]).then_inc(sXT.h, 16)
                    sXT.n += 16

        if _EMIT_UPTO >= 5:
            with ExitStack() as ph:
                for eng in (nc.vector, nc.scalar, nc.tensor):
                    eng.wait_ge(sX.h, sX.n)
                    eng.wait_ge(sXT.h, sXT.n)
                    eng.wait_ge(sL2.h, sL2.n)
                emit_layernorm(BUFB, BUFA, ph, gb=(LN1G, LN1B))
            with ExitStack() as ph:
                nc.sync.wait_ge(sV.h, sV.n)
                nc.sync.wait_ge(sS.h, sS.n)
                nc.sync.wait_ge(sT.h, sT.n)
                wqt = DblW(p, ph, "WQT", D)
                QSB = p.sb(ph, "QSB", [128, 512])
                TSB = p.sb(ph, "TSB", [128, 512])
                sQW = p.sem("qkvw")
                sQD = p.sem("qkvd")
                wqt.issue(w1qkvf.ap()[:, 0:128]
                          .rearrange("(c p) j -> p c j", p=128), rearr_c=8)
                for jj in range(32):
                    stream = jj // 8; pr = jj % 8
                    WQT = wqt.consume()
                    nc.tensor.wait_ge(sV.h, sV.n)
                    for tc in range(4):
                        b = psum.acquire()
                        for dch in range(DCH):
                            mm = nc.tensor.matmul(
                                psum.bank(b)[:, 0:512], WQT[:, dch * 128:(dch + 1) * 128],
                                BUFA[:, dch * T + tc * 512: dch * T + tc * 512 + 512],
                                start=(dch == 0), stop=(dch == DCH - 1))
                        mm.then_inc(sT.h, 1); sT.n += 1
                        if tc == 3:
                            wqt.done()
                            if jj < 31:
                                wqt.issue(w1qkvf.ap()[:, (jj + 1) * 128:(jj + 2) * 128]
                                          .rearrange("(c p) j -> p c j", p=128),
                                          rearr_c=8)
                        nc.vector.wait_ge(sT.h, sT.n)
                        nc.vector.wait_ge(sQD.h, sQD.n)
                        cp = nc.vector.tensor_copy(QSB[:], psum.bank(b)[:, 0:512])
                        psum.consumed(b, cp)
                        p.vmark(sV)
                        if stream != 2:
                            dstd = (scq, sck, None, scf)[stream]
                            nc.sync.wait_ge(sV.h, sV.n)
                            nc.sync.dma_start(
                                out=dstd.ap()[pr * 128:(pr + 1) * 128,
                                              tc * 512:(tc + 1) * 512],
                                in_=QSB[:]).then_inc(sQD.h, 16); sQD.n += 16
                        else:
                            nc.tensor.wait_ge(sV.h, sV.n)
                            for kk in range(4):
                                b2 = psum.acquire()
                                nc.tensor.transpose(psum.bank(b2)[:, 0:128],
                                                    QSB[:, kk * 128:(kk + 1) * 128],
                                                    IDENT[:]).then_inc(sT.h, 1)
                                sT.n += 1
                                nc.vector.wait_ge(sT.h, sT.n)
                                cp2 = nc.vector.tensor_copy(
                                    TSB[:, kk * 128:(kk + 1) * 128],
                                    psum.bank(b2)[:, 0:128])
                                psum.consumed(b2, cp2)
                            p.vmark(sV)
                            nc.sync.wait_ge(sV.h, sV.n)
                            nc.sync.dma_start(
                                out=scvt.ap()[tc * 512:(tc + 1) * 512,
                                              pr * 128:(pr + 1) * 128]
                                    .rearrange("(k p) d -> p k d", p=128),
                                in_=TSB[:].rearrange("p (k d) -> p k d", k=4),
                            ).then_inc(sQD.h, 16); sQD.n += 16
            with ExitStack() as ph:
                sQW = p.sem("qkvw"); sQD = p.sem("qkvd")
                for eng in (nc.sync, nc.vector, nc.scalar, nc.tensor):
                    eng.wait_ge(sQW.h, sQW.n)
                    eng.wait_ge(sQD.h, sQD.n)
                FSB = p.sb(ph, "FSB", [128, T])
                WCV2 = p.sb(ph, "WCV2", [128, 64])
                wkt = DblW(p, ph, "WKT", D)
                KSB = p.sb(ph, "KSB", [128, 512])
                sF = p.sem("fld")
                sKD = p.sem("kfd")
                UPF = BUFA
                for fi in range(FUT):
                    nc.sync.wait_ge(sT.h, sT.n)
                    nc.sync.dma_start(out=WCV2[0:64, :], in_=w1conv.ap()[fi]
                                      ).then_inc(sF.h, 16); sF.n += 16
                    nc.sync.dma_start(out=WCV2[64:128, :], in_=w1conv.ap()[fi]
                                      ).then_inc(sF.h, 16); sF.n += 16
                    for hp in range(8):
                        nc.sync.wait_ge(sT.h, sT.n)
                        nc.sync.dma_start(out=FSB[:],
                                          in_=scf.ap()[hp * 128:(hp + 1) * 128, :]
                                          ).then_inc(sF.h, 16); sF.n += 16
                        nc.tensor.wait_ge(sF.h, sF.n)
                        nc.tensor.wait_ge(sQD.h, sQD.n)
                        for tc in range(4):
                            b0 = psum.acquire(); b1 = psum.acquire()
                            nc.tensor.matmul(psum.bank(b0)[0:64, 0:512],
                                             WCV2[0:64, :],
                                             FSB[0:64, tc * 512: tc * 512 + 512],
                                             start=True, stop=True)
                            mm1 = nc.tensor.matmul(psum.bank(b1)[64:128, 0:512],
                                                   WCV2[64:128, :],
                                                   FSB[64:128, tc * 512: tc * 512 + 512],
                                                   start=True, stop=True,
                                                   tile_position=(64, 64))
                            mm1.then_inc(sT.h, 1); sT.n += 1
                            nc.vector.wait_ge(sT.h, sT.n)
                            dsl0 = UPF[0:64, hp * T + tc * 512: hp * T + tc * 512 + 512]
                            dsl1 = UPF[64:128, hp * T + tc * 512: hp * T + tc * 512 + 512]
                            c0 = nc.vector.tensor_copy(dsl0, psum.bank(b0)[0:64, 0:512])
                            psum.consumed(b0, c0)
                            c1 = nc.vector.tensor_copy(dsl1, psum.bank(b1)[64:128, 0:512])
                            psum.consumed(b1, c1)
                    p.vmark(sV)
                    nc.tensor.wait_ge(sV.h, sV.n)
                    kvjobs = [(col0, tc, jg) for col0 in (D, 2 * D)
                              for tc in range(4) for jg in range(8)]
                    wkt.issue(w1qkvf.ap()[:, D: D + 128]
                              .rearrange("(c p) j -> p c j", p=128), rearr_c=8)
                    for ji, (col0, tc, jg) in enumerate(kvjobs):
                        if True:
                            if True:
                                WKT = wkt.consume()
                                b = psum.acquire()
                                for dch in range(DCH):
                                    mm = nc.tensor.matmul(
                                        psum.bank(b)[:, 0:512],
                                        WKT[:, dch * 128:(dch + 1) * 128],
                                        UPF[:, dch * T + tc * 512:
                                            dch * T + tc * 512 + 512],
                                        start=(dch == 0), stop=(dch == DCH - 1))
                                mm.then_inc(sT.h, 1); sT.n += 1
                                wkt.done()
                                if ji < len(kvjobs) - 1:
                                    c2, _, j2 = kvjobs[ji + 1]
                                    wkt.issue(w1qkvf.ap()[:, c2 + j2 * 128:
                                                          c2 + (j2 + 1) * 128]
                                              .rearrange("(c p) j -> p c j", p=128),
                                              rearr_c=8)
                                dstd = sckf if col0 == D else scvf
                                nc.vector.wait_ge(sT.h, sT.n)
                                nc.vector.wait_ge(sKD.h, sKD.n)
                                cp = nc.vector.tensor_copy(KSB[:],
                                                           psum.bank(b)[:, 0:512])
                                psum.consumed(b, cp)
                                p.vmark(sV)
                                nc.sync.wait_ge(sV.h, sV.n)
                                nc.sync.dma_start(
                                    out=dstd.ap()[fi, jg * 128:(jg + 1) * 128,
                                                  tc * 512:(tc + 1) * 512],
                                    in_=KSB[:]).then_inc(sKD.h, 16); sKD.n += 16

        if _EMIT_UPTO >= 5:
            with ExitStack() as ph:
                sF = p.sem("fld"); sKD = p.sem("kfd")
                for eng in (nc.sync, nc.vector, nc.scalar, nc.tensor):
                    eng.wait_ge(sF.h, sF.n)
                    eng.wait_ge(sKD.h, sKD.n)
                OQ, OKk, OKV, OPB, OOU = 0, T, 2 * T, 3 * T, 7 * T
                QPs = BUFA[:, OQ:OQ + T]
                KPs = BUFA[:, OKk:OKk + T]
                KVF = BUFA[:, OKV:OKV + T]
                OUTP = BUFA[:, OOU:OOU + T]
                pbslot = lambda hh, stp: BUFA[:, OPB + (hh * 8 + stp) * 512:
                                              OPB + (hh * 8 + stp) * 512 + 512]
                FAW = p.sb(ph, "FAW", [8, T])
                RBCW = p.sb(ph, "RBCW", [128, 512])
                DROW = p.sb(ph, "DROW", [2, 256])
                RB8 = p.sb(ph, "RB8", [8, 256])
                WPJ = p.sb(ph, "WPJ", [128, D])
                sPR = p.sem("prld")
                for pr in range(8):
                    nc.sync.wait_ge(sT.h, sT.n)
                    nc.sync.wait_ge(sV.h, sV.n)
                    nc.sync.dma_start(out=QPs, in_=scq.ap()[pr * 128:(pr + 1) * 128, :]
                                      ).then_inc(sPR.h, 16); sPR.n += 16
                    nc.sync.dma_start(out=KPs, in_=sck.ap()[pr * 128:(pr + 1) * 128, :]
                                      ).then_inc(sPR.h, 16); sPR.n += 16
                    nc.sync.dma_start(out=KVF, in_=sckf.ap()[0, pr * 128:(pr + 1) * 128, :]
                                      ).then_inc(sPR.h, 16); sPR.n += 16
                    bfa = [psum.reserve() for _ in range(4)]
                    for fi in range(FUT):
                        if fi > 0:
                            nc.sync.wait_ge(sV.h, sV.n)
                            nc.sync.dma_start(
                                out=KVF, in_=sckf.ap()[fi, pr * 128:(pr + 1) * 128, :]
                            ).then_inc(sPR.h, 16); sPR.n += 16
                        nc.vector.wait_ge(sPR.h, sPR.n)
                        nc.vector.wait_ge(sT.h, sT.n)
                        nc.vector.drain()
                        nc.vector.tensor_tensor(OUTP, QPs, KVF, op=OP.mult)
                        p.vmark(sV)
                        nc.tensor.wait_ge(sV.h, sV.n)
                        for tc in range(4):
                            nc.tensor.matmul(
                                psum.bank(bfa[tc])[0:8, 0:512],
                                SELH8[:, fi * 8:(fi + 1) * 8],
                                OUTP[:, tc * 512: tc * 512 + 512],
                                start=(fi == 0), stop=(fi == FUT - 1)
                            ).then_inc(sT.h, 1); sT.n += 1
                    nc.scalar.wait_ge(sT.h, sT.n)
                    for tc in range(4):
                        i = nc.scalar.activation(out=FAW[:, tc * 512: tc * 512 + 512],
                                                 in_=psum.bank(bfa[tc])[0:8, 0:512],
                                                 func=AF.Exp)
                        psum.consumed(bfa[tc], i)
                        psum.release(bfa[tc])
                    p.smark(sS)
                    nc.sync.wait_ge(sV.h, sV.n)
                    nc.sync.dma_start(
                        out=KVF.rearrange("p (k d) -> p k d", k=16),
                        in_=scvt.ap()[:, pr * 128:(pr + 1) * 128]
                            .rearrange("(k p) d -> p k d", p=128),
                    ).then_inc(sPR.h, 16); sPR.n += 16
                    nc.tensor.wait_ge(sPR.h, sPR.n)
                    nc.vector.wait_ge(sS.h, sS.n)
                    for qc in range(8):
                        nst = 2 * (qc + 1)
                        for stp in range(nst // 2):
                            bq = [psum.acquire(), psum.acquire()]
                            for half in range(2):
                                st = stp * 2 + half
                                for hh in range(2):
                                    mm = nc.tensor.matmul(
                                        psum.bank(bq[hh])[:, half * 256:(half + 1) * 256],
                                        KPs[64 * hh:64 * hh + 64, st * 128:(st + 1) * 128],
                                        QPs[64 * hh:64 * hh + 64, qc * 256:(qc + 1) * 256],
                                        start=True, stop=True,
                                        tile_position=(64 * hh, 0))
                            mm.then_inc(sT.h, 1); sT.n += 1
                            nc.scalar.wait_ge(sT.h, sT.n)
                            for hh in range(2):
                                i = nc.scalar.activation(out=pbslot(hh, stp),
                                                         in_=psum.bank(bq[hh])[:, 0:512],
                                                         func=AF.Exp, scale=0.125)
                                psum.consumed(bq[hh], i)
                        p.smark(sS)
                        nc.vector.wait_ge(sS.h, sS.n)
                        for hh in range(2):
                            slot = pbslot(hh, nst // 2 - 1)
                            nc.vector.drain()
                            nc.vector.tensor_tensor(slot, slot, MKC[:], op=OP.mult)
                        p.vmark(sV)
                        nc.tensor.wait_ge(sV.h, sV.n)
                        bav = [psum.reserve(), psum.reserve()]
                        bden = psum.reserve()
                        for st in range(nst):
                            stp, half = st // 2, st % 2
                            for hh in range(2):
                                ps = BUFA[:, OPB + (hh * 8 + stp) * 512 + half * 256:
                                          OPB + (hh * 8 + stp) * 512 + half * 256 + 256]
                                nc.tensor.matmul(
                                    psum.bank(bden)[0:2, 0:256],
                                    ONES2[:, hh * 2:hh * 2 + 2], ps,
                                    start=(st == 0 and hh == 0), stop=False)
                                nc.tensor.matmul(
                                    psum.bank(bav[hh])[64 * hh:64 * hh + 64, 0:256],
                                    KVF[:, st * 128 + 64 * hh: st * 128 + 64 * hh + 64],
                                    ps, start=(st == 0), stop=(st == nst - 1),
                                    tile_position=(0, 64 * hh))
                        nc.tensor.matmul(psum.bank(bden)[0:2, 0:256], SEL8D[:],
                                         FAW[:, qc * 256:(qc + 1) * 256],
                                         start=False, stop=True)
                        p.tmark(sT)
                        nc.vector.wait_ge(sT.h, sT.n)
                        rc = nc.vector.reciprocal(DROW[:], psum.bank(bden)[0:2, 0:256])
                        psum.consumed(bden, rc)
                        p.vmark(sV)
                        nc.tensor.wait_ge(sV.h, sV.n)
                        brb = psum.acquire()
                        nc.tensor.matmul(psum.bank(brb)[:, 0:256], SEL2T[:], DROW[:],
                                         start=True, stop=True)
                        br8 = psum.acquire()
                        nc.tensor.matmul(psum.bank(br8)[0:8, 0:256], SEL2TO8[:], DROW[:],
                                         start=True, stop=True).then_inc(sT.h, 1)
                        sT.n += 1
                        nc.vector.wait_ge(sT.h, sT.n)
                        cp = nc.vector.tensor_copy(RBCW[:, 0:256], psum.bank(brb)[:, 0:256])
                        psum.consumed(brb, cp)
                        cp8 = nc.vector.tensor_copy(RB8[:], psum.bank(br8)[0:8, 0:256])
                        psum.consumed(br8, cp8)
                        nc.vector.drain()
                        for hh in range(2):
                            i = nc.vector.tensor_tensor(
                                OUTP[64 * hh:64 * hh + 64, qc * 256:(qc + 1) * 256],
                                psum.bank(bav[hh])[64 * hh:64 * hh + 64, 0:256],
                                RBCW[64 * hh:64 * hh + 64, 0:256], op=OP.mult)
                            psum.consumed(bav[hh], i)
                        nc.vector.drain()
                        fsl = FAW[:, qc * 256:(qc + 1) * 256]
                        nc.vector.tensor_tensor(fsl, fsl, RB8[:], op=OP.mult)
                        p.vmark(sV)
                        psum.release(bav[0]); psum.release(bav[1])
                        psum.release(bden)
                    for fi in range(FUT):
                        nc.sync.wait_ge(sT.h, sT.n)
                        nc.sync.wait_ge(sV.h, sV.n)
                        nc.sync.dma_start(
                            out=KVF, in_=scvf.ap()[fi, pr * 128:(pr + 1) * 128, :]
                        ).then_inc(sPR.h, 16); sPR.n += 16
                        nc.tensor.wait_ge(sPR.h, sPR.n)
                        nc.tensor.wait_ge(sV.h, sV.n)
                        for qc in range(8):
                            bwb = psum.acquire()
                            nc.tensor.matmul(psum.bank(bwb)[:, 0:256],
                                             SELF8[:, fi * 128:(fi + 1) * 128],
                                             FAW[:, qc * 256:(qc + 1) * 256],
                                             start=True, stop=True).then_inc(sT.h, 1)
                            sT.n += 1
                            nc.vector.wait_ge(sT.h, sT.n)
                            cw = nc.vector.tensor_copy(RBCW[:, 256:512],
                                                       psum.bank(bwb)[:, 0:256])
                            psum.consumed(bwb, cw)
                            nc.vector.drain()
                            nc.vector.tensor_tensor(RBCW[:, 0:256], RBCW[:, 256:512],
                                                    KVF[:, qc * 256:(qc + 1) * 256],
                                                    op=OP.mult)
                            nc.vector.drain()
                            osl = OUTP[:, qc * 256:(qc + 1) * 256]
                            nc.vector.tensor_tensor(osl, osl, RBCW[:, 0:256], op=OP.add)
                            nc.vector.drain()
                        p.vmark(sV)
                    nc.sync.wait_ge(sT.h, sT.n)
                    nc.sync.dma_start(out=WPJ[:], in_=w1projp.ap()[pr]
                                      ).then_inc(sPR.h, 16); sPR.n += 16
                    nc.tensor.wait_ge(sPR.h, sPR.n)
                    nc.tensor.wait_ge(sV.h, sV.n)
                    for j in range(DCH):
                        for tc in range(4):
                            b = psum.acquire()
                            nc.tensor.matmul(psum.bank(b)[:, 0:512],
                                             WPJ[:, j * 128:(j + 1) * 128],
                                             OUTP[:, tc * 512: tc * 512 + 512],
                                             start=True, stop=True).then_inc(sT.h, 1)
                            sT.n += 1
                            nc.vector.wait_ge(sT.h, sT.n)
                            asl = BUFB[:, j * T + tc * 512: j * T + tc * 512 + 512]
                            nc.vector.drain()
                            i = nc.vector.tensor_tensor(asl, asl,
                                                        psum.bank(b)[:, 0:512],
                                                        op=OP.add)
                            psum.consumed(b, i)
                    p.vmark(sV)

        if _EMIT_UPTO >= 5:
            with ExitStack() as ph:
                sPR = p.sem("prld")
                for eng in (nc.vector, nc.scalar, nc.tensor, nc.sync):
                    eng.wait_ge(sPR.h, sPR.n)
                emit_layernorm(BUFB, BUFA, ph, gb=(LN2G, LN2B))
            with ExitStack() as ph:
                nc.sync.wait_ge(sV.h, sV.n)
                nc.sync.wait_ge(sS.h, sS.n)
                nc.sync.wait_ge(sT.h, sT.n)
                WFT = p.sb(ph, "WFT", [128, DCH * 1024])
                WF2T = p.sb(ph, "WF2T", [128, D])
                G1 = p.sb(ph, "G1", [128, 8 * 256])
                ACC2 = p.sb(ph, "ACC2", [128, 8 * 256])
                sFF = p.sem("ffw")
                pb2 = [psum.reserve() for _ in range(4)]
                for tc in range(8):
                    for fg in range(4):
                        nc.sync.wait_ge(sT.h, sT.n)
                        nc.sync.dma_start(
                            out=WFT[:].rearrange("p (c j) -> p c j", c=DCH),
                            in_=w1ff1.ap()[:, fg * 1024:(fg + 1) * 1024]
                                .rearrange("(c p) j -> p c j", p=128),
                        ).then_inc(sFF.h, 16); sFF.n += 16
                        nc.tensor.wait_ge(sFF.h, sFF.n)
                        nc.tensor.wait_ge(sV.h, sV.n)
                        for f8 in range(8):
                            b = psum.acquire()
                            for dch in range(DCH):
                                mm = nc.tensor.matmul(
                                    psum.bank(b)[:, 0:256],
                                    WFT[:, dch * 1024 + f8 * 128:
                                        dch * 1024 + (f8 + 1) * 128],
                                    BUFA[:, dch * T + tc * 256: dch * T + tc * 256 + 256],
                                    start=(dch == 0), stop=(dch == DCH - 1))
                            mm.then_inc(sT.h, 1); sT.n += 1
                            nc.scalar.wait_ge(sT.h, sT.n)
                            i = nc.scalar.activation(out=G1[:, f8 * 256:(f8 + 1) * 256],
                                                     in_=psum.bank(b)[:, 0:256],
                                                     func=AF.Gelu)
                            psum.consumed(b, i)
                        p.smark(sS)
                        nc.tensor.wait_ge(sS.h, sS.n)
                        for jhalf in range(2):
                            for f8 in range(8):
                                nc.sync.wait_ge(sT.h, sT.n)
                                nc.sync.dma_start(
                                    out=WF2T[:],
                                    in_=w1ff2.ap()[fg * 1024 + f8 * 128:
                                                   fg * 1024 + (f8 + 1) * 128, :],
                                ).then_inc(sFF.h, 16); sFF.n += 16
                                nc.tensor.wait_ge(sFF.h, sFF.n)
                                for j4 in range(4):
                                    mm = nc.tensor.matmul(
                                        psum.bank(pb2[j4])[:, 0:256],
                                        WF2T[:, (jhalf * 4 + j4) * 128:
                                             (jhalf * 4 + j4 + 1) * 128],
                                        G1[:, f8 * 256:(f8 + 1) * 256],
                                        start=(f8 == 0), stop=(f8 == 7))
                                mm.then_inc(sT.h, 1); sT.n += 1
                            nc.vector.wait_ge(sT.h, sT.n)
                            for j4 in range(4):
                                j8 = jhalf * 4 + j4
                                asl = ACC2[:, j8 * 256:(j8 + 1) * 256]
                                nc.vector.drain()
                                if fg == 0:
                                    i = nc.vector.tensor_copy(asl,
                                                              psum.bank(pb2[j4])[:, 0:256])
                                else:
                                    i = nc.vector.tensor_tensor(
                                        asl, asl, psum.bank(pb2[j4])[:, 0:256], op=OP.add)
                                psum.consumed(pb2[j4], i)
                            p.vmark(sV)
                            nc.tensor.wait_ge(sV.h, sV.n)
                    for j8 in range(8):
                        asl = BUFB[:, j8 * T + tc * 256: j8 * T + tc * 256 + 256]
                        nc.vector.drain()
                        nc.vector.tensor_tensor(asl, asl, ACC2[:, j8 * 256:(j8 + 1) * 256],
                                                op=OP.add)
                    p.vmark(sV)
                for b in pb2:
                    psum.release(b)

        if _EMIT_UPTO >= 6:
            with ExitStack() as ph:
                sFF = p.sem("ffw")
                for eng in (nc.sync, nc.vector, nc.scalar, nc.tensor):
                    eng.wait_ge(sFF.h, sFF.n)
                nc.sync.wait_ge(sCC.h, sCC.n)
                sSC = p.sem("scat")
                LO = p.sb(ph, "LO", [128, 6])
                THROW = p.sb(ph, "THROW", [1, 128])
                ph6 = ExitStack()
                XGT = p.sb(ph6, "XGT", [128, T])
                GLT = p.sb(ph6, "GLT", [128, T])
                GROW = p.sb(ph6, "GROW", [16, T])
                SELPM = p.sb(ph6, "SELPM", [128, 16])
                SELPMT = p.sb(ph6, "SELPMT", [16, 128])
                sGD = p.sem("gated")
                sGL = p.sem("glw")
                nc.sync.dma_start(out=SELPM[:], in_=selpm.ap()
                                  ).then_inc(sGD.h, 16); sGD.n += 16
                nc.sync.dma_start(out=SELPMT[:], in_=selpmt.ap()
                                  ).then_inc(sGD.h, 16); sGD.n += 16
                for r in range(NCORE):
                    btc = [psum.reserve() for _ in range(4)]
                    for dch in range(DCH):
                        nc.sync.wait_ge(sT.h, sT.n)
                        nc.sync.dma_start(
                            out=XGT[:],
                            in_=xg_all.ap()[r * D + dch * 128:
                                            r * D + (dch + 1) * 128, :],
                        ).then_inc(sGD.h, 16); sGD.n += 16
                        nc.tensor.wait_ge(sGD.h, sGD.n)
                        for tc in range(4):
                            mm = nc.tensor.matmul(
                                psum.bank(btc[tc])[0:16, 0:512],
                                GWT[:, dch * 16:(dch + 1) * 16],
                                XGT[:, tc * 512: tc * 512 + 512],
                                start=(dch == 0), stop=(dch == DCH - 1))
                        mm.then_inc(sT.h, 1); sT.n += 1
                    nc.vector.wait_ge(sT.h, sT.n)
                    nc.vector.wait_ge(sGL.h, sGL.n)
                    for tc in range(4):
                        cp = nc.vector.tensor_copy(GROW[:, tc * 512: tc * 512 + 512],
                                                   psum.bank(btc[tc])[0:16, 0:512])
                        psum.consumed(btc[tc], cp)
                        psum.release(btc[tc])
                    p.vmark(sV)
                    nc.sync.wait_ge(sV.h, sV.n)
                    nc.sync.dma_start(out=GLT[r * 16:(r + 1) * 16, :], in_=GROW[:]
                                      ).then_inc(sGL.h, 16); sGL.n += 16
                CROW = p.sb(ph6, "CROW", [1, 16])
                CCOL = p.sb(ph6, "CCOL", [16, 1])
                CNTB = p.sb(ph6, "CNTB", [128, 1])
                MSK = p.sb(ph6, "MSK", [128, T])
                nc.vector.wait_ge(sGL.h, sGL.n)
                nc.tensor.wait_ge(sGD.h, sGD.n)
                nc.vector.memset(LO[:, 0:1], -64.0)
                nc.vector.memset(LO[:, 1:2], 64.0)
                p.vmark(sV)
                for it in range(52):
                    nc.vector.drain()
                    nc.vector.tensor_tensor(LO[:, 2:3], LO[:, 0:1], LO[:, 1:2],
                                            op=OP.add)
                    nc.vector.drain()
                    nc.vector.tensor_scalar(out=LO[:, 2:3], in0=LO[:, 2:3],
                                            scalar1=0.5, scalar2=None, op0=OP.mult)
                    nc.vector.drain()
                    nc.vector.tensor_scalar(out=MSK[:], in0=GLT[:],
                                            scalar1=LO[:, 2:3], scalar2=None,
                                            op0=OP.is_ge)
                    nc.vector.drain()
                    nc.vector.reduce_sum(CNTB[:], MSK[:], axis=mybir.AxisListType.X)
                    p.vmark(sV)
                    nc.tensor.wait_ge(sV.h, sV.n)
                    b1 = psum.acquire()
                    nc.tensor.matmul(psum.bank(b1)[0:1, 0:16], CNTB[:], SELPM[:],
                                     start=True, stop=True).then_inc(sT.h, 1)
                    sT.n += 1
                    nc.vector.wait_ge(sT.h, sT.n)
                    c1 = nc.vector.tensor_copy(CROW[:], psum.bank(b1)[0:1, 0:16])
                    psum.consumed(b1, c1)
                    p.vmark(sV)
                    nc.tensor.wait_ge(sV.h, sV.n)
                    b2 = psum.acquire()
                    nc.tensor.transpose(psum.bank(b2)[0:16, 0:1], CROW[:],
                                        IDENT[0:1, 0:1]).then_inc(sT.h, 1)
                    sT.n += 1
                    nc.vector.wait_ge(sT.h, sT.n)
                    c2 = nc.vector.tensor_copy(CCOL[:], psum.bank(b2)[0:16, 0:1])
                    psum.consumed(b2, c2)
                    p.vmark(sV)
                    nc.tensor.wait_ge(sV.h, sV.n)
                    b3 = psum.acquire()
                    nc.tensor.matmul(psum.bank(b3)[:, 0:1], SELPMT[:], CCOL[:],
                                     start=True, stop=True).then_inc(sT.h, 1)
                    sT.n += 1
                    nc.vector.wait_ge(sT.h, sT.n)
                    c3 = nc.vector.tensor_copy(CNTB[:], psum.bank(b3)[:, 0:1])
                    psum.consumed(b3, c3)
                    nc.vector.drain()
                    nc.vector.tensor_scalar(out=LO[:, 3:4], in0=CNTB[:],
                                            scalar1=2048.0, scalar2=None,
                                            op0=OP.is_ge)
                    nc.vector.drain()
                    nc.vector.tensor_tensor(LO[:, 4:5], LO[:, 2:3], LO[:, 0:1],
                                            op=OP.subtract)
                    nc.vector.tensor_tensor(LO[:, 5:6], LO[:, 1:2], LO[:, 2:3],
                                            op=OP.subtract)
                    nc.vector.drain()
                    nc.vector.scalar_tensor_tensor(out=LO[:, 0:1], in0=LO[:, 4:5],
                                                   scalar=LO[:, 3:4], in1=LO[:, 0:1],
                                                   op0=OP.mult, op1=OP.add)
                    nc.vector.scalar_tensor_tensor(out=LO[:, 1:2], in0=LO[:, 5:6],
                                                   scalar=LO[:, 3:4], in1=LO[:, 2:3],
                                                   op0=OP.mult, op1=OP.add)
                p.vmark(sV)

                nc.tensor.wait_ge(sV.h, sV.n)
                bt0 = psum.acquire()
                nc.tensor.transpose(psum.bank(bt0)[0:1, 0:128], LO[:, 0:1],
                                    IDENT[:]).then_inc(sT.h, 1); sT.n += 1
                nc.vector.wait_ge(sT.h, sT.n)
                ct0 = nc.vector.tensor_copy(THROW[:], psum.bank(bt0)[0:1, 0:128])
                psum.consumed(bt0, ct0)
                p.vmark(sV)
                ph6.close()
                IOTAC = p.sb(ph, "IOTAC", [128, 384])
                IOTAR = p.sb(ph, "IOTAR", [128, 3])
                ROWS = BUFA[0:1, 0:5 * T]
                ZROW = BUFA[0:1, 5 * T:6 * T]
                POSBC = p.sb(ph, "POSBC", [128, T])
                SIGBC = p.sb(ph, "SIGBC", [128, T])
                POSTX = p.sb(ph, "POSTX", [128, 16])
                OHT = p.sb(ph, "OHT", [128, 2 * T])
                OH1 = BUFA[:, 32 * 384 + 8 * 384: 32 * 384 + 8 * 384 + 384]
                xtk = DblW(p, ph, "XTK", D)
                wug = DblW(p, ph, "WUG", D)
                wdw = DblW(p, ph, "WDW", D)
                DBB = BUFA[:, 32 * 384:32 * 384 + 1024]
                DBROW = BUFA[0:1, 32 * 384 + 1024:32 * 384 + 2048]
                UBF = p.sb(ph, "UBF", [128, 32])
                GBF = p.sb(ph, "GBF", [128, 32])
                EO = BUFA[:, 0:32 * 384]
                XIN = BUFA[:, 32 * 384:32 * 384 + 8 * 384]
                DVN = BUFA[:, 0:3 * 1024]
                TME = BUFA[:, 32 * 384 + 8 * 384: 32 * 384 + 8 * 384 + 384]
                sEW = p.sem("expw")
                sGA = p.sem("gath")
                nc.sync.wait_ge(sV.h, sV.n)
                nc.sync.dma_start(out=IOTAC[:], in_=iotac.ap()
                                  ).then_inc(sEW.h, 16); sEW.n += 16
                nc.sync.dma_start(out=IOTAR[:], in_=iotar.ap()
                                  ).then_inc(sEW.h, 16); sEW.n += 16
                nc.vector.wait_ge(sEW.h, sEW.n)
                rR = lambda i: ROWS[0:1, i * T:(i + 1) * T]
                for e in range(E):
                    nc.sync.wait_ge(sT.h, sT.n)
                    nc.sync.wait_ge(sV.h, sV.n)
                    sLL = p.sem("llrow")
                    nc.sync.dma_start(out=rR(4), in_=LLOG[e:e + 1, :]
                                      ).then_inc(sLL.h, 16); sLL.n += 16
                    nc.vector.wait_ge(sT.h, sT.n)
                    nc.vector.wait_ge(sLL.h, sLL.n)
                    nc.vector.memset(ZROW, 0.0)
                    nc.vector.drain()
                    nc.vector.tensor_scalar(out=rR(0), in0=rR(4),
                                            scalar1=THROW[0:1, e:e + 1],
                                            scalar2=None, op0=OP.is_ge)
                    nc.vector.drain()
                    nc.vector.tensor_tensor_scan(rR(1), rR(0), ZROW, 0.0,
                                                 op0=OP.add, op1=OP.add)
                    nc.vector.drain()
                    nc.vector.tensor_tensor(rR(2), rR(1), rR(0), op=OP.mult)
                    p.vmark(sV)
                    nc.scalar.wait_ge(sV.h, sV.n)
                    nc.scalar.activation(out=rR(3), in_=rR(4), func=AF.Sigmoid)
                    p.smark(sS)
                    nc.tensor.wait_ge(sS.h, sS.n)
                    bcast_row(rR(2), POSBC[:], T)
                    bcast_row(rR(3), SIGBC[:], T)
                    for k in range(16):
                        bx = psum.acquire()
                        nc.tensor.transpose(psum.bank(bx)[:, 0:1],
                                            ROWS[0:1, 2 * T + k * 128: 2 * T + (k + 1) * 128],
                                            IDENT[0:1, 0:1]).then_inc(sT.h, 1)
                        sT.n += 1
                        nc.vector.wait_ge(sT.h, sT.n)
                        cx = nc.vector.tensor_copy(POSTX[:, k:k + 1],
                                                   psum.bank(bx)[:, 0:1])
                        psum.consumed(bx, cx)
                    p.vmark(sV)
                    bg = [psum.reserve() for _ in range(8)]
                    xtk.issue(scxt.ap()[0:128, :])
                    for k in range(16):
                        nc.vector.wait_ge(sT.h, sT.n)
                        nc.vector.wait_ge(sV.h, sV.n)
                        nc.vector.drain()
                        nc.vector.tensor_scalar(out=OH1[:], in0=IOTAC[:],
                                                scalar1=POSTX[:, k:k + 1],
                                                scalar2=None, op0=OP.is_equal)
                        p.vmark(sV)
                        XTEK = xtk.consume()
                        nc.tensor.wait_ge(sV.h, sV.n)
                        for dch in range(DCH):
                            mm = nc.tensor.matmul(
                                psum.bank(bg[dch])[:, 0:384],
                                XTEK[:, dch * 128:(dch + 1) * 128], OH1[:],
                                start=(k == 0), stop=(k == 15))
                        mm.then_inc(sT.h, 1); sT.n += 1
                        xtk.done()
                        if k < 15:
                            xtk.issue(scxt.ap()[(k + 1) * 128:(k + 2) * 128, :])
                    nc.vector.wait_ge(sT.h, sT.n)
                    nc.vector.wait_ge(sSC.h, sSC.n)
                    for dch in range(DCH):
                        cx = nc.vector.tensor_copy(
                            XIN[:, dch * 384:(dch + 1) * 384],
                            psum.bank(bg[dch])[:, 0:384])
                        psum.consumed(bg[dch], cx)
                        psum.release(bg[dch])
                    p.vmark(sV)
                    nc.sync.wait_ge(sS.h, sS.n)
                    nc.sync.wait_ge(sV.h, sV.n)
                    sUB = p.sem("ubgb")
                    nc.sync.dma_start(out=UBF[:], in_=upbf.ap()[e]
                                      ).then_inc(sUB.h, 16); sUB.n += 16
                    nc.sync.dma_start(out=GBF[:], in_=gbf.ap()[e]
                                      ).then_inc(sUB.h, 16); sUB.n += 16
                    nc.scalar.wait_ge(sUB.h, sUB.n)
                    bdown = [psum.reserve() for _ in range(6)]
                    wug.issue(upwt.ap()[e, :, 0:128]
                              .rearrange("(c p) j -> p c j", p=128), rearr_c=8)
                    wug.issue(gwt2.ap()[e, :, 0:128]
                              .rearrange("(c p) j -> p c j", p=128), rearr_c=8)
                    wdw.issue(dwnt.ap()[e, 0:128, :])
                    for fc in range(32):
                        WUG = wug.consume()
                        nc.tensor.wait_ge(sV.h, sV.n)
                        bup = psum.acquire()
                        for dch in range(DCH):
                            mm = nc.tensor.matmul(
                                psum.bank(bup)[:, 0:384],
                                WUG[:, dch * 128:(dch + 1) * 128],
                                XIN[:, dch * 384: dch * 384 + 384],
                                start=(dch == 0), stop=(dch == DCH - 1))
                        mm.then_inc(sT.h, 1); sT.n += 1
                        wug.done()
                        if fc < 31:
                            wug.issue(upwt.ap()[e, :, (fc + 1) * 128:(fc + 2) * 128]
                                      .rearrange("(c p) j -> p c j", p=128), rearr_c=8)
                        WUG = wug.consume()
                        bgw = psum.acquire()
                        for dch in range(DCH):
                            mm = nc.tensor.matmul(
                                psum.bank(bgw)[:, 0:384],
                                WUG[:, dch * 128:(dch + 1) * 128],
                                XIN[:, dch * 384: dch * 384 + 384],
                                start=(dch == 0), stop=(dch == DCH - 1))
                        mm.then_inc(sT.h, 1); sT.n += 1
                        wug.done()
                        if fc < 31:
                            wug.issue(gwt2.ap()[e, :, (fc + 1) * 128:(fc + 2) * 128]
                                      .rearrange("(c p) j -> p c j", p=128), rearr_c=8)
                        nc.scalar.wait_ge(sT.h, sT.n)
                        nc.scalar.wait_ge(sSC.h, sSC.n)
                        i1 = nc.scalar.activation(out=TME, in_=psum.bank(bgw)[:, 0:384],
                                                  func=AF.Silu,
                                                  bias=GBF[:, fc:fc + 1], scale=1.0)
                        psum.consumed(bgw, i1)
                        esl = EO[:, fc * 384:(fc + 1) * 384]
                        i2 = nc.scalar.activation(out=esl, in_=psum.bank(bup)[:, 0:384],
                                                  func=AF.Identity,
                                                  bias=UBF[:, fc:fc + 1], scale=1.0)
                        psum.consumed(bup, i2)
                        p.smark(sS)
                        nc.vector.wait_ge(sS.h, sS.n)
                        nc.vector.drain()
                        nc.vector.tensor_tensor(esl, esl, TME, op=OP.mult)
                        p.vmark(sV)
                        DWN = wdw.consume()
                        nc.tensor.wait_ge(sV.h, sV.n)
                        for tt in range(3):
                            for dh in range(2):
                                mm = nc.tensor.matmul(
                                    psum.bank(bdown[tt * 2 + dh])[:, 0:512],
                                    EO[:, fc * 384 + tt * 128: fc * 384 + (tt + 1) * 128],
                                    DWN[:, dh * 512:(dh + 1) * 512],
                                    start=(fc == 0), stop=(fc == 31))
                        mm.then_inc(sT.h, 1); sT.n += 1
                        wdw.done()
                        if fc < 31:
                            wdw.issue(dwnt.ap()[e, (fc + 1) * 128:(fc + 2) * 128, :])
                    nc.sync.wait_ge(sT.h, sT.n)
                    sDB = p.sem("dbrow")
                    nc.sync.dma_start(out=DBROW, in_=dbw.ap()[e:e + 1, :]
                                      ).then_inc(sDB.h, 16); sDB.n += 16
                    nc.tensor.wait_ge(sDB.h, sDB.n)
                    for half in range(2):
                        bd = psum.acquire()
                        nc.tensor.matmul(psum.bank(bd)[:, 0:512], ONESR[:],
                                         DBROW[:, half * 512:(half + 1) * 512],
                                         start=True, stop=True).then_inc(sT.h, 1)
                        sT.n += 1
                        nc.vector.wait_ge(sT.h, sT.n)
                        cd = nc.vector.tensor_copy(DBB[:, half * 512:(half + 1) * 512],
                                                   psum.bank(bd)[:, 0:512])
                        psum.consumed(bd, cd)
                    p.vmark(sV)
                    nc.vector.wait_ge(sT.h, sT.n)
                    for tt in range(3):
                        for dh in range(2):
                            bb = bdown[tt * 2 + dh]
                            dsl = DVN[:, tt * 1024 + dh * 512: tt * 1024 + (dh + 1) * 512]
                            nc.vector.drain()
                            i = nc.vector.tensor_tensor(dsl, psum.bank(bb)[:, 0:512],
                                                        DBB[:, dh * 512:(dh + 1) * 512],
                                                        op=OP.add)
                            psum.consumed(bb, i)
                            psum.release(bb)
                    for tts in ((0, 1), (2,)):
                        nc.vector.wait_ge(sT.h, sT.n)
                        for j, tt in enumerate(tts):
                            osl = OHT[:, j * T:(j + 1) * T]
                            nc.vector.drain()
                            nc.vector.tensor_scalar(out=osl, in0=POSBC[:],
                                                    scalar1=IOTAR[:, tt:tt + 1],
                                                    scalar2=None, op0=OP.is_equal)
                            nc.vector.drain()
                            nc.vector.tensor_tensor(osl, osl, SIGBC[:], op=OP.mult)
                        p.vmark(sV)
                        nc.tensor.wait_ge(sV.h, sV.n)
                        for dch in range(DCH):
                            for tc in range(4):
                                bb = psum.acquire()
                                for j, tt in enumerate(tts):
                                    mm = nc.tensor.matmul(
                                        psum.bank(bb)[:, 0:512],
                                        DVN[:, tt * 1024 + dch * 128:
                                            tt * 1024 + dch * 128 + 128],
                                        OHT[:, j * T + tc * 512: j * T + tc * 512 + 512],
                                        start=(j == 0), stop=(j == len(tts) - 1))
                                mm.then_inc(sT.h, 1); sT.n += 1
                                nc.vector.wait_ge(sT.h, sT.n)
                                asl = BUFB[:, dch * T + tc * 512:
                                           dch * T + tc * 512 + 512]
                                nc.vector.drain()
                                i = nc.vector.tensor_tensor(asl, asl,
                                                            psum.bank(bb)[:, 0:512],
                                                            op=OP.add)
                                psum.consumed(bb, i)
                        p.vmark(sV)
                    nc.vector.drain().then_inc(sSC.h, 1); sSC.n += 1

        if _EMIT_UPTO >= 8:
            with ExitStack() as ph:
                sEW = p.sem("expw"); sGA = p.sem("gath"); sSC = p.sem("scat")
                for eng in (nc.sync, nc.vector, nc.scalar, nc.tensor):
                    eng.wait_ge(sEW.h, sEW.n)
                    eng.wait_ge(sGA.h, sGA.n)
                    eng.wait_ge(sSC.h, sSC.n)
                XN2S = p.sb(ph, "XN2S", [128, DCH * 512])
                wsh = DblW(p, ph, "WSH", D)
                GS = BUFA[:, 0:16 * 512]
                US = BUFA[:, 16 * 512:32 * 512]
                sSH = p.sem("shw")
                for tc in range(4):
                    nc.sync.wait_ge(sT.h, sT.n)
                    nc.sync.dma_start(
                        out=XN2S[:].rearrange("p (c t) -> p c t", c=DCH),
                        in_=xg_in.ap()[:, tc * 512:(tc + 1) * 512]
                            .rearrange("(c p) t -> p c t", p=128),
                    ).then_inc(sSH.h, 16); sSH.n += 16
                    gujobs = [(wsrc, dst, fn, fc)
                              for (wsrc, dst, fn) in ((shgt, GS, AF.Silu),
                                                      (shut, US, AF.Identity))
                              for fc in range(16)]
                    if tc == 0:
                        wsh.issue(shgt.ap()[:, 0:128]
                                  .rearrange("(c p) j -> p c j", p=128), rearr_c=8)
                    nc.tensor.wait_ge(sSH.h, sSH.n)
                    for gi, (wsrc, dst, fn, fc) in enumerate(gujobs):
                        if True:
                            WSH = wsh.consume()
                            nc.tensor.wait_ge(sV.h, sV.n)
                            bb = psum.acquire()
                            for dch in range(DCH):
                                mm = nc.tensor.matmul(
                                    psum.bank(bb)[:, 0:512],
                                    WSH[:, dch * 128:(dch + 1) * 128],
                                    XN2S[:, dch * 512: dch * 512 + 512],
                                    start=(dch == 0), stop=(dch == DCH - 1))
                            mm.then_inc(sT.h, 1); sT.n += 1
                            wsh.done()
                            if gi < len(gujobs) - 1:
                                w2, _, _, f2 = gujobs[gi + 1]
                                wsh.issue(w2.ap()[:, f2 * 128:(f2 + 1) * 128]
                                          .rearrange("(c p) j -> p c j", p=128),
                                          rearr_c=8)
                            else:
                                wsh.issue(shdt.ap()[0:128, :]
                                          .rearrange("f (c j) -> f c j", c=8),
                                          rearr_c=8)
                            nc.scalar.wait_ge(sT.h, sT.n)
                            i = nc.scalar.activation(out=dst[:, fc * 512:(fc + 1) * 512],
                                                     in_=psum.bank(bb)[:, 0:512],
                                                     func=fn)
                            psum.consumed(bb, i)
                    p.smark(sS)
                    nc.vector.wait_ge(sS.h, sS.n)
                    nc.vector.drain()
                    nc.vector.tensor_tensor(GS, GS, US, op=OP.mult)
                    p.vmark(sV)
                    bsh = [psum.reserve() for _ in range(8)]
                    for fc in range(16):
                        WSH = wsh.consume()
                        nc.tensor.wait_ge(sV.h, sV.n)
                        for jg in range(8):
                            mm = nc.tensor.matmul(
                                psum.bank(bsh[jg])[:, 0:512],
                                WSH[:, jg * 128:(jg + 1) * 128],
                                GS[:, fc * 512:(fc + 1) * 512]
                                if False else BUFA[:, fc * 512:(fc + 1) * 512],
                                start=(fc == 0), stop=(fc == 15))
                        mm.then_inc(sT.h, 1); sT.n += 1
                        wsh.done()
                        if fc < 15:
                            wsh.issue(shdt.ap()[(fc + 1) * 128:(fc + 2) * 128, :]
                                      .rearrange("f (c j) -> f c j", c=8), rearr_c=8)
                        elif tc < 3:
                            wsh.issue(shgt.ap()[:, 0:128]
                                      .rearrange("(c p) j -> p c j", p=128), rearr_c=8)
                    nc.vector.wait_ge(sT.h, sT.n)
                    for jg in range(8):
                        asl = BUFB[:, jg * T + tc * 512: jg * T + tc * 512 + 512]
                        nc.vector.drain()
                        i = nc.vector.tensor_tensor(asl, asl,
                                                    psum.bank(bsh[jg])[:, 0:512],
                                                    op=OP.add)
                        psum.consumed(bsh[jg], i)
                        psum.release(bsh[jg])
                    p.vmark(sV)

        if _EMIT_UPTO >= 9:
            with ExitStack() as ph:
                sSH = p.sem("shw"); sSC = p.sem("scat")
                for eng in (nc.sync, nc.vector, nc.scalar, nc.tensor):
                    eng.wait_ge(sSH.h, sSH.n)
                    eng.wait_ge(sSC.h, sSC.n)
                nc.sync.wait_ge(sT.h, sT.n)
                nc.sync.wait_ge(sV.h, sV.n)
                nc.sync.wait_ge(sS.h, sS.n)
                sXR = p.sem("xrl")
                nc.sync.dma_start(out=BUFA[:], in_=sc_x.ap()
                                  ).then_inc(sXR.h, 16); sXR.n += 16
                nc.vector.wait_ge(sXR.h, sXR.n)
                for dch in range(DCH):
                    nc.vector.drain()
                    asl = BUFA[:, dch * T:(dch + 1) * T]
                    nc.vector.tensor_tensor(asl, asl, BUFB[:, dch * T:(dch + 1) * T],
                                            op=OP.add)
                p.vmark(sV)
                nc.tensor.wait_ge(sV.h, sV.n)
                OT = p.sb(ph, "OT", [128, D])
                for k in range(16):
                    nc.vector.wait_ge(sO.h, sO.n)
                    for dch in range(DCH):
                        bx = psum.acquire()
                        nc.tensor.transpose(
                            psum.bank(bx)[:, 0:128],
                            BUFA[:, dch * T + k * 128: dch * T + k * 128 + 128],
                            IDENT[:]).then_inc(sT.h, 1)
                        sT.n += 1
                        nc.vector.wait_ge(sT.h, sT.n)
                        cp = nc.vector.tensor_copy(OT[:, dch * 128:(dch + 1) * 128],
                                                   psum.bank(bx)[:, 0:128])
                        psum.consumed(bx, cp)
                    p.vmark(sV)
                    nc.sync.wait_ge(sV.h, sV.n)
                    nc.sync.dma_start(out=out_t.ap()[k * 128:(k + 1) * 128, :],
                                      in_=OT[:]).then_inc(sO.h, 16); sO.n += 16

        if _DEBUG_DUMP is not None:
            name = _DEBUG_DUMP[0]
            src = {"BUFA": BUFA, "BUFB": BUFB}[name]
            dbg = p.dout("dbg", [128, DCH * T])
            for s in (sV, sS, sT):
                nc.sync.wait_ge(s.h, s.n)
            nc.sync.dma_start(out=dbg.ap(), in_=src[:]).then_inc(sO.h, 16); sO.n += 16

        if _EMIT_UPTO < 9:
            with ExitStack() as ph:
                OT = p.sb(ph, "OT", [128, D])
                nc.tensor.wait_ge(sV.h, sV.n)
                nc.tensor.wait_ge(sS.h, sS.n)
                for k in range(16):
                    nc.vector.wait_ge(sO.h, sO.n)
                    for dch in range(DCH):
                        b = psum.acquire()
                        nc.tensor.transpose(
                            psum.bank(b)[:, 0:128],
                            BUFA[:, dch * T + k * 128: dch * T + k * 128 + 128],
                            IDENT[:]).then_inc(sT.h, 1)
                        sT.n += 1
                        nc.vector.wait_ge(sT.h, sT.n)
                        cp = nc.vector.tensor_copy(OT[:, dch * 128: dch * 128 + 128],
                                                   psum.bank(b)[:, 0:128])
                        psum.consumed(b, cp)
                    p.vmark(sV)
                    nc.sync.wait_ge(sV.h, sV.n)
                    nc.sync.dma_start(out=out_t.ap()[k * 128:(k + 1) * 128, :],
                                      in_=OT[:]).then_inc(sO.h, 16); sO.n += 16
        nc.sync.wait_ge(sO.h, sO.n)

    return p


def _host_inputs(inputs):
    f = np.float32
    c = np.ascontiguousarray
    w_in = inputs["L0_attn_w_in"]
    wq, wk, wv = w_in[0:D], w_in[D:2 * D], w_in[2 * D:3 * D]
    w0qkvp = np.empty((8, D, 384), f)
    for pr in range(8):
        r = slice(128 * pr, 128 * pr + 128)
        w0qkvp[pr, :, 0:128] = wq[r].T
        w0qkvp[pr, :, 128:256] = wk[r].T
        w0qkvp[pr, :, 256:384] = wv[r].T
    b_in = inputs["L0_attn_b_in"]
    b0qkv = np.zeros((128, 24), f)
    for pr in range(8):
        b0qkv[:, pr] = b_in[128 * pr:128 * pr + 128]
        b0qkv[:, 8 + pr] = b_in[D + 128 * pr:D + 128 * pr + 128]
        b0qkv[:, 16 + pr] = b_in[2 * D + 128 * pr:2 * D + 128 * pr + 128]
    w_out = inputs["L0_attn_w_out"]
    w0outp = np.empty((8, 128, D), f)
    for pr in range(8):
        w0outp[pr] = w_out.T[128 * pr:128 * pr + 128, :]
    b0out = c(inputs["L0_attn_b_out"].reshape(DCH, 128).T)

    com = {
        "ident": np.eye(128, dtype=f),
        "ones_col": np.ones((128, 1), f),
        "ones_row": np.ones((1, 128), f),
        "mask_c": np.concatenate([
            (np.arange(128)[:, None] <= np.arange(256)[None, :]).astype(f),
            (np.arange(128)[:, None] <= (np.arange(256)[None, :] - 128)).astype(f)],
            axis=1),
        "ones2": np.array([[1, 0, 0, 1]], f).repeat(128, 0),
        "sel2t": np.concatenate([np.ones((2, 64), f) * np.array([[1], [0]], f),
                                 np.ones((2, 64), f) * np.array([[0], [1]], f)],
                                axis=1),
        "w0qkvp": w0qkvp, "b0qkv": b0qkv, "w0outp": w0outp, "b0out": b0out,
        "w0r": c(inputs["L0_ffn_W_r"].T), "w0k": c(inputs["L0_ffn_W_k"].T),
        "w0v": c(inputs["L0_ffn_W_v"].T),
        "mur": c(inputs["L0_ffn_mu_r"].reshape(DCH, 128).T),
        "muk": c(inputs["L0_ffn_mu_k"].reshape(DCH, 128).T),
    }
    com["w1qkvf"] = c(inputs["L1_attn_W_qkvf"].T)
    com["w1conv"] = c(inputs["L1_attn_W_conv"].transpose(2, 0, 1))
    wproj_t = inputs["L1_attn_W_proj"].T
    w1projp = np.empty((8, 128, D), f)
    for pr in range(8):
        w1projp[pr] = wproj_t[128 * pr:128 * pr + 128, :]
    com["w1projp"] = w1projp
    com["w1ff1"] = c(inputs["L1_attn_W_ff1"])
    com["w1ff2"] = c(inputs["L1_attn_W_ff2"])
    for nm, key in (("ln1g", "L1_attn_ln1_g"), ("ln1b", "L1_attn_ln1_b"),
                    ("ln2g", "L1_attn_ln2_g"), ("ln2b", "L1_attn_ln2_b")):
        com[nm] = c(inputs[key].reshape(DCH, 128).T)
    selh8 = np.zeros((128, 32), f)
    for fi in range(4):
        for hh in range(2):
            selh8[64 * hh:64 * hh + 64, fi * 8 + hh * 4 + fi] = 1.0
    com["selh8"] = selh8
    sel8d = np.zeros((8, 2), f)
    sel8d[0:4, 0] = 1.0
    sel8d[4:8, 1] = 1.0
    com["sel8d"] = sel8d
    sel2to8 = np.zeros((2, 8), f)
    sel2to8[0, 0:4] = 1.0
    sel2to8[1, 4:8] = 1.0
    com["sel2to8"] = sel2to8
    self8 = np.zeros((8, 4 * 128), f)
    for fi in range(4):
        self8[fi, fi * 128: fi * 128 + 64] = 1.0
        self8[4 + fi, fi * 128 + 64: fi * 128 + 128] = 1.0
    com["self8"] = self8
    com["gwt"] = c(inputs["L1_ffn_gate_w"].T)
    iot = (np.arange(16)[:, None] * 128 + np.arange(128)[None, :]).astype(f)
    com["iotap1"] = iot + 1.0
    com["iotac"] = np.broadcast_to(np.arange(1, 385, dtype=f)[None, :], (128, 384)).copy()
    com["iotar"] = (np.arange(3)[None, :] * 128 + np.arange(128)[:, None] + 1).astype(f)
    com["selpm"] = (np.arange(128)[:, None] % 16 == np.arange(16)[None, :]).astype(f)
    com["selpmt"] = c(com["selpm"].T)
    com["upwt"] = c(inputs["L1_ffn_up_w"].transpose(0, 2, 1))
    com["gwt2"] = c(inputs["L1_ffn_gw"].transpose(0, 2, 1))
    com["dwnt"] = c(inputs["L1_ffn_down_w"].transpose(0, 2, 1))
    com["upbf"] = c(inputs["L1_ffn_up_b"].reshape(E, 32, 128).transpose(0, 2, 1))
    com["gbf"] = c(inputs["L1_ffn_gb"].reshape(E, 32, 128).transpose(0, 2, 1))
    com["dbw"] = c(inputs["L1_ffn_down_b"])
    com["shgt"] = c(inputs["L1_ffn_sh_gate_w"].T)
    com["shut"] = c(inputs["L1_ffn_sh_up_w"].T)
    com["shdt"] = c(inputs["L1_ffn_sh_down_w"].T)
    x = inputs["L0_attn_x"]
    maps = []
    for core in range(NCORE):
        m = dict(com)
        m["x_in"] = c(x[core])
        maps.append(m)
    return maps


def kernel(**inputs):
    maps = _host_inputs(inputs)
    p = build_program()
    res = run_bass_kernel_spmd(p.nc, maps, core_ids=list(range(NCORE)))
    out = np.stack([res.results[i]["out"] for i in range(NCORE)], axis=0)
    return out.astype(np.float32)


# version 07 (final)
# speedup vs baseline: 1.4760x; 1.0851x over previous
import os
import numpy as np
from contextlib import ExitStack
import concourse.bass as bass
import concourse.mybir as mybir
from concourse.bass_utils import run_bass_kernel_spmd
from concourse.alu_op_type import AluOpType as OP

F32 = mybir.dt.float32
I16 = mybir.dt.int16
I32 = mybir.dt.int32
AF = mybir.ActivationFunctionType

D = 1024
T = 2048
NCORE = 8
DCH = D // 128
H = 16
HS = 64
FUT = 4
E = 16
FF = 4096
SHD = 2048
CAP = 384
EPS = 1e-5

_EMIT_UPTO = 99
_DEBUG_DUMP = None


class Sem:
    __slots__ = ("h", "n", "name")

    def __init__(self, stack, nc, name):
        self.h = stack.enter_context(nc.semaphore(name))
        self.n = 0
        self.name = name


class Prog:
    def __init__(self):
        self.nc = bass.Bass("TRN2", target_bir_lowering=False,
                            debug=bool(os.environ.get("BASSDBG")))
        self.stack = ExitStack()
        self.sems = {}
        self.dram = {}

    def sem(self, name):
        if name not in self.sems:
            self.sems[name] = Sem(self.stack, self.nc, name)
        return self.sems[name]

    _names = None

    def sb(self, stack, name, shape, dtype=F32):
        if self._names is None:
            self._names = {}
        n = self._names.get(name, 0)
        self._names[name] = n + 1
        if n:
            name = f"{name}_{n}"
        return stack.enter_context(self.nc.sbuf_tensor(name, list(shape), dtype))

    def din(self, name, shape, dtype=F32):
        t = self.nc.dram_tensor(name, list(shape), dtype, kind="ExternalInput")
        self.dram[name] = t
        return t

    def dout(self, name, shape, dtype=F32):
        t = self.nc.dram_tensor(name, list(shape), dtype, kind="ExternalOutput")
        self.dram[name] = t
        return t

    def dtmp(self, name, shape, dtype=F32):
        t = self.nc.dram_tensor(name, list(shape), dtype)
        self.dram[name] = t
        return t

    def vmark(self, sem):
        self.nc.vector.drain().then_inc(sem.h, 1); sem.n += 1

    def smark(self, sem):
        self.nc.scalar.drain().then_inc(sem.h, 1); sem.n += 1

    def tmark(self, sem):
        self.nc.tensor.drain().then_inc(sem.h, 1); sem.n += 1


class PsumPool:

    def __init__(self, p, stack):
        self.p = p
        self.banks = [stack.enter_context(p.nc.psum_tensor(f"pb{i}", [128, 512], F32))
                      for i in range(8)]
        self.sems = [p.sem(f"pbs{i}") for i in range(8)]
        self.pending = [0] * 8
        self.rr = 0

    def __post(self):
        pass

    def acquire(self, eng="tensor"):
        for _ in range(8):
            i = self.rr
            self.rr = (self.rr + 1) % 8
            if i not in getattr(self, "reserved", ()):
                break
        if self.pending[i] > 0:
            getattr(self.p.nc, eng).wait_ge(self.sems[i].h, self.pending[i])
        return i

    def reserve(self, eng="tensor"):
        if not hasattr(self, "reserved"):
            self.reserved = set()
        i = self.acquire(eng)
        self.reserved.add(i)
        return i

    def release(self, i):
        self.reserved.discard(i)

    def bank(self, i):
        return self.banks[i]

    def consumed(self, i, instr):
        instr.then_inc(self.sems[i].h, 1)
        self.sems[i].n += 1
        self.pending[i] = self.sems[i].n


class DblW:

    def __init__(self, p, ph, name, cols, dtype=F32):
        self.p = p
        self.bufs = [p.sb(ph, name + "0", [128, cols], dtype),
                     p.sb(ph, name + "1", [128, cols], dtype)]
        self.sems = [p.sem(name + "s0"), p.sem(name + "s1")]
        self.freed = [0, 0]
        self.n = 0
        self.c = 0

    def issue(self, in_ap, rearr_c=None):
        nc = self.p.nc
        i = self.n % 2
        sT = self.p.sem("ten")
        nc.sync.wait_ge(sT.h, self.freed[i])
        out = (self.bufs[i][:].rearrange("p (c j) -> p c j", c=rearr_c)
               if rearr_c else self.bufs[i][:])
        nc.sync.dma_start(out=out, in_=in_ap).then_inc(self.sems[i].h, 16)
        self.sems[i].n += 16
        self.n += 1

    def consume(self):
        k = self.c
        self.c += 1
        i = k % 2
        self.p.nc.tensor.wait_ge(self.sems[i].h, (k // 2 + 1) * 16)
        return self.bufs[i]

    def done(self):
        self.freed[(self.c - 1) % 2] = self.p.sem("ten").n


def build_program():
    p = Prog()
    nc = p.nc

    x_in = p.din("x_in", [T, D])
    out_t = p.dout("out", [T, D])
    ident = p.din("ident", [128, 128])
    ones = p.din("ones_col", [128, 1])
    onesr = p.din("ones_row", [1, 128])
    maskc = p.din("mask_c", [128, 512])
    ones2 = p.din("ones2", [128, 4])
    sel2t = p.din("sel2t", [2, 128])
    w0qkvp = p.din("w0qkvp", [8, D, 384])
    b0qkv = p.din("b0qkv", [128, 24])
    w0outp = p.din("w0outp", [8, 128, D])
    b0out = p.din("b0out", [128, 8])
    w0r = p.din("w0r", [D, D])
    w0k = p.din("w0k", [D, D])
    w0v = p.din("w0v", [D, D])
    mur = p.din("mur", [128, 8])
    muk = p.din("muk", [128, 8])
    w1qkvf = p.din("w1qkvf", [D, 4 * D])
    w1conv = p.din("w1conv", [FUT, 64, 64])
    w1projp = p.din("w1projp", [8, 128, D])
    w1ff1 = p.din("w1ff1", [D, FF])
    w1ff2 = p.din("w1ff2", [FF, D])
    ln1g = p.din("ln1g", [128, 8])
    ln1b = p.din("ln1b", [128, 8])
    ln2g = p.din("ln2g", [128, 8])
    ln2b = p.din("ln2b", [128, 8])
    selh8 = p.din("selh8", [128, 32])
    sel8d = p.din("sel8d", [8, 2])
    sel2to8 = p.din("sel2to8", [2, 8])
    self8 = p.din("self8", [8, FUT * 128])
    gwt = p.din("gwt", [D, E])
    iotap1 = p.din("iotap1", [16, 128])
    iotac = p.din("iotac", [128, 384])
    iotar = p.din("iotar", [128, 3])
    selpm = p.din("selpm", [128, 16])
    selpmt = p.din("selpmt", [16, 128])
    upwt = p.din("upwt", [E, D, FF])
    gwt2 = p.din("gwt2", [E, D, FF])
    dwnt = p.din("dwnt", [E, FF, D])
    upbf = p.din("upbf", [E, 128, 32])
    gbf = p.din("gbf", [E, 128, 32])
    dbw = p.din("dbw", [E, D])
    shgt = p.din("shgt", [D, SHD])
    shut = p.din("shut", [D, SHD])
    shdt = p.din("shdt", [SHD, D])
    scq = p.dtmp("scq", [D, T])
    sck = p.dtmp("sck", [D, T])
    scf = p.dtmp("scf", [D, T])
    scvt = p.dtmp("scvt", [T, D])
    sckf = p.dtmp("sckf", [FUT, D, T])
    scvf = p.dtmp("scvf", [FUT, D, T])
    sc_x = p.dtmp("sc_x", [128, DCH * T])
    xg_in = p.dtmp("xg_in", [D, T])
    xg_all = p.dtmp("xg_all", [NCORE * D, T])
    scxt = p.dtmp("scxt", [T, D])

    with p.stack:
        psum = PsumPool(p, p.stack)
        BUFA = p.sb(p.stack, "BUFA", [128, DCH * T])
        BUFB = p.sb(p.stack, "BUFB", [128, DCH * T])
        IDENT = p.sb(p.stack, "IDENT", [128, 128])
        ONES = p.sb(p.stack, "ONES", [128, 1])
        ONESR = p.sb(p.stack, "ONESR", [1, 128])
        MKC = p.sb(p.stack, "MKC", [128, 512])
        ONES2 = p.sb(p.stack, "ONES2", [128, 4])
        SEL2T = p.sb(p.stack, "SEL2T", [2, 128])
        BQKV = p.sb(p.stack, "BQKV", [128, 24])
        BOUT = p.sb(p.stack, "BOUT", [128, 8])
        MUR = p.sb(p.stack, "MUR", [128, 8])
        MUK = p.sb(p.stack, "MUK", [128, 8])
        OMR = p.sb(p.stack, "OMR", [128, 8])
        OMK = p.sb(p.stack, "OMK", [128, 8])
        GWT = p.sb(p.stack, "GWT", [128, DCH * E])

        sL = p.sem("ld")
        sW = p.sem("wload")
        sV = p.sem("vec")
        sS = p.sem("sca")
        sT = p.sem("ten")
        sO = p.sem("outd")

        for (dst, src) in [(IDENT, ident.ap()), (ONES, ones.ap()),
                           (ONESR, onesr.ap()), (MKC, maskc.ap()),
                           (ONES2, ones2.ap()), (SEL2T, sel2t.ap()),
                           (BQKV, b0qkv.ap()), (BOUT, b0out.ap()),
                           (MUR, mur.ap()), (MUK, muk.ap())]:
            nc.sync.dma_start(out=dst[:], in_=src).then_inc(sL.h, 16); sL.n += 16
        nc.sync.dma_start(out=GWT[:].rearrange("p (c e) -> p c e", c=DCH),
                          in_=gwt.ap().rearrange("(c p) e -> p c e", p=128)
                          ).then_inc(sL.h, 16); sL.n += 16

        def bcast_row(row_ap, dst_ap, n, nparts=128):
            for c0 in range(0, n, 512):
                w = min(512, n - c0)
                b = psum.acquire()
                nc.tensor.matmul(psum.bank(b)[0:nparts, 0:w],
                                 ONESR[:, 0:nparts], row_ap[:, c0:c0 + w],
                                 start=True, stop=True).then_inc(sT.h, 1)
                sT.n += 1
                nc.vector.wait_ge(sT.h, sT.n)
                cp = nc.vector.tensor_copy(dst_ap[:, c0:c0 + w],
                                           psum.bank(b)[0:nparts, 0:w])
                psum.consumed(b, cp)

        def transpose_fm(src_ap_fn, dst_ap_fn, ktiles, dchunks, consumer="vector"):
            for k in range(ktiles):
                for dch in range(dchunks):
                    b = psum.acquire()
                    nc.tensor.transpose(psum.bank(b)[:, 0:128],
                                        src_ap_fn(k, dch), IDENT[:]
                                        ).then_inc(sT.h, 1)
                    sT.n += 1
                    nc.vector.wait_ge(sT.h, sT.n)
                    cp = nc.vector.tensor_copy(dst_ap_fn(k, dch),
                                               psum.bank(b)[:, 0:128])
                    psum.consumed(b, cp)

        nc.sync.dma_start(
            out=BUFB[:].rearrange("p (k d) -> p k d", k=16),
            in_=x_in.ap().rearrange("(k p) d -> p k d", p=128),
        ).then_inc(sL.h, 16); sL.n += 16
        nc.tensor.wait_ge(sL.h, sL.n)
        transpose_fm(
            lambda k, dch: BUFB[:, k * D + dch * 128: k * D + dch * 128 + 128],
            lambda k, dch: BUFA[:, dch * T + k * 128: dch * T + k * 128 + 128],
            16, DCH)
        p.vmark(sV)

        def emit_layernorm(src, dst, ph, gb=None):
            SQ = p.sb(ph, "SQ", [128, DCH * 512])
            STAT = p.sb(ph, "STAT", [1, 4 * 512])
            BC0 = p.sb(ph, "BC0", [128, 512])
            BC1 = p.sb(ph, "BC1", [128, 512])
            nc.scalar.wait_ge(sV.h, sV.n)
            nc.tensor.wait_ge(sV.h, sV.n)
            for tc in range(4):
                bs = psum.acquire(); bq = psum.acquire()
                nc.scalar.wait_ge(sT.h, sT.n)
                for dch in range(DCH):
                    sl = src[:, dch * T + tc * 512: dch * T + tc * 512 + 512]
                    nc.scalar.activation(out=SQ[:, dch * 512: dch * 512 + 512],
                                         in_=sl, func=AF.Square).then_inc(sS.h, 1)
                    sS.n += 1
                    nc.tensor.matmul(psum.bank(bs)[0:1, :], ONES[:], sl,
                                     start=(dch == 0), stop=(dch == DCH - 1))
                    nc.tensor.wait_ge(sS.h, sS.n)
                    mmq = nc.tensor.matmul(psum.bank(bq)[0:1, :], ONES[:],
                                           SQ[:, dch * 512: dch * 512 + 512],
                                           start=(dch == 0), stop=(dch == DCH - 1))
                    if dch == DCH - 1:
                        mmq.then_inc(sT.h, 1); sT.n += 1
                nc.vector.wait_ge(sT.h, sT.n)
                nc.vector.wait_ge(sV.h, sV.n)
                srow = lambda i: STAT[0:1, i * 512: i * 512 + 512]
                mrow, vrow, trow = srow(2), srow(0), srow(1)
                c0 = nc.vector.tensor_scalar(out=mrow, in0=psum.bank(bs)[0:1, :],
                                             scalar1=1.0 / D, scalar2=None,
                                             op0=OP.mult)
                psum.consumed(bs, c0)
                c1 = nc.vector.tensor_scalar(out=vrow, in0=psum.bank(bq)[0:1, :],
                                             scalar1=1.0 / D, scalar2=None,
                                             op0=OP.mult)
                psum.consumed(bq, c1)
                nc.vector.drain()
                nc.vector.tensor_tensor(trow, mrow, mrow, op=OP.mult)
                nc.vector.drain()
                nc.vector.tensor_tensor(vrow, vrow, trow, op=OP.subtract)
                nc.vector.drain()
                nc.vector.tensor_scalar(out=vrow, in0=vrow, scalar1=EPS,
                                        scalar2=None, op0=OP.add)
                p.vmark(sV)
                nc.scalar.wait_ge(sV.h, sV.n)
                nc.scalar.activation(out=trow, in_=vrow, func=AF.Sqrt)
                p.smark(sS)
                nc.vector.wait_ge(sS.h, sS.n)
                nc.vector.reciprocal(srow(3), trow)
                p.vmark(sV)
                nc.tensor.wait_ge(sV.h, sV.n)
                bcast_row(srow(2), BC0[:], 512)
                bcast_row(srow(3), BC1[:], 512)
                nc.vector.drain()
                for dch in range(DCH):
                    sl = src[:, dch * T + tc * 512: dch * T + tc * 512 + 512]
                    dl = dst[:, dch * T + tc * 512: dch * T + tc * 512 + 512]
                    nc.vector.tensor_tensor(dl, sl, BC0[:], op=OP.subtract)
                    nc.vector.drain()
                    nc.vector.tensor_tensor(dl, dl, BC1[:], op=OP.mult)
                    nc.vector.drain()
                    if gb is not None:
                        g, bb = gb
                        nc.vector.tensor_scalar(out=dl, in0=dl,
                                                scalar1=g[:, dch:dch + 1],
                                                scalar2=bb[:, dch:dch + 1],
                                                op0=OP.mult, op1=OP.add)
                        nc.vector.drain()
                p.vmark(sV)

        if _EMIT_UPTO >= 1:
            with ExitStack() as ph:
                emit_layernorm(BUFA, BUFB, ph)

        if _EMIT_UPTO >= 2:
            with ExitStack() as ph:
                XR = p.sb(ph, "XR", [128, DCH * 512])
                RR = p.sb(ph, "RR", [128, DCH * 512])
                K2 = p.sb(ph, "K2", [128, DCH * 512])
                wtf = DblW(p, ph, "WTF", D)
                TMPc = p.sb(ph, "TMPc", [128, 512])
                nc.vector.wait_ge(sL.h, sL.n)
                for (m, o) in ((MUR, OMR), (MUK, OMK)):
                    nc.vector.tensor_scalar(out=o[:], in0=m[:], scalar1=-1.0,
                                            scalar2=1.0, op0=OP.mult, op1=OP.add)
                p.vmark(sV)

                def mix(dst, mu, om, tc):
                    nc.vector.wait_ge(sT.h, sT.n)
                    for dch in range(DCH):
                        base = dch * T + tc * 512
                        dsl = dst[:, dch * 512: dch * 512 + 512]
                        cur = BUFB[:, base: base + 512]
                        nc.vector.drain()
                        nc.vector.tensor_scalar(out=TMPc[:], in0=cur,
                                                scalar1=mu[:, dch: dch + 1],
                                                scalar2=None, op0=OP.mult)
                        nc.vector.drain()
                        if tc == 0:
                            prev = BUFB[:, base: base + 511]
                            nc.vector.scalar_tensor_tensor(
                                out=dst[:, dch * 512 + 1: dch * 512 + 512],
                                in0=prev, scalar=om[:, dch: dch + 1],
                                in1=TMPc[:, 1:512], op0=OP.mult, op1=OP.add)
                            nc.vector.tensor_copy(dst[:, dch * 512: dch * 512 + 1],
                                                  TMPc[:, 0:1])
                        else:
                            prev = BUFB[:, base - 1: base + 511]
                            nc.vector.scalar_tensor_tensor(
                                out=dsl, in0=prev, scalar=om[:, dch: dch + 1],
                                in1=TMPc[:], op0=OP.mult, op1=OP.add)
                    p.vmark(sV)

                def stream_mm(wdram, rhs_buf, act_out):
                    banks = [psum.acquire() for _ in range(8)]
                    wtf.issue(wdram.ap()[0:128, :])
                    for dch in range(DCH):
                        wbuf = wtf.consume()
                        for j in range(8):
                            nc.tensor.matmul(
                                psum.bank(banks[j])[:, 0:512],
                                wbuf[:, j * 128: j * 128 + 128],
                                rhs_buf[:, dch * 512: dch * 512 + 512],
                                start=(dch == 0), stop=(dch == DCH - 1))
                        p.tmark(sT)
                        wtf.done()
                        if dch < DCH - 1:
                            wtf.issue(wdram.ap()[(dch + 1) * 128:(dch + 2) * 128, :])
                    for j in range(8):
                        act_out(j, banks[j])

                for tc in range(4):
                    mix(XR, MUR, OMR, tc)
                    nc.tensor.wait_ge(sV.h, sV.n)

                    def act_r(j, b):
                        nc.scalar.wait_ge(sT.h, sT.n)
                        i = nc.scalar.activation(out=RR[:, j * 512: j * 512 + 512],
                                                 in_=psum.bank(b)[:, 0:512],
                                                 func=AF.Sigmoid)
                        psum.consumed(b, i)
                    stream_mm(w0r, XR, act_r)
                    p.smark(sS)

                    mix(XR, MUK, OMK, tc)
                    nc.tensor.wait_ge(sV.h, sV.n)

                    def act_k(j, b):
                        nc.scalar.wait_ge(sT.h, sT.n)
                        sl = K2[:, j * 512: j * 512 + 512]
                        nc.scalar.activation(out=sl, in_=psum.bank(b)[:, 0:512],
                                             func=AF.Relu)
                        nc.scalar.drain()
                        i = nc.scalar.activation(out=sl, in_=sl, func=AF.Square)
                        psum.consumed(b, i)
                    stream_mm(w0k, XR, act_k)
                    p.smark(sS)
                    nc.tensor.wait_ge(sS.h, sS.n)

                    def act_v(j, b):
                        nc.vector.wait_ge(sT.h, sT.n)
                        xsl = BUFA[:, j * T + tc * 512: j * T + tc * 512 + 512]
                        rsl = RR[:, j * 512: j * 512 + 512]
                        nc.vector.drain()
                        nc.vector.tensor_tensor(TMPc[:], psum.bank(b)[:, 0:512],
                                                rsl, op=OP.mult)
                        nc.vector.drain()
                        i = nc.vector.tensor_tensor(xsl, xsl, TMPc[:], op=OP.add)
                        psum.consumed(b, i)
                    stream_mm(w0v, K2, act_v)
                    p.vmark(sV)

        if _EMIT_UPTO >= 3:
            with ExitStack() as ph:
                WQB = p.sb(ph, "WQB", [128, D])
                WOUT = p.sb(ph, "WOUT", [128, D])
                QP = p.sb(ph, "QP", [128, T])
                KP = p.sb(ph, "KP", [128, T])
                VXP = p.sb(ph, "VXP", [128, T])
                PBUF = p.sb(ph, "PBUF", [128, 2 * 8 * 512])
                OUTP = p.sb(ph, "OUTP", [128, T])
                RBC = p.sb(ph, "RBC", [128, 256])
                DROW = p.sb(ph, "DROW", [2, 256])
                sP = p.sem("pairw")

                nc.vector.wait_ge(sL.h, sL.n)
                nc.vector.wait_ge(sV.h, sV.n)
                for dch in range(DCH):
                    nc.vector.drain()
                    xs = BUFA[:, dch * T:(dch + 1) * T]
                    nc.vector.tensor_scalar(out=xs, in0=xs,
                                            scalar1=BOUT[:, dch:dch + 1],
                                            scalar2=None, op0=OP.add)
                p.vmark(sV)

                for pr in range(8):
                    nc.sync.wait_ge(sT.h, sT.n)
                    nc.sync.dma_start(out=WOUT[:], in_=w0outp.ap()[pr]
                                      ).then_inc(sP.h, 16); sP.n += 16
                    for blk, DEST in ((0, QP), (1, KP), (2, VXP)):
                        nc.sync.wait_ge(sT.h, sT.n)
                        nc.sync.dma_start(
                            out=WQB[:].rearrange("p (c j) -> p c j", c=8),
                            in_=w0qkvp.ap()[pr, :, blk * 128:(blk + 1) * 128]
                                .rearrange("(c p) j -> p c j", p=128),
                        ).then_inc(sP.h, 16); sP.n += 16
                        nc.tensor.wait_ge(sP.h, sP.n)
                        nc.tensor.wait_ge(sV.h, sV.n)
                        for tc in range(4):
                            b = psum.acquire()
                            for dch in range(DCH):
                                mm = nc.tensor.matmul(
                                    psum.bank(b)[:, 0:512],
                                    WQB[:, dch * 128:(dch + 1) * 128],
                                    BUFB[:, dch * T + tc * 512: dch * T + tc * 512 + 512],
                                    start=(dch == 0), stop=(dch == DCH - 1))
                            mm.then_inc(sT.h, 1); sT.n += 1
                            nc.scalar.wait_ge(sT.h, sT.n)
                            i = nc.scalar.activation(
                                out=DEST[:, tc * 512: tc * 512 + 512],
                                in_=psum.bank(b)[:, 0:512], func=AF.Identity,
                                bias=BQKV[:, blk * 8 + pr: blk * 8 + pr + 1],
                                scale=1.0)
                            psum.consumed(b, i)
                    p.smark(sS)
                    nc.tensor.wait_ge(sS.h, sS.n)
                    for k in range(16):
                        b = psum.acquire()
                        nc.tensor.transpose(psum.bank(b)[:, 0:128],
                                            VXP[:, k * 128:(k + 1) * 128],
                                            IDENT[:]).then_inc(sT.h, 1)
                        sT.n += 1
                        nc.vector.wait_ge(sT.h, sT.n)
                        cp = nc.vector.tensor_copy(VXP[:, k * 128:(k + 1) * 128],
                                                   psum.bank(b)[:, 0:128])
                        psum.consumed(b, cp)
                    p.vmark(sV)
                    nc.tensor.wait_ge(sV.h, sV.n)

                    for qc in range(8):
                        nst = 2 * (qc + 1)
                        for stp in range(nst // 2):
                            bq = [psum.acquire(), psum.acquire()]
                            for half in range(2):
                                st = stp * 2 + half
                                for hh in range(2):
                                    mm = nc.tensor.matmul(
                                        psum.bank(bq[hh])[:, half * 256:(half + 1) * 256],
                                        KP[64 * hh:64 * hh + 64, st * 128:(st + 1) * 128],
                                        QP[64 * hh:64 * hh + 64, qc * 256:(qc + 1) * 256],
                                        start=True, stop=True,
                                        tile_position=(64 * hh, 0))
                            mm.then_inc(sT.h, 1); sT.n += 1
                            nc.scalar.wait_ge(sT.h, sT.n)
                            for hh in range(2):
                                slot = PBUF[:, (hh * 8 + stp) * 512:
                                            (hh * 8 + stp) * 512 + 512]
                                i = nc.scalar.activation(out=slot,
                                                         in_=psum.bank(bq[hh])[:, 0:512],
                                                         func=AF.Exp, scale=0.125)
                                psum.consumed(bq[hh], i)
                        p.smark(sS)
                        nc.vector.wait_ge(sS.h, sS.n)
                        for hh in range(2):
                            slot = PBUF[:, (hh * 8 + nst // 2 - 1) * 512:
                                        (hh * 8 + nst // 2 - 1) * 512 + 512]
                            nc.vector.drain()
                            nc.vector.tensor_tensor(slot, slot, MKC[:], op=OP.mult)
                        p.vmark(sV)
                        nc.tensor.wait_ge(sV.h, sV.n)
                        bav = [psum.reserve(), psum.reserve()]
                        bden = psum.reserve()
                        for st in range(nst):
                            stp, half = st // 2, st % 2
                            for hh in range(2):
                                ps = PBUF[:, (hh * 8 + stp) * 512 + half * 256:
                                          (hh * 8 + stp) * 512 + half * 256 + 256]
                                mmd = nc.tensor.matmul(
                                    psum.bank(bden)[0:2, 0:256],
                                    ONES2[:, hh * 2:hh * 2 + 2], ps,
                                    start=(st == 0 and hh == 0),
                                    stop=(st == nst - 1 and hh == 1))
                                nc.tensor.matmul(
                                    psum.bank(bav[hh])[64 * hh:64 * hh + 64, 0:256],
                                    VXP[:, st * 128 + 64 * hh: st * 128 + 64 * hh + 64],
                                    ps, start=(st == 0), stop=(st == nst - 1),
                                    tile_position=(0, 64 * hh))
                        p.tmark(sT)
                        nc.vector.wait_ge(sT.h, sT.n)
                        rc = nc.vector.reciprocal(DROW[:], psum.bank(bden)[0:2, 0:256])
                        psum.consumed(bden, rc)
                        p.vmark(sV)
                        nc.tensor.wait_ge(sV.h, sV.n)
                        brb = psum.acquire()
                        nc.tensor.matmul(psum.bank(brb)[:, 0:256], SEL2T[:], DROW[:],
                                         start=True, stop=True).then_inc(sT.h, 1)
                        sT.n += 1
                        nc.vector.wait_ge(sT.h, sT.n)
                        cp = nc.vector.tensor_copy(RBC[:], psum.bank(brb)[:, 0:256])
                        psum.consumed(brb, cp)
                        nc.vector.drain()
                        for hh in range(2):
                            i = nc.vector.tensor_tensor(
                                OUTP[64 * hh:64 * hh + 64, qc * 256:(qc + 1) * 256],
                                psum.bank(bav[hh])[64 * hh:64 * hh + 64, 0:256],
                                RBC[64 * hh:64 * hh + 64, :], op=OP.mult)
                            psum.consumed(bav[hh], i)
                        p.vmark(sV)
                        psum.release(bav[0]); psum.release(bav[1])
                        psum.release(bden)

                    nc.tensor.wait_ge(sV.h, sV.n)
                    for j in range(DCH):
                        for tc in range(4):
                            b = psum.acquire()
                            nc.tensor.matmul(psum.bank(b)[:, 0:512],
                                             WOUT[:, j * 128:(j + 1) * 128],
                                             OUTP[:, tc * 512: tc * 512 + 512],
                                             start=True, stop=True).then_inc(sT.h, 1)
                            sT.n += 1
                            nc.vector.wait_ge(sT.h, sT.n)
                            xsl = BUFA[:, j * T + tc * 512: j * T + tc * 512 + 512]
                            nc.vector.drain()
                            i = nc.vector.tensor_tensor(xsl, xsl,
                                                        psum.bank(b)[:, 0:512],
                                                        op=OP.add)
                            psum.consumed(b, i)
                    p.vmark(sV)

        sX = p.sem("xsp")
        sCC = p.sem("cc")
        LLOG = p.sb(p.stack, "LLOG", [16, T])
        LN1G = p.sb(p.stack, "LN1G", [128, 8])
        LN1B = p.sb(p.stack, "LN1B", [128, 8])
        LN2G = p.sb(p.stack, "LN2G", [128, 8])
        LN2B = p.sb(p.stack, "LN2B", [128, 8])
        SELH8 = p.sb(p.stack, "SELH8", [128, 32])
        SEL8D = p.sb(p.stack, "SEL8D", [8, 2])
        SEL2TO8 = p.sb(p.stack, "SEL2TO8", [2, 8])
        SELF8 = p.sb(p.stack, "SELF8", [8, FUT * 128])
        sL2 = p.sem("ld2")
        for (dst, srcap) in [(LN1G, ln1g.ap()), (LN1B, ln1b.ap()),
                             (LN2G, ln2g.ap()), (LN2B, ln2b.ap()),
                             (SELH8, selh8.ap()), (SEL8D, sel8d.ap()),
                             (SEL2TO8, sel2to8.ap()), (SELF8, self8.ap())]:
            nc.scalar.dma_start(out=dst[:], in_=srcap).then_inc(sL2.h, 16)
            sL2.n += 16
        if _EMIT_UPTO >= 4:
            with ExitStack() as ph:
                emit_layernorm(BUFA, BUFB, ph)
            with ExitStack() as ph:
                nc.tensor.wait_ge(sV.h, sV.n)
                nc.tensor.wait_ge(sL.h, sL.n)
                for tc in range(4):
                    b = psum.acquire()
                    for dch in range(DCH):
                        mm = nc.tensor.matmul(psum.bank(b)[0:16, 0:512],
                                              GWT[:, dch * 16:(dch + 1) * 16],
                                              BUFB[:, dch * T + tc * 512:
                                                   dch * T + tc * 512 + 512],
                                              start=(dch == 0), stop=(dch == DCH - 1))
                    mm.then_inc(sT.h, 1); sT.n += 1
                    nc.vector.wait_ge(sT.h, sT.n)
                    cp = nc.vector.tensor_copy(LLOG[:, tc * 512: tc * 512 + 512],
                                               psum.bank(b)[0:16, 0:512])
                    psum.consumed(b, cp)
                p.vmark(sV)
                nc.sync.wait_ge(sV.h, sV.n)
                nc.sync.dma_start(
                    out=xg_in.ap().rearrange("(c p) t -> p c t", p=128),
                    in_=BUFB[:].rearrange("p (c t) -> p c t", c=DCH),
                ).then_inc(sX.h, 16); sX.n += 16
                nc.sync.dma_start(out=sc_x.ap(), in_=BUFA[:]
                                  ).then_inc(sX.h, 16); sX.n += 16
                nc.gpsimd.wait_ge(sX.h, sX.n)
                nc.gpsimd.collective_compute(
                    "AllGather", OP.bypass,
                    replica_groups=[list(range(NCORE))],
                    ins=[xg_in.ap().opt()], outs=[xg_all.ap().opt()],
                ).then_inc(sCC.h, 1); sCC.n += 1
                XTT = p.sb(ph, "XTT", [128, D])
                sXT = p.sem("xtew")
                for k in range(16):
                    nc.vector.wait_ge(sXT.h, sXT.n)
                    for dch in range(DCH):
                        b = psum.acquire()
                        nc.tensor.transpose(psum.bank(b)[:, 0:128],
                                            BUFB[:, dch * T + k * 128:
                                                 dch * T + k * 128 + 128],
                                            IDENT[:]).then_inc(sT.h, 1)
                        sT.n += 1
                        nc.vector.wait_ge(sT.h, sT.n)
                        cp = nc.vector.tensor_copy(
                            XTT[:, dch * 128:(dch + 1) * 128],
                            psum.bank(b)[:, 0:128])
                        psum.consumed(b, cp)
                    p.vmark(sV)
                    nc.sync.wait_ge(sV.h, sV.n)
                    nc.sync.dma_start(out=scxt.ap()[k * 128:(k + 1) * 128, :],
                                      in_=XTT[:]).then_inc(sXT.h, 16)
                    sXT.n += 16

        if _EMIT_UPTO >= 5:
            with ExitStack() as ph:
                for eng in (nc.vector, nc.scalar, nc.tensor):
                    eng.wait_ge(sX.h, sX.n)
                    eng.wait_ge(sXT.h, sXT.n)
                    eng.wait_ge(sL2.h, sL2.n)
                emit_layernorm(BUFB, BUFA, ph, gb=(LN1G, LN1B))
            with ExitStack() as ph:
                nc.sync.wait_ge(sV.h, sV.n)
                nc.sync.wait_ge(sS.h, sS.n)
                nc.sync.wait_ge(sT.h, sT.n)
                wqt = DblW(p, ph, "WQT", D)
                QSB = p.sb(ph, "QSB", [128, 512])
                TSB = p.sb(ph, "TSB", [128, 512])
                sQW = p.sem("qkvw")
                sQD = p.sem("qkvd")
                wqt.issue(w1qkvf.ap()[:, 0:128]
                          .rearrange("(c p) j -> p c j", p=128), rearr_c=8)
                for jj in range(32):
                    stream = jj // 8; pr = jj % 8
                    WQT = wqt.consume()
                    nc.tensor.wait_ge(sV.h, sV.n)
                    for tc in range(4):
                        b = psum.acquire()
                        for dch in range(DCH):
                            mm = nc.tensor.matmul(
                                psum.bank(b)[:, 0:512], WQT[:, dch * 128:(dch + 1) * 128],
                                BUFA[:, dch * T + tc * 512: dch * T + tc * 512 + 512],
                                start=(dch == 0), stop=(dch == DCH - 1))
                        mm.then_inc(sT.h, 1); sT.n += 1
                        if tc == 3:
                            wqt.done()
                            if jj < 31:
                                wqt.issue(w1qkvf.ap()[:, (jj + 1) * 128:(jj + 2) * 128]
                                          .rearrange("(c p) j -> p c j", p=128),
                                          rearr_c=8)
                        nc.vector.wait_ge(sT.h, sT.n)
                        nc.vector.wait_ge(sQD.h, sQD.n)
                        cp = nc.vector.tensor_copy(QSB[:], psum.bank(b)[:, 0:512])
                        psum.consumed(b, cp)
                        p.vmark(sV)
                        if stream != 2:
                            dstd = (scq, sck, None, scf)[stream]
                            nc.sync.wait_ge(sV.h, sV.n)
                            nc.sync.dma_start(
                                out=dstd.ap()[pr * 128:(pr + 1) * 128,
                                              tc * 512:(tc + 1) * 512],
                                in_=QSB[:]).then_inc(sQD.h, 16); sQD.n += 16
                        else:
                            nc.tensor.wait_ge(sV.h, sV.n)
                            for kk in range(4):
                                b2 = psum.acquire()
                                nc.tensor.transpose(psum.bank(b2)[:, 0:128],
                                                    QSB[:, kk * 128:(kk + 1) * 128],
                                                    IDENT[:]).then_inc(sT.h, 1)
                                sT.n += 1
                                nc.vector.wait_ge(sT.h, sT.n)
                                cp2 = nc.vector.tensor_copy(
                                    TSB[:, kk * 128:(kk + 1) * 128],
                                    psum.bank(b2)[:, 0:128])
                                psum.consumed(b2, cp2)
                            p.vmark(sV)
                            nc.sync.wait_ge(sV.h, sV.n)
                            nc.sync.dma_start(
                                out=scvt.ap()[tc * 512:(tc + 1) * 512,
                                              pr * 128:(pr + 1) * 128]
                                    .rearrange("(k p) d -> p k d", p=128),
                                in_=TSB[:].rearrange("p (k d) -> p k d", k=4),
                            ).then_inc(sQD.h, 16); sQD.n += 16
            with ExitStack() as ph:
                sQW = p.sem("qkvw"); sQD = p.sem("qkvd")
                for eng in (nc.sync, nc.vector, nc.scalar, nc.tensor):
                    eng.wait_ge(sQW.h, sQW.n)
                    eng.wait_ge(sQD.h, sQD.n)
                FSB = p.sb(ph, "FSB", [128, T])
                WCV2 = p.sb(ph, "WCV2", [128, 64])
                wkt = DblW(p, ph, "WKT", D)
                KSB = p.sb(ph, "KSB", [128, 512])
                sF = p.sem("fld")
                sKD = p.sem("kfd")
                UPF = BUFA
                for fi in range(FUT):
                    nc.sync.wait_ge(sT.h, sT.n)
                    nc.sync.dma_start(out=WCV2[0:64, :], in_=w1conv.ap()[fi]
                                      ).then_inc(sF.h, 16); sF.n += 16
                    nc.sync.dma_start(out=WCV2[64:128, :], in_=w1conv.ap()[fi]
                                      ).then_inc(sF.h, 16); sF.n += 16
                    for hp in range(8):
                        nc.sync.wait_ge(sT.h, sT.n)
                        nc.sync.dma_start(out=FSB[:],
                                          in_=scf.ap()[hp * 128:(hp + 1) * 128, :]
                                          ).then_inc(sF.h, 16); sF.n += 16
                        nc.tensor.wait_ge(sF.h, sF.n)
                        nc.tensor.wait_ge(sQD.h, sQD.n)
                        for tc in range(4):
                            b0 = psum.acquire(); b1 = psum.acquire()
                            nc.tensor.matmul(psum.bank(b0)[0:64, 0:512],
                                             WCV2[0:64, :],
                                             FSB[0:64, tc * 512: tc * 512 + 512],
                                             start=True, stop=True)
                            mm1 = nc.tensor.matmul(psum.bank(b1)[64:128, 0:512],
                                                   WCV2[64:128, :],
                                                   FSB[64:128, tc * 512: tc * 512 + 512],
                                                   start=True, stop=True,
                                                   tile_position=(64, 64))
                            mm1.then_inc(sT.h, 1); sT.n += 1
                            nc.vector.wait_ge(sT.h, sT.n)
                            dsl0 = UPF[0:64, hp * T + tc * 512: hp * T + tc * 512 + 512]
                            dsl1 = UPF[64:128, hp * T + tc * 512: hp * T + tc * 512 + 512]
                            c0 = nc.vector.tensor_copy(dsl0, psum.bank(b0)[0:64, 0:512])
                            psum.consumed(b0, c0)
                            c1 = nc.vector.tensor_copy(dsl1, psum.bank(b1)[64:128, 0:512])
                            psum.consumed(b1, c1)
                    p.vmark(sV)
                    nc.tensor.wait_ge(sV.h, sV.n)
                    kvjobs = [(col0, tc, jg) for col0 in (D, 2 * D)
                              for tc in range(4) for jg in range(8)]
                    wkt.issue(w1qkvf.ap()[:, D: D + 128]
                              .rearrange("(c p) j -> p c j", p=128), rearr_c=8)
                    for ji, (col0, tc, jg) in enumerate(kvjobs):
                        if True:
                            if True:
                                WKT = wkt.consume()
                                b = psum.acquire()
                                for dch in range(DCH):
                                    mm = nc.tensor.matmul(
                                        psum.bank(b)[:, 0:512],
                                        WKT[:, dch * 128:(dch + 1) * 128],
                                        UPF[:, dch * T + tc * 512:
                                            dch * T + tc * 512 + 512],
                                        start=(dch == 0), stop=(dch == DCH - 1))
                                mm.then_inc(sT.h, 1); sT.n += 1
                                wkt.done()
                                if ji < len(kvjobs) - 1:
                                    c2, _, j2 = kvjobs[ji + 1]
                                    wkt.issue(w1qkvf.ap()[:, c2 + j2 * 128:
                                                          c2 + (j2 + 1) * 128]
                                              .rearrange("(c p) j -> p c j", p=128),
                                              rearr_c=8)
                                dstd = sckf if col0 == D else scvf
                                nc.vector.wait_ge(sT.h, sT.n)
                                nc.vector.wait_ge(sKD.h, sKD.n)
                                cp = nc.vector.tensor_copy(KSB[:],
                                                           psum.bank(b)[:, 0:512])
                                psum.consumed(b, cp)
                                p.vmark(sV)
                                nc.sync.wait_ge(sV.h, sV.n)
                                nc.sync.dma_start(
                                    out=dstd.ap()[fi, jg * 128:(jg + 1) * 128,
                                                  tc * 512:(tc + 1) * 512],
                                    in_=KSB[:]).then_inc(sKD.h, 16); sKD.n += 16

        if _EMIT_UPTO >= 5:
            with ExitStack() as ph:
                sF = p.sem("fld"); sKD = p.sem("kfd")
                for eng in (nc.sync, nc.vector, nc.scalar, nc.tensor):
                    eng.wait_ge(sF.h, sF.n)
                    eng.wait_ge(sKD.h, sKD.n)
                OQ, OKk, OKV, OPB, OOU = 0, T, 2 * T, 3 * T, 7 * T
                QPs = BUFA[:, OQ:OQ + T]
                KPs = BUFA[:, OKk:OKk + T]
                KVF = BUFA[:, OKV:OKV + T]
                OUTP = BUFA[:, OOU:OOU + T]
                pbslot = lambda hh, stp: BUFA[:, OPB + (hh * 8 + stp) * 512:
                                              OPB + (hh * 8 + stp) * 512 + 512]
                FAW = p.sb(ph, "FAW", [8, T])
                RBCW = p.sb(ph, "RBCW", [128, 512])
                DROW = p.sb(ph, "DROW", [2, 256])
                RB8 = p.sb(ph, "RB8", [8, 256])
                WPJ = p.sb(ph, "WPJ", [128, D])
                sPR = p.sem("prld")
                for pr in range(8):
                    nc.sync.wait_ge(sT.h, sT.n)
                    nc.sync.wait_ge(sV.h, sV.n)
                    nc.sync.dma_start(out=QPs, in_=scq.ap()[pr * 128:(pr + 1) * 128, :]
                                      ).then_inc(sPR.h, 16); sPR.n += 16
                    nc.sync.dma_start(out=KPs, in_=sck.ap()[pr * 128:(pr + 1) * 128, :]
                                      ).then_inc(sPR.h, 16); sPR.n += 16
                    nc.sync.dma_start(out=KVF, in_=sckf.ap()[0, pr * 128:(pr + 1) * 128, :]
                                      ).then_inc(sPR.h, 16); sPR.n += 16
                    bfa = [psum.reserve() for _ in range(4)]
                    for fi in range(FUT):
                        if fi > 0:
                            nc.sync.wait_ge(sV.h, sV.n)
                            nc.sync.dma_start(
                                out=KVF, in_=sckf.ap()[fi, pr * 128:(pr + 1) * 128, :]
                            ).then_inc(sPR.h, 16); sPR.n += 16
                        nc.vector.wait_ge(sPR.h, sPR.n)
                        nc.vector.wait_ge(sT.h, sT.n)
                        nc.vector.drain()
                        nc.vector.tensor_tensor(OUTP, QPs, KVF, op=OP.mult)
                        p.vmark(sV)
                        nc.tensor.wait_ge(sV.h, sV.n)
                        for tc in range(4):
                            nc.tensor.matmul(
                                psum.bank(bfa[tc])[0:8, 0:512],
                                SELH8[:, fi * 8:(fi + 1) * 8],
                                OUTP[:, tc * 512: tc * 512 + 512],
                                start=(fi == 0), stop=(fi == FUT - 1)
                            ).then_inc(sT.h, 1); sT.n += 1
                    nc.scalar.wait_ge(sT.h, sT.n)
                    for tc in range(4):
                        i = nc.scalar.activation(out=FAW[:, tc * 512: tc * 512 + 512],
                                                 in_=psum.bank(bfa[tc])[0:8, 0:512],
                                                 func=AF.Exp)
                        psum.consumed(bfa[tc], i)
                        psum.release(bfa[tc])
                    p.smark(sS)
                    nc.sync.wait_ge(sV.h, sV.n)
                    nc.sync.dma_start(
                        out=KVF.rearrange("p (k d) -> p k d", k=16),
                        in_=scvt.ap()[:, pr * 128:(pr + 1) * 128]
                            .rearrange("(k p) d -> p k d", p=128),
                    ).then_inc(sPR.h, 16); sPR.n += 16
                    nc.tensor.wait_ge(sPR.h, sPR.n)
                    nc.vector.wait_ge(sS.h, sS.n)
                    for qc in range(8):
                        nst = 2 * (qc + 1)
                        for stp in range(nst // 2):
                            bq = [psum.acquire(), psum.acquire()]
                            for half in range(2):
                                st = stp * 2 + half
                                for hh in range(2):
                                    mm = nc.tensor.matmul(
                                        psum.bank(bq[hh])[:, half * 256:(half + 1) * 256],
                                        KPs[64 * hh:64 * hh + 64, st * 128:(st + 1) * 128],
                                        QPs[64 * hh:64 * hh + 64, qc * 256:(qc + 1) * 256],
                                        start=True, stop=True,
                                        tile_position=(64 * hh, 0))
                            mm.then_inc(sT.h, 1); sT.n += 1
                            nc.scalar.wait_ge(sT.h, sT.n)
                            for hh in range(2):
                                i = nc.scalar.activation(out=pbslot(hh, stp),
                                                         in_=psum.bank(bq[hh])[:, 0:512],
                                                         func=AF.Exp, scale=0.125)
                                psum.consumed(bq[hh], i)
                        p.smark(sS)
                        nc.vector.wait_ge(sS.h, sS.n)
                        for hh in range(2):
                            slot = pbslot(hh, nst // 2 - 1)
                            nc.vector.drain()
                            nc.vector.tensor_tensor(slot, slot, MKC[:], op=OP.mult)
                        p.vmark(sV)
                        nc.tensor.wait_ge(sV.h, sV.n)
                        bav = [psum.reserve(), psum.reserve()]
                        bden = psum.reserve()
                        for st in range(nst):
                            stp, half = st // 2, st % 2
                            for hh in range(2):
                                ps = BUFA[:, OPB + (hh * 8 + stp) * 512 + half * 256:
                                          OPB + (hh * 8 + stp) * 512 + half * 256 + 256]
                                nc.tensor.matmul(
                                    psum.bank(bden)[0:2, 0:256],
                                    ONES2[:, hh * 2:hh * 2 + 2], ps,
                                    start=(st == 0 and hh == 0), stop=False)
                                nc.tensor.matmul(
                                    psum.bank(bav[hh])[64 * hh:64 * hh + 64, 0:256],
                                    KVF[:, st * 128 + 64 * hh: st * 128 + 64 * hh + 64],
                                    ps, start=(st == 0), stop=(st == nst - 1),
                                    tile_position=(0, 64 * hh))
                        nc.tensor.matmul(psum.bank(bden)[0:2, 0:256], SEL8D[:],
                                         FAW[:, qc * 256:(qc + 1) * 256],
                                         start=False, stop=True)
                        p.tmark(sT)
                        nc.vector.wait_ge(sT.h, sT.n)
                        rc = nc.vector.reciprocal(DROW[:], psum.bank(bden)[0:2, 0:256])
                        psum.consumed(bden, rc)
                        p.vmark(sV)
                        nc.tensor.wait_ge(sV.h, sV.n)
                        brb = psum.acquire()
                        nc.tensor.matmul(psum.bank(brb)[:, 0:256], SEL2T[:], DROW[:],
                                         start=True, stop=True)
                        br8 = psum.acquire()
                        nc.tensor.matmul(psum.bank(br8)[0:8, 0:256], SEL2TO8[:], DROW[:],
                                         start=True, stop=True).then_inc(sT.h, 1)
                        sT.n += 1
                        nc.vector.wait_ge(sT.h, sT.n)
                        cp = nc.vector.tensor_copy(RBCW[:, 0:256], psum.bank(brb)[:, 0:256])
                        psum.consumed(brb, cp)
                        cp8 = nc.vector.tensor_copy(RB8[:], psum.bank(br8)[0:8, 0:256])
                        psum.consumed(br8, cp8)
                        nc.vector.drain()
                        for hh in range(2):
                            i = nc.vector.tensor_tensor(
                                OUTP[64 * hh:64 * hh + 64, qc * 256:(qc + 1) * 256],
                                psum.bank(bav[hh])[64 * hh:64 * hh + 64, 0:256],
                                RBCW[64 * hh:64 * hh + 64, 0:256], op=OP.mult)
                            psum.consumed(bav[hh], i)
                        nc.vector.drain()
                        fsl = FAW[:, qc * 256:(qc + 1) * 256]
                        nc.vector.tensor_tensor(fsl, fsl, RB8[:], op=OP.mult)
                        p.vmark(sV)
                        psum.release(bav[0]); psum.release(bav[1])
                        psum.release(bden)
                    for fi in range(FUT):
                        nc.sync.wait_ge(sT.h, sT.n)
                        nc.sync.wait_ge(sV.h, sV.n)
                        nc.sync.dma_start(
                            out=KVF, in_=scvf.ap()[fi, pr * 128:(pr + 1) * 128, :]
                        ).then_inc(sPR.h, 16); sPR.n += 16
                        nc.tensor.wait_ge(sPR.h, sPR.n)
                        nc.tensor.wait_ge(sV.h, sV.n)
                        for qc in range(8):
                            bwb = psum.acquire()
                            nc.tensor.matmul(psum.bank(bwb)[:, 0:256],
                                             SELF8[:, fi * 128:(fi + 1) * 128],
                                             FAW[:, qc * 256:(qc + 1) * 256],
                                             start=True, stop=True).then_inc(sT.h, 1)
                            sT.n += 1
                            nc.vector.wait_ge(sT.h, sT.n)
                            cw = nc.vector.tensor_copy(RBCW[:, 256:512],
                                                       psum.bank(bwb)[:, 0:256])
                            psum.consumed(bwb, cw)
                            nc.vector.drain()
                            nc.vector.tensor_tensor(RBCW[:, 0:256], RBCW[:, 256:512],
                                                    KVF[:, qc * 256:(qc + 1) * 256],
                                                    op=OP.mult)
                            nc.vector.drain()
                            osl = OUTP[:, qc * 256:(qc + 1) * 256]
                            nc.vector.tensor_tensor(osl, osl, RBCW[:, 0:256], op=OP.add)
                            nc.vector.drain()
                        p.vmark(sV)
                    nc.sync.wait_ge(sT.h, sT.n)
                    nc.sync.dma_start(out=WPJ[:], in_=w1projp.ap()[pr]
                                      ).then_inc(sPR.h, 16); sPR.n += 16
                    nc.tensor.wait_ge(sPR.h, sPR.n)
                    nc.tensor.wait_ge(sV.h, sV.n)
                    for j in range(DCH):
                        for tc in range(4):
                            b = psum.acquire()
                            nc.tensor.matmul(psum.bank(b)[:, 0:512],
                                             WPJ[:, j * 128:(j + 1) * 128],
                                             OUTP[:, tc * 512: tc * 512 + 512],
                                             start=True, stop=True).then_inc(sT.h, 1)
                            sT.n += 1
                            nc.vector.wait_ge(sT.h, sT.n)
                            asl = BUFB[:, j * T + tc * 512: j * T + tc * 512 + 512]
                            nc.vector.drain()
                            i = nc.vector.tensor_tensor(asl, asl,
                                                        psum.bank(b)[:, 0:512],
                                                        op=OP.add)
                            psum.consumed(b, i)
                    p.vmark(sV)

        if _EMIT_UPTO >= 5:
            with ExitStack() as ph:
                sPR = p.sem("prld")
                for eng in (nc.vector, nc.scalar, nc.tensor, nc.sync):
                    eng.wait_ge(sPR.h, sPR.n)
                emit_layernorm(BUFB, BUFA, ph, gb=(LN2G, LN2B))
            with ExitStack() as ph:
                nc.sync.wait_ge(sV.h, sV.n)
                nc.sync.wait_ge(sS.h, sS.n)
                nc.sync.wait_ge(sT.h, sT.n)
                wf1 = DblW(p, ph, "WF1", D)
                wf2 = DblW(p, ph, "WF2", 2 * D)
                G1 = p.sb(ph, "G1", [128, 8 * 256])
                ACC2 = p.sb(ph, "ACC2", [128, 8 * 256])
                pb2 = [psum.reserve() for _ in range(4)]
                wf2seq = [fg2 * 1024 + f8p2 * 256
                          for _tc in range(8) for fg2 in range(4)
                          for _jh in range(2) for f8p2 in range(4)]
                wf2i = [0]
                wf1.issue(w1ff1.ap()[:, 0:128]
                          .rearrange("(c p) j -> p c j", p=128), rearr_c=8)
                for tc in range(8):
                    for fg in range(4):
                        for f8 in range(8):
                            WFT = wf1.consume()
                            nc.tensor.wait_ge(sV.h, sV.n)
                            b = psum.acquire()
                            for dch in range(DCH):
                                mm = nc.tensor.matmul(
                                    psum.bank(b)[:, 0:256],
                                    WFT[:, dch * 128:(dch + 1) * 128],
                                    BUFA[:, dch * T + tc * 256: dch * T + tc * 256 + 256],
                                    start=(dch == 0), stop=(dch == DCH - 1))
                            mm.then_inc(sT.h, 1); sT.n += 1
                            wf1.done()
                            nf = fg * 8 + f8 + 1
                            if nf < 32:
                                wf1.issue(w1ff1.ap()[:, nf * 128:(nf + 1) * 128]
                                          .rearrange("(c p) j -> p c j", p=128),
                                          rearr_c=8)
                            elif tc < 7:
                                wf1.issue(w1ff1.ap()[:, 0:128]
                                          .rearrange("(c p) j -> p c j", p=128),
                                          rearr_c=8)
                            nc.scalar.wait_ge(sT.h, sT.n)
                            i = nc.scalar.activation(out=G1[:, f8 * 256:(f8 + 1) * 256],
                                                     in_=psum.bank(b)[:, 0:256],
                                                     func=AF.Gelu)
                            psum.consumed(b, i)
                        p.smark(sS)
                        nc.tensor.wait_ge(sS.h, sS.n)
                        for jhalf in range(2):
                            if fg == 0 and jhalf == 0 and tc == 0:
                                r0 = wf2seq[0]
                                wf2.issue(w1ff2.ap()[r0:r0 + 256, :]
                                          .rearrange("(q p) j -> p q j", p=128),
                                          rearr_c=None)
                            for f8p in range(4):
                                WF2T = wf2.consume()
                                for fo in range(2):
                                    f8 = f8p * 2 + fo
                                    for j4 in range(4):
                                        mm = nc.tensor.matmul(
                                            psum.bank(pb2[j4])[:, 0:256],
                                            WF2T[:, fo * 1024 + (jhalf * 4 + j4) * 128:
                                                 fo * 1024 + (jhalf * 4 + j4 + 1) * 128],
                                            G1[:, f8 * 256:(f8 + 1) * 256],
                                            start=(f8 == 0), stop=(f8 == 7))
                                mm.then_inc(sT.h, 1); sT.n += 1
                                wf2.done()
                                wf2i[0] += 1
                                if wf2i[0] < len(wf2seq):
                                    r0 = wf2seq[wf2i[0]]
                                    wf2.issue(w1ff2.ap()[r0:r0 + 256, :]
                                              .rearrange("(q p) j -> p q j", p=128),
                                              rearr_c=None)
                            nc.vector.wait_ge(sT.h, sT.n)
                            for j4 in range(4):
                                j8 = jhalf * 4 + j4
                                asl = ACC2[:, j8 * 256:(j8 + 1) * 256]
                                nc.vector.drain()
                                if fg == 0:
                                    i = nc.vector.tensor_copy(asl,
                                                              psum.bank(pb2[j4])[:, 0:256])
                                else:
                                    i = nc.vector.tensor_tensor(
                                        asl, asl, psum.bank(pb2[j4])[:, 0:256], op=OP.add)
                                psum.consumed(pb2[j4], i)
                            p.vmark(sV)
                            nc.tensor.wait_ge(sV.h, sV.n)
                    for j8 in range(8):
                        asl = BUFB[:, j8 * T + tc * 256: j8 * T + tc * 256 + 256]
                        nc.vector.drain()
                        nc.vector.tensor_tensor(asl, asl, ACC2[:, j8 * 256:(j8 + 1) * 256],
                                                op=OP.add)
                    p.vmark(sV)
                for b in pb2:
                    psum.release(b)

        if _EMIT_UPTO >= 6:
            with ExitStack() as ph:
                sFF = p.sem("ffw")
                for eng in (nc.sync, nc.vector, nc.scalar, nc.tensor):
                    eng.wait_ge(sFF.h, sFF.n)
                nc.sync.wait_ge(sCC.h, sCC.n)
                sSC = p.sem("scat")
                LO = p.sb(ph, "LO", [128, 6])
                THROW = p.sb(ph, "THROW", [1, 128])
                ph6 = ExitStack()
                XGT = p.sb(ph6, "XGT", [128, T])
                GLT = p.sb(ph6, "GLT", [128, T])
                GROW = p.sb(ph6, "GROW", [16, T])
                SELPM = p.sb(ph6, "SELPM", [128, 16])
                SELPMT = p.sb(ph6, "SELPMT", [16, 128])
                sGD = p.sem("gated")
                sGL = p.sem("glw")
                nc.sync.dma_start(out=SELPM[:], in_=selpm.ap()
                                  ).then_inc(sGD.h, 16); sGD.n += 16
                nc.sync.dma_start(out=SELPMT[:], in_=selpmt.ap()
                                  ).then_inc(sGD.h, 16); sGD.n += 16
                for r in range(NCORE):
                    btc = [psum.reserve() for _ in range(4)]
                    for dch in range(DCH):
                        nc.sync.wait_ge(sT.h, sT.n)
                        nc.sync.dma_start(
                            out=XGT[:],
                            in_=xg_all.ap()[r * D + dch * 128:
                                            r * D + (dch + 1) * 128, :],
                        ).then_inc(sGD.h, 16); sGD.n += 16
                        nc.tensor.wait_ge(sGD.h, sGD.n)
                        for tc in range(4):
                            mm = nc.tensor.matmul(
                                psum.bank(btc[tc])[0:16, 0:512],
                                GWT[:, dch * 16:(dch + 1) * 16],
                                XGT[:, tc * 512: tc * 512 + 512],
                                start=(dch == 0), stop=(dch == DCH - 1))
                        mm.then_inc(sT.h, 1); sT.n += 1
                    nc.vector.wait_ge(sT.h, sT.n)
                    nc.vector.wait_ge(sGL.h, sGL.n)
                    for tc in range(4):
                        cp = nc.vector.tensor_copy(GROW[:, tc * 512: tc * 512 + 512],
                                                   psum.bank(btc[tc])[0:16, 0:512])
                        psum.consumed(btc[tc], cp)
                        psum.release(btc[tc])
                    p.vmark(sV)
                    nc.sync.wait_ge(sV.h, sV.n)
                    nc.sync.dma_start(out=GLT[r * 16:(r + 1) * 16, :], in_=GROW[:]
                                      ).then_inc(sGL.h, 16); sGL.n += 16
                CROW = p.sb(ph6, "CROW", [1, 16])
                CCOL = p.sb(ph6, "CCOL", [16, 1])
                CNTB = p.sb(ph6, "CNTB", [128, 1])
                MSK = p.sb(ph6, "MSK", [128, T])
                nc.vector.wait_ge(sGL.h, sGL.n)
                nc.tensor.wait_ge(sGD.h, sGD.n)
                nc.vector.memset(LO[:, 0:1], -64.0)
                nc.vector.memset(LO[:, 1:2], 64.0)
                p.vmark(sV)
                for it in range(52):
                    nc.vector.drain()
                    nc.vector.tensor_tensor(LO[:, 2:3], LO[:, 0:1], LO[:, 1:2],
                                            op=OP.add)
                    nc.vector.drain()
                    nc.vector.tensor_scalar(out=LO[:, 2:3], in0=LO[:, 2:3],
                                            scalar1=0.5, scalar2=None, op0=OP.mult)
                    nc.vector.drain()
                    nc.vector.tensor_scalar(out=MSK[:], in0=GLT[:],
                                            scalar1=LO[:, 2:3], scalar2=None,
                                            op0=OP.is_ge)
                    nc.vector.drain()
                    nc.vector.reduce_sum(CNTB[:], MSK[:], axis=mybir.AxisListType.X)
                    p.vmark(sV)
                    nc.tensor.wait_ge(sV.h, sV.n)
                    b1 = psum.acquire()
                    nc.tensor.matmul(psum.bank(b1)[0:1, 0:16], CNTB[:], SELPM[:],
                                     start=True, stop=True).then_inc(sT.h, 1)
                    sT.n += 1
                    nc.vector.wait_ge(sT.h, sT.n)
                    c1 = nc.vector.tensor_copy(CROW[:], psum.bank(b1)[0:1, 0:16])
                    psum.consumed(b1, c1)
                    p.vmark(sV)
                    nc.tensor.wait_ge(sV.h, sV.n)
                    b2 = psum.acquire()
                    nc.tensor.transpose(psum.bank(b2)[0:16, 0:1], CROW[:],
                                        IDENT[0:1, 0:1]).then_inc(sT.h, 1)
                    sT.n += 1
                    nc.vector.wait_ge(sT.h, sT.n)
                    c2 = nc.vector.tensor_copy(CCOL[:], psum.bank(b2)[0:16, 0:1])
                    psum.consumed(b2, c2)
                    p.vmark(sV)
                    nc.tensor.wait_ge(sV.h, sV.n)
                    b3 = psum.acquire()
                    nc.tensor.matmul(psum.bank(b3)[:, 0:1], SELPMT[:], CCOL[:],
                                     start=True, stop=True).then_inc(sT.h, 1)
                    sT.n += 1
                    nc.vector.wait_ge(sT.h, sT.n)
                    c3 = nc.vector.tensor_copy(CNTB[:], psum.bank(b3)[:, 0:1])
                    psum.consumed(b3, c3)
                    nc.vector.drain()
                    nc.vector.tensor_scalar(out=LO[:, 3:4], in0=CNTB[:],
                                            scalar1=2048.0, scalar2=None,
                                            op0=OP.is_ge)
                    nc.vector.drain()
                    nc.vector.tensor_tensor(LO[:, 4:5], LO[:, 2:3], LO[:, 0:1],
                                            op=OP.subtract)
                    nc.vector.tensor_tensor(LO[:, 5:6], LO[:, 1:2], LO[:, 2:3],
                                            op=OP.subtract)
                    nc.vector.drain()
                    nc.vector.scalar_tensor_tensor(out=LO[:, 0:1], in0=LO[:, 4:5],
                                                   scalar=LO[:, 3:4], in1=LO[:, 0:1],
                                                   op0=OP.mult, op1=OP.add)
                    nc.vector.scalar_tensor_tensor(out=LO[:, 1:2], in0=LO[:, 5:6],
                                                   scalar=LO[:, 3:4], in1=LO[:, 2:3],
                                                   op0=OP.mult, op1=OP.add)
                p.vmark(sV)

                nc.tensor.wait_ge(sV.h, sV.n)
                bt0 = psum.acquire()
                nc.tensor.transpose(psum.bank(bt0)[0:1, 0:128], LO[:, 0:1],
                                    IDENT[:]).then_inc(sT.h, 1); sT.n += 1
                nc.vector.wait_ge(sT.h, sT.n)
                ct0 = nc.vector.tensor_copy(THROW[:], psum.bank(bt0)[0:1, 0:128])
                psum.consumed(bt0, ct0)
                p.vmark(sV)
                ph6.close()
                IOTAC = p.sb(ph, "IOTAC", [128, 384])
                IOTAR = p.sb(ph, "IOTAR", [128, 3])
                ROWS = BUFA[0:1, 0:5 * T]
                ZROW = BUFA[0:1, 5 * T:6 * T]
                POSBC = p.sb(ph, "POSBC", [128, T])
                SIGBC = p.sb(ph, "SIGBC", [128, T])
                POSTX = p.sb(ph, "POSTX", [128, 16])
                OHT = p.sb(ph, "OHT", [128, 2 * T])
                OH1 = BUFA[:, 32 * 384 + 8 * 384: 32 * 384 + 8 * 384 + 384]
                xtk = DblW(p, ph, "XTK", D)
                wug = DblW(p, ph, "WUG", D)
                wdw = DblW(p, ph, "WDW", D)
                DBB = BUFA[:, 32 * 384:32 * 384 + 1024]
                DBROW = BUFA[0:1, 32 * 384 + 1024:32 * 384 + 2048]
                UBF = p.sb(ph, "UBF", [128, 32])
                GBF = p.sb(ph, "GBF", [128, 32])
                EO = BUFA[:, 0:32 * 384]
                XIN = BUFA[:, 32 * 384:32 * 384 + 8 * 384]
                DVN = BUFA[:, 0:3 * 1024]
                TME = BUFA[:, 32 * 384 + 8 * 384: 32 * 384 + 8 * 384 + 384]
                sEW = p.sem("expw")
                sGA = p.sem("gath")
                nc.sync.wait_ge(sV.h, sV.n)
                nc.sync.dma_start(out=IOTAC[:], in_=iotac.ap()
                                  ).then_inc(sEW.h, 16); sEW.n += 16
                nc.sync.dma_start(out=IOTAR[:], in_=iotar.ap()
                                  ).then_inc(sEW.h, 16); sEW.n += 16
                nc.vector.wait_ge(sEW.h, sEW.n)
                rR = lambda i: ROWS[0:1, i * T:(i + 1) * T]
                for e in range(E):
                    nc.sync.wait_ge(sT.h, sT.n)
                    nc.sync.wait_ge(sV.h, sV.n)
                    sLL = p.sem("llrow")
                    nc.sync.dma_start(out=rR(4), in_=LLOG[e:e + 1, :]
                                      ).then_inc(sLL.h, 16); sLL.n += 16
                    nc.vector.wait_ge(sT.h, sT.n)
                    nc.vector.wait_ge(sLL.h, sLL.n)
                    nc.vector.memset(ZROW, 0.0)
                    nc.vector.drain()
                    nc.vector.tensor_scalar(out=rR(0), in0=rR(4),
                                            scalar1=THROW[0:1, e:e + 1],
                                            scalar2=None, op0=OP.is_ge)
                    nc.vector.drain()
                    nc.vector.tensor_tensor_scan(rR(1), rR(0), ZROW, 0.0,
                                                 op0=OP.add, op1=OP.add)
                    nc.vector.drain()
                    nc.vector.tensor_tensor(rR(2), rR(1), rR(0), op=OP.mult)
                    p.vmark(sV)
                    nc.scalar.wait_ge(sV.h, sV.n)
                    nc.scalar.activation(out=rR(3), in_=rR(4), func=AF.Sigmoid)
                    p.smark(sS)
                    nc.tensor.wait_ge(sS.h, sS.n)
                    bcast_row(rR(2), POSBC[:], T)
                    bcast_row(rR(3), SIGBC[:], T)
                    for k in range(16):
                        bx = psum.acquire()
                        nc.tensor.transpose(psum.bank(bx)[:, 0:1],
                                            ROWS[0:1, 2 * T + k * 128: 2 * T + (k + 1) * 128],
                                            IDENT[0:1, 0:1]).then_inc(sT.h, 1)
                        sT.n += 1
                        nc.vector.wait_ge(sT.h, sT.n)
                        cx = nc.vector.tensor_copy(POSTX[:, k:k + 1],
                                                   psum.bank(bx)[:, 0:1])
                        psum.consumed(bx, cx)
                    p.vmark(sV)
                    bg = [psum.reserve() for _ in range(8)]
                    xtk.issue(scxt.ap()[0:128, :])
                    for k in range(16):
                        nc.vector.wait_ge(sT.h, sT.n)
                        nc.vector.wait_ge(sV.h, sV.n)
                        nc.vector.drain()
                        nc.vector.tensor_scalar(out=OH1[:], in0=IOTAC[:],
                                                scalar1=POSTX[:, k:k + 1],
                                                scalar2=None, op0=OP.is_equal)
                        p.vmark(sV)
                        XTEK = xtk.consume()
                        nc.tensor.wait_ge(sV.h, sV.n)
                        for dch in range(DCH):
                            mm = nc.tensor.matmul(
                                psum.bank(bg[dch])[:, 0:384],
                                XTEK[:, dch * 128:(dch + 1) * 128], OH1[:],
                                start=(k == 0), stop=(k == 15))
                        mm.then_inc(sT.h, 1); sT.n += 1
                        xtk.done()
                        if k < 15:
                            xtk.issue(scxt.ap()[(k + 1) * 128:(k + 2) * 128, :])
                    nc.vector.wait_ge(sT.h, sT.n)
                    nc.vector.wait_ge(sSC.h, sSC.n)
                    for dch in range(DCH):
                        cx = nc.vector.tensor_copy(
                            XIN[:, dch * 384:(dch + 1) * 384],
                            psum.bank(bg[dch])[:, 0:384])
                        psum.consumed(bg[dch], cx)
                        psum.release(bg[dch])
                    p.vmark(sV)
                    xin_ready = sV.n
                    nc.sync.wait_ge(sS.h, sS.n)
                    nc.sync.wait_ge(sV.h, sV.n)
                    sUB = p.sem("ubgb")
                    nc.sync.dma_start(out=UBF[:], in_=upbf.ap()[e]
                                      ).then_inc(sUB.h, 16); sUB.n += 16
                    nc.sync.dma_start(out=GBF[:], in_=gbf.ap()[e]
                                      ).then_inc(sUB.h, 16); sUB.n += 16
                    nc.scalar.wait_ge(sUB.h, sUB.n)
                    bdown = [psum.reserve() for _ in range(6)]
                    wug.issue(upwt.ap()[e, :, 0:128]
                              .rearrange("(c p) j -> p c j", p=128), rearr_c=8)
                    wug.issue(gwt2.ap()[e, :, 0:128]
                              .rearrange("(c p) j -> p c j", p=128), rearr_c=8)
                    wdw.issue(dwnt.ap()[e, 0:128, :])
                    for fc in range(32):
                        WUG = wug.consume()
                        nc.tensor.wait_ge(sV.h, xin_ready)
                        bup = psum.acquire()
                        for dch in range(DCH):
                            mm = nc.tensor.matmul(
                                psum.bank(bup)[:, 0:384],
                                WUG[:, dch * 128:(dch + 1) * 128],
                                XIN[:, dch * 384: dch * 384 + 384],
                                start=(dch == 0), stop=(dch == DCH - 1))
                        mm.then_inc(sT.h, 1); sT.n += 1
                        wug.done()
                        if fc < 31:
                            wug.issue(upwt.ap()[e, :, (fc + 1) * 128:(fc + 2) * 128]
                                      .rearrange("(c p) j -> p c j", p=128), rearr_c=8)
                        WUG = wug.consume()
                        bgw = psum.acquire()
                        for dch in range(DCH):
                            mm = nc.tensor.matmul(
                                psum.bank(bgw)[:, 0:384],
                                WUG[:, dch * 128:(dch + 1) * 128],
                                XIN[:, dch * 384: dch * 384 + 384],
                                start=(dch == 0), stop=(dch == DCH - 1))
                        mm.then_inc(sT.h, 1); sT.n += 1
                        wug.done()
                        if fc < 31:
                            wug.issue(gwt2.ap()[e, :, (fc + 1) * 128:(fc + 2) * 128]
                                      .rearrange("(c p) j -> p c j", p=128), rearr_c=8)
                        nc.scalar.wait_ge(sT.h, sT.n)
                        nc.scalar.wait_ge(sSC.h, sSC.n)
                        i1 = nc.scalar.activation(out=TME, in_=psum.bank(bgw)[:, 0:384],
                                                  func=AF.Silu,
                                                  bias=GBF[:, fc:fc + 1], scale=1.0)
                        psum.consumed(bgw, i1)
                        esl = EO[:, fc * 384:(fc + 1) * 384]
                        i2 = nc.scalar.activation(out=esl, in_=psum.bank(bup)[:, 0:384],
                                                  func=AF.Identity,
                                                  bias=UBF[:, fc:fc + 1], scale=1.0)
                        psum.consumed(bup, i2)
                        p.smark(sS)
                        nc.vector.wait_ge(sS.h, sS.n)
                        nc.vector.drain()
                        nc.vector.tensor_tensor(esl, esl, TME, op=OP.mult)
                        p.vmark(sV)
                        DWN = wdw.consume()
                        nc.tensor.wait_ge(sV.h, sV.n)
                        for tt in range(3):
                            for dh in range(2):
                                mm = nc.tensor.matmul(
                                    psum.bank(bdown[tt * 2 + dh])[:, 0:512],
                                    EO[:, fc * 384 + tt * 128: fc * 384 + (tt + 1) * 128],
                                    DWN[:, dh * 512:(dh + 1) * 512],
                                    start=(fc == 0), stop=(fc == 31))
                        mm.then_inc(sT.h, 1); sT.n += 1
                        wdw.done()
                        if fc < 31:
                            wdw.issue(dwnt.ap()[e, (fc + 1) * 128:(fc + 2) * 128, :])
                    nc.sync.wait_ge(sT.h, sT.n)
                    sDB = p.sem("dbrow")
                    nc.sync.dma_start(out=DBROW, in_=dbw.ap()[e:e + 1, :]
                                      ).then_inc(sDB.h, 16); sDB.n += 16
                    nc.tensor.wait_ge(sDB.h, sDB.n)
                    for half in range(2):
                        bd = psum.acquire()
                        nc.tensor.matmul(psum.bank(bd)[:, 0:512], ONESR[:],
                                         DBROW[:, half * 512:(half + 1) * 512],
                                         start=True, stop=True).then_inc(sT.h, 1)
                        sT.n += 1
                        nc.vector.wait_ge(sT.h, sT.n)
                        cd = nc.vector.tensor_copy(DBB[:, half * 512:(half + 1) * 512],
                                                   psum.bank(bd)[:, 0:512])
                        psum.consumed(bd, cd)
                    p.vmark(sV)
                    nc.vector.wait_ge(sT.h, sT.n)
                    for tt in range(3):
                        for dh in range(2):
                            bb = bdown[tt * 2 + dh]
                            dsl = DVN[:, tt * 1024 + dh * 512: tt * 1024 + (dh + 1) * 512]
                            nc.vector.drain()
                            i = nc.vector.tensor_tensor(dsl, psum.bank(bb)[:, 0:512],
                                                        DBB[:, dh * 512:(dh + 1) * 512],
                                                        op=OP.add)
                            psum.consumed(bb, i)
                            psum.release(bb)
                    for tts in ((0, 1), (2,)):
                        nc.vector.wait_ge(sT.h, sT.n)
                        for j, tt in enumerate(tts):
                            osl = OHT[:, j * T:(j + 1) * T]
                            nc.vector.drain()
                            nc.vector.tensor_scalar(out=osl, in0=POSBC[:],
                                                    scalar1=IOTAR[:, tt:tt + 1],
                                                    scalar2=None, op0=OP.is_equal)
                            nc.vector.drain()
                            nc.vector.tensor_tensor(osl, osl, SIGBC[:], op=OP.mult)
                        p.vmark(sV)
                        nc.tensor.wait_ge(sV.h, sV.n)
                        for dch in range(DCH):
                            for tc in range(4):
                                bb = psum.acquire()
                                for j, tt in enumerate(tts):
                                    mm = nc.tensor.matmul(
                                        psum.bank(bb)[:, 0:512],
                                        DVN[:, tt * 1024 + dch * 128:
                                            tt * 1024 + dch * 128 + 128],
                                        OHT[:, j * T + tc * 512: j * T + tc * 512 + 512],
                                        start=(j == 0), stop=(j == len(tts) - 1))
                                mm.then_inc(sT.h, 1); sT.n += 1
                                nc.vector.wait_ge(sT.h, sT.n)
                                asl = BUFB[:, dch * T + tc * 512:
                                           dch * T + tc * 512 + 512]
                                nc.vector.drain()
                                i = nc.vector.tensor_tensor(asl, asl,
                                                            psum.bank(bb)[:, 0:512],
                                                            op=OP.add)
                                psum.consumed(bb, i)
                        p.vmark(sV)
                    nc.vector.drain().then_inc(sSC.h, 1); sSC.n += 1

        if _EMIT_UPTO >= 8:
            with ExitStack() as ph:
                sEW = p.sem("expw"); sGA = p.sem("gath"); sSC = p.sem("scat")
                for eng in (nc.sync, nc.vector, nc.scalar, nc.tensor):
                    eng.wait_ge(sEW.h, sEW.n)
                    eng.wait_ge(sGA.h, sGA.n)
                    eng.wait_ge(sSC.h, sSC.n)
                XN2S = p.sb(ph, "XN2S", [128, DCH * 512])
                wsh = DblW(p, ph, "WSH", D)
                GS = BUFA[:, 0:16 * 512]
                US = BUFA[:, 16 * 512:32 * 512]
                sSH = p.sem("shw")
                for tc in range(4):
                    nc.sync.wait_ge(sT.h, sT.n)
                    nc.sync.dma_start(
                        out=XN2S[:].rearrange("p (c t) -> p c t", c=DCH),
                        in_=xg_in.ap()[:, tc * 512:(tc + 1) * 512]
                            .rearrange("(c p) t -> p c t", p=128),
                    ).then_inc(sSH.h, 16); sSH.n += 16
                    gujobs = [(wsrc, dst, fn, fc)
                              for (wsrc, dst, fn) in ((shgt, GS, AF.Silu),
                                                      (shut, US, AF.Identity))
                              for fc in range(16)]
                    if tc == 0:
                        wsh.issue(shgt.ap()[:, 0:128]
                                  .rearrange("(c p) j -> p c j", p=128), rearr_c=8)
                    nc.tensor.wait_ge(sSH.h, sSH.n)
                    for gi, (wsrc, dst, fn, fc) in enumerate(gujobs):
                        if True:
                            WSH = wsh.consume()
                            nc.tensor.wait_ge(sV.h, sV.n)
                            bb = psum.acquire()
                            for dch in range(DCH):
                                mm = nc.tensor.matmul(
                                    psum.bank(bb)[:, 0:512],
                                    WSH[:, dch * 128:(dch + 1) * 128],
                                    XN2S[:, dch * 512: dch * 512 + 512],
                                    start=(dch == 0), stop=(dch == DCH - 1))
                            mm.then_inc(sT.h, 1); sT.n += 1
                            wsh.done()
                            if gi < len(gujobs) - 1:
                                w2, _, _, f2 = gujobs[gi + 1]
                                wsh.issue(w2.ap()[:, f2 * 128:(f2 + 1) * 128]
                                          .rearrange("(c p) j -> p c j", p=128),
                                          rearr_c=8)
                            else:
                                wsh.issue(shdt.ap()[0:128, :]
                                          .rearrange("f (c j) -> f c j", c=8),
                                          rearr_c=8)
                            nc.scalar.wait_ge(sT.h, sT.n)
                            i = nc.scalar.activation(out=dst[:, fc * 512:(fc + 1) * 512],
                                                     in_=psum.bank(bb)[:, 0:512],
                                                     func=fn)
                            psum.consumed(bb, i)
                    p.smark(sS)
                    nc.vector.wait_ge(sS.h, sS.n)
                    nc.vector.drain()
                    nc.vector.tensor_tensor(GS, GS, US, op=OP.mult)
                    p.vmark(sV)
                    bsh = [psum.reserve() for _ in range(8)]
                    for fc in range(16):
                        WSH = wsh.consume()
                        nc.tensor.wait_ge(sV.h, sV.n)
                        for jg in range(8):
                            mm = nc.tensor.matmul(
                                psum.bank(bsh[jg])[:, 0:512],
                                WSH[:, jg * 128:(jg + 1) * 128],
                                GS[:, fc * 512:(fc + 1) * 512]
                                if False else BUFA[:, fc * 512:(fc + 1) * 512],
                                start=(fc == 0), stop=(fc == 15))
                        mm.then_inc(sT.h, 1); sT.n += 1
                        wsh.done()
                        if fc < 15:
                            wsh.issue(shdt.ap()[(fc + 1) * 128:(fc + 2) * 128, :]
                                      .rearrange("f (c j) -> f c j", c=8), rearr_c=8)
                        elif tc < 3:
                            wsh.issue(shgt.ap()[:, 0:128]
                                      .rearrange("(c p) j -> p c j", p=128), rearr_c=8)
                    nc.vector.wait_ge(sT.h, sT.n)
                    for jg in range(8):
                        asl = BUFB[:, jg * T + tc * 512: jg * T + tc * 512 + 512]
                        nc.vector.drain()
                        i = nc.vector.tensor_tensor(asl, asl,
                                                    psum.bank(bsh[jg])[:, 0:512],
                                                    op=OP.add)
                        psum.consumed(bsh[jg], i)
                        psum.release(bsh[jg])
                    p.vmark(sV)

        if _EMIT_UPTO >= 9:
            with ExitStack() as ph:
                sSH = p.sem("shw"); sSC = p.sem("scat")
                for eng in (nc.sync, nc.vector, nc.scalar, nc.tensor):
                    eng.wait_ge(sSH.h, sSH.n)
                    eng.wait_ge(sSC.h, sSC.n)
                nc.sync.wait_ge(sT.h, sT.n)
                nc.sync.wait_ge(sV.h, sV.n)
                nc.sync.wait_ge(sS.h, sS.n)
                sXR = p.sem("xrl")
                nc.sync.dma_start(out=BUFA[:], in_=sc_x.ap()
                                  ).then_inc(sXR.h, 16); sXR.n += 16
                nc.vector.wait_ge(sXR.h, sXR.n)
                for dch in range(DCH):
                    nc.vector.drain()
                    asl = BUFA[:, dch * T:(dch + 1) * T]
                    nc.vector.tensor_tensor(asl, asl, BUFB[:, dch * T:(dch + 1) * T],
                                            op=OP.add)
                p.vmark(sV)
                nc.tensor.wait_ge(sV.h, sV.n)
                OT = p.sb(ph, "OT", [128, D])
                for k in range(16):
                    nc.vector.wait_ge(sO.h, sO.n)
                    for dch in range(DCH):
                        bx = psum.acquire()
                        nc.tensor.transpose(
                            psum.bank(bx)[:, 0:128],
                            BUFA[:, dch * T + k * 128: dch * T + k * 128 + 128],
                            IDENT[:]).then_inc(sT.h, 1)
                        sT.n += 1
                        nc.vector.wait_ge(sT.h, sT.n)
                        cp = nc.vector.tensor_copy(OT[:, dch * 128:(dch + 1) * 128],
                                                   psum.bank(bx)[:, 0:128])
                        psum.consumed(bx, cp)
                    p.vmark(sV)
                    nc.sync.wait_ge(sV.h, sV.n)
                    nc.sync.dma_start(out=out_t.ap()[k * 128:(k + 1) * 128, :],
                                      in_=OT[:]).then_inc(sO.h, 16); sO.n += 16

        if _DEBUG_DUMP is not None:
            name = _DEBUG_DUMP[0]
            src = {"BUFA": BUFA, "BUFB": BUFB}[name]
            dbg = p.dout("dbg", [128, DCH * T])
            for s in (sV, sS, sT):
                nc.sync.wait_ge(s.h, s.n)
            nc.sync.dma_start(out=dbg.ap(), in_=src[:]).then_inc(sO.h, 16); sO.n += 16

        if _EMIT_UPTO < 9:
            with ExitStack() as ph:
                OT = p.sb(ph, "OT", [128, D])
                nc.tensor.wait_ge(sV.h, sV.n)
                nc.tensor.wait_ge(sS.h, sS.n)
                for k in range(16):
                    nc.vector.wait_ge(sO.h, sO.n)
                    for dch in range(DCH):
                        b = psum.acquire()
                        nc.tensor.transpose(
                            psum.bank(b)[:, 0:128],
                            BUFA[:, dch * T + k * 128: dch * T + k * 128 + 128],
                            IDENT[:]).then_inc(sT.h, 1)
                        sT.n += 1
                        nc.vector.wait_ge(sT.h, sT.n)
                        cp = nc.vector.tensor_copy(OT[:, dch * 128: dch * 128 + 128],
                                                   psum.bank(b)[:, 0:128])
                        psum.consumed(b, cp)
                    p.vmark(sV)
                    nc.sync.wait_ge(sV.h, sV.n)
                    nc.sync.dma_start(out=out_t.ap()[k * 128:(k + 1) * 128, :],
                                      in_=OT[:]).then_inc(sO.h, 16); sO.n += 16
        nc.sync.wait_ge(sO.h, sO.n)

    return p


def _host_inputs(inputs):
    f = np.float32
    c = np.ascontiguousarray
    w_in = inputs["L0_attn_w_in"]
    wq, wk, wv = w_in[0:D], w_in[D:2 * D], w_in[2 * D:3 * D]
    w0qkvp = np.empty((8, D, 384), f)
    for pr in range(8):
        r = slice(128 * pr, 128 * pr + 128)
        w0qkvp[pr, :, 0:128] = wq[r].T
        w0qkvp[pr, :, 128:256] = wk[r].T
        w0qkvp[pr, :, 256:384] = wv[r].T
    b_in = inputs["L0_attn_b_in"]
    b0qkv = np.zeros((128, 24), f)
    for pr in range(8):
        b0qkv[:, pr] = b_in[128 * pr:128 * pr + 128]
        b0qkv[:, 8 + pr] = b_in[D + 128 * pr:D + 128 * pr + 128]
        b0qkv[:, 16 + pr] = b_in[2 * D + 128 * pr:2 * D + 128 * pr + 128]
    w_out = inputs["L0_attn_w_out"]
    w0outp = np.empty((8, 128, D), f)
    for pr in range(8):
        w0outp[pr] = w_out.T[128 * pr:128 * pr + 128, :]
    b0out = c(inputs["L0_attn_b_out"].reshape(DCH, 128).T)

    com = {
        "ident": np.eye(128, dtype=f),
        "ones_col": np.ones((128, 1), f),
        "ones_row": np.ones((1, 128), f),
        "mask_c": np.concatenate([
            (np.arange(128)[:, None] <= np.arange(256)[None, :]).astype(f),
            (np.arange(128)[:, None] <= (np.arange(256)[None, :] - 128)).astype(f)],
            axis=1),
        "ones2": np.array([[1, 0, 0, 1]], f).repeat(128, 0),
        "sel2t": np.concatenate([np.ones((2, 64), f) * np.array([[1], [0]], f),
                                 np.ones((2, 64), f) * np.array([[0], [1]], f)],
                                axis=1),
        "w0qkvp": w0qkvp, "b0qkv": b0qkv, "w0outp": w0outp, "b0out": b0out,
        "w0r": c(inputs["L0_ffn_W_r"].T), "w0k": c(inputs["L0_ffn_W_k"].T),
        "w0v": c(inputs["L0_ffn_W_v"].T),
        "mur": c(inputs["L0_ffn_mu_r"].reshape(DCH, 128).T),
        "muk": c(inputs["L0_ffn_mu_k"].reshape(DCH, 128).T),
    }
    com["w1qkvf"] = c(inputs["L1_attn_W_qkvf"].T)
    com["w1conv"] = c(inputs["L1_attn_W_conv"].transpose(2, 0, 1))
    wproj_t = inputs["L1_attn_W_proj"].T
    w1projp = np.empty((8, 128, D), f)
    for pr in range(8):
        w1projp[pr] = wproj_t[128 * pr:128 * pr + 128, :]
    com["w1projp"] = w1projp
    com["w1ff1"] = c(inputs["L1_attn_W_ff1"])
    com["w1ff2"] = c(inputs["L1_attn_W_ff2"])
    for nm, key in (("ln1g", "L1_attn_ln1_g"), ("ln1b", "L1_attn_ln1_b"),
                    ("ln2g", "L1_attn_ln2_g"), ("ln2b", "L1_attn_ln2_b")):
        com[nm] = c(inputs[key].reshape(DCH, 128).T)
    selh8 = np.zeros((128, 32), f)
    for fi in range(4):
        for hh in range(2):
            selh8[64 * hh:64 * hh + 64, fi * 8 + hh * 4 + fi] = 1.0
    com["selh8"] = selh8
    sel8d = np.zeros((8, 2), f)
    sel8d[0:4, 0] = 1.0
    sel8d[4:8, 1] = 1.0
    com["sel8d"] = sel8d
    sel2to8 = np.zeros((2, 8), f)
    sel2to8[0, 0:4] = 1.0
    sel2to8[1, 4:8] = 1.0
    com["sel2to8"] = sel2to8
    self8 = np.zeros((8, 4 * 128), f)
    for fi in range(4):
        self8[fi, fi * 128: fi * 128 + 64] = 1.0
        self8[4 + fi, fi * 128 + 64: fi * 128 + 128] = 1.0
    com["self8"] = self8
    com["gwt"] = c(inputs["L1_ffn_gate_w"].T)
    iot = (np.arange(16)[:, None] * 128 + np.arange(128)[None, :]).astype(f)
    com["iotap1"] = iot + 1.0
    com["iotac"] = np.broadcast_to(np.arange(1, 385, dtype=f)[None, :], (128, 384)).copy()
    com["iotar"] = (np.arange(3)[None, :] * 128 + np.arange(128)[:, None] + 1).astype(f)
    com["selpm"] = (np.arange(128)[:, None] % 16 == np.arange(16)[None, :]).astype(f)
    com["selpmt"] = c(com["selpm"].T)
    com["upwt"] = c(inputs["L1_ffn_up_w"].transpose(0, 2, 1))
    com["gwt2"] = c(inputs["L1_ffn_gw"].transpose(0, 2, 1))
    com["dwnt"] = c(inputs["L1_ffn_down_w"].transpose(0, 2, 1))
    com["upbf"] = c(inputs["L1_ffn_up_b"].reshape(E, 32, 128).transpose(0, 2, 1))
    com["gbf"] = c(inputs["L1_ffn_gb"].reshape(E, 32, 128).transpose(0, 2, 1))
    com["dbw"] = c(inputs["L1_ffn_down_b"])
    com["shgt"] = c(inputs["L1_ffn_sh_gate_w"].T)
    com["shut"] = c(inputs["L1_ffn_sh_up_w"].T)
    com["shdt"] = c(inputs["L1_ffn_sh_down_w"].T)
    x = inputs["L0_attn_x"]
    maps = []
    for core in range(NCORE):
        m = dict(com)
        m["x_in"] = c(x[core])
        maps.append(m)
    return maps


def kernel(**inputs):
    maps = _host_inputs(inputs)
    p = build_program()
    res = run_bass_kernel_spmd(p.nc, maps, core_ids=list(range(NCORE)))
    out = np.stack([res.results[i]["out"] for i in range(NCORE)], axis=0)
    return out.astype(np.float32)
